# version 02 (final)
# baseline (speedup 1.0000x reference)
import contextlib
import numpy as np
import ml_dtypes
import concourse.bass as bass
import concourse.mybir as mybir
from concourse.ap import AP
from concourse.bass_utils import run_bass_kernel_spmd

F32 = mybir.dt.float32
BF16 = mybir.dt.bfloat16
AF = mybir.ActivationFunctionType
ALU = mybir.AluOpType
AX = mybir.AxisListType
BF = ml_dtypes.bfloat16

NCORES = 8

ENGINES = ("sy", "gp", "ve", "sc", "te")
_ENG_ATTR = {"sy": "sync", "gp": "gpsimd", "ve": "vector", "sc": "scalar", "te": "tensor"}


class Sched:
    def __init__(self, nc, es):
        self.nc = nc
        self.sem = {e: es.enter_context(nc.semaphore(f"sem_{e}")) for e in ENGINES}
        self.steps = {e: [] for e in ENGINES}
        self.count = {e: 0 for e in ENGINES}

    def step(self, eng, emit, deps=(), dma=False, tag=""):
        inc = 16 if dma else 1
        dd = []
        for d in deps:
            if d is None or d[0] == eng:
                continue
            dd.append(d)
        self.steps[eng].append((emit, tuple(dd), inc, tag))
        self.count[eng] += inc
        return (eng, self.count[eng])

    def tok(self, eng):
        return (eng, self.count[eng])

    def _simulate(self):
        idx = {e: 0 for e in ENGINES}
        done = {e: 0 for e in ENGINES}
        total = sum(len(v) for v in self.steps.values())
        ndone = 0
        while ndone < total:
            progressed = False
            for e in ENGINES:
                while idx[e] < len(self.steps[e]):
                    emit, deps, inc, tag = self.steps[e][idx[e]]
                    if all(done[de] >= dv for de, dv in deps):
                        done[e] += inc
                        idx[e] += 1
                        ndone += 1
                        progressed = True
                    else:
                        break
            if not progressed:
                heads = {e: (idx[e], len(self.steps[e]),
                             self.steps[e][idx[e]][3] if idx[e] < len(self.steps[e]) else None)
                         for e in ENGINES}
                raise RuntimeError(f"sched deadlock: done={done} heads={heads}")

    def finalize(self, block):
        self._simulate()
        final_counts = dict(self.count)
        sems = self.sem

        def mk_body(eng):
            steps = self.steps[eng]
            is_sy = eng == "sy"

            auto_chain = eng in ("ve", "sc", "gp")

            def body(engobj):
                hw = {e: 0 for e in ENGINES}
                cum = 0
                any_dma_before = False
                first = True
                for emit, deps, inc, tag in steps:
                    if not first and (auto_chain or (inc == 16 and any_dma_before)):
                        engobj.wait_ge(sems[eng], cum)
                    for de, dv in deps:
                        if dv > hw[de]:
                            engobj.wait_ge(sems[de], dv)
                            hw[de] = dv
                    inst = emit(engobj)
                    if isinstance(inst, (list, tuple)):
                        inst = inst[-1]
                    inst.then_inc(sems[eng], inc)
                    cum += inc
                    first = False
                    if inc == 16:
                        any_dma_before = True
                if is_sy:
                    for e in ENGINES:
                        if final_counts[e] > 0:
                            engobj.wait_ge(sems[e], final_counts[e])
            return body

        for eng in ENGINES:
            if self.steps[eng] or eng == "sy":
                getattr(block, _ENG_ATTR[eng])(mk_body(eng))


class Prog:
    def __init__(self, name):
        self.name = name
        self.nc = bass.Bass("TRN2", target_bir_lowering=False, debug=False)
        self.es = contextlib.ExitStack()
        self.s = Sched(self.nc, self.es)
        self.inputs = []
        self.outputs = []

    def din(self, name, shape, dtype=F32):
        self.inputs.append(name)
        return self.nc.dram_tensor(name, list(shape), dtype, kind="ExternalInput")

    def dout(self, name, shape, dtype=F32):
        self.outputs.append(name)
        return self.nc.dram_tensor(name, list(shape), dtype, kind="ExternalOutput")

    def dtmp(self, name, shape, dtype=F32):
        return self.nc.dram_tensor(name, list(shape), dtype, kind="Internal")

    def sb(self, name, shape, dtype=F32):
        return self.es.enter_context(self.nc.sbuf_tensor(name, list(shape), dtype))

    def ps(self, name, shape, dtype=F32):
        return self.es.enter_context(self.nc.psum_tensor(name, list(shape), dtype))

    def finish(self):
        block = self.es.enter_context(self.nc.Block())
        self.s.finalize(block)
        self.es.close()
        return self

    def run(self, in_maps):
        res = run_bass_kernel_spmd(self.nc, in_maps, core_ids=list(range(NCORES)))
        return res.results



def emit_consts(p):
    ones_bf = p.sb("ones_bf", [128, 2], BF16)
    ones_f = p.sb("ones_f", [128, 128], F32)
    t1 = p.s.step("ve", lambda v: v.memset(ones_bf[:], 1.0), tag="c1")
    t2 = p.s.step("ve", lambda v: v.memset(ones_f[:], 1.0), tag="c2")
    return ones_bf, ones_f, (t1, t2)


def emit_ln(p, x_dram, T, xn, xn_cols, ones_bf, ones_f, const_toks,
            ps_m, ps_q, ps_bc, pfx="ln"):
    s = p.s
    NT = T // 512
    xg = [p.sb(f"{pfx}_xg0", [128, 8 * 512], F32)]
    cast_bf = p.sb(f"{pfx}_cast", [128, 512], BF16)
    sq_bf = p.sb(f"{pfx}_sq", [128, 512], BF16)
    row_v = p.sb(f"{pfx}_rv", [1, 512], F32)
    row_r = p.sb(f"{pfx}_rr", [1, 512], F32)
    row_ms = p.sb(f"{pfx}_rms", [1, 512], F32)
    xd = x_dram.ap()

    done_toks = []
    grp_free = [None, None]
    row_free = None
    bc_free = None
    mm_last_m = mm_last_q = None
    for t in range(NT):
        g = xg[0]
        ld = []
        for c in range(8):
            dep = [grp_free[0]] if (c == 0 and grp_free[0] is not None) else []
            ld.append(s.step("sy", lambda y, c=c, t=t, g=g: y.dma_start(
                out=g[:, c * 512:(c + 1) * 512], in_=xd[c * 128:(c + 1) * 128, t * 512:(t + 1) * 512]),
                deps=dep, dma=True, tag=f"{pfx}_ld{t}_{c}"))
        for c in range(8):
            tcast = s.step("ve", lambda v, c=c, g=g: v.tensor_copy(out=cast_bf[:], in_=g[:, c * 512:(c + 1) * 512]),
                           deps=[ld[c], mm_last_m] + list(const_toks), tag=f"{pfx}_cast{t}_{c}")
            tsq = s.step("sc", lambda sc, c=c, g=g: sc.activation(out=sq_bf[:], in_=g[:, c * 512:(c + 1) * 512],
                                                                  func=AF.Square),
                         deps=[ld[c], mm_last_q], tag=f"{pfx}_sq{t}_{c}")
            mm_last_m = s.step("te", lambda te, c=c: te.matmul(
                ps_m, ones_bf[:, 0:1], cast_bf[:], start=(c == 0), stop=(c == 7)),
                deps=[tcast, row_free if c == 0 else None], tag=f"{pfx}_mm{t}_{c}")
            mm_last_q = s.step("te", lambda te, c=c: te.matmul(
                ps_q, ones_bf[:, 0:1], sq_bf[:], start=(c == 0), stop=(c == 7)),
                deps=[tsq, row_free if c == 0 else None], tag=f"{pfx}_mq{t}_{c}")
        t1 = s.step("ve", lambda v: v.tensor_scalar_mul(row_ms[:], ps_m, 1.0 / 1024.0),
                    deps=[mm_last_m, row_free], tag=f"{pfx}_st1_{t}")
        t2 = s.step("ve", lambda v: v.tensor_mul(row_v[:], row_ms[:], row_ms[:]),
                    deps=[t1], tag=f"{pfx}_st2_{t}")
        t3 = s.step("ve", lambda v: v.scalar_tensor_tensor(row_v[:], ps_q, 1.0 / 1024.0, row_v[:],
                                                           ALU.mult, ALU.subtract),
                    deps=[t2, mm_last_q], tag=f"{pfx}_st3_{t}")
        t4 = s.step("ve", lambda v: v.tensor_scalar_add(row_v[:], row_v[:], 1e-5),
                    deps=[t3], tag=f"{pfx}_st4_{t}")
        t5 = s.step("sc", lambda sc: sc.activation(out=row_r[:], in_=row_v[:], func=AF.Sqrt),
                    deps=[t4, bc_free], tag=f"{pfx}_st5_{t}")
        t6 = s.step("ve", lambda v: v.reciprocal(row_r[:], row_r[:]),
                    deps=[t5], tag=f"{pfx}_st6_{t}")
        t7 = s.step("ve", lambda v: v.tensor_mul(row_ms[:], row_ms[:], row_r[:]),
                    deps=[t6], tag=f"{pfx}_st7_{t}")
        tb = s.step("te", lambda te: [
            te.matmul(ps_bc[:, 0:512], ones_f[0:1, 0:128], row_r[:], start=True, stop=True),
            te.matmul(ps_bc[:, 512:1024], ones_f[0:1, 0:128], row_ms[:], start=True, stop=True)],
            deps=[t6, t7, bc_free], tag=f"{pfx}_bc_{t}")
        row_free = tb
        toks = []
        for c in range(8):
            ta = s.step("ve", lambda v, c=c, g=g: v.tensor_tensor(
                out=g[:, c * 512:(c + 1) * 512], in0=g[:, c * 512:(c + 1) * 512],
                in1=ps_bc[:, 0:512], op=ALU.mult),
                deps=[tb], tag=f"{pfx}_xa{t}_{c}")
            tbb = s.step("ve", lambda v, c=c, t=t, g=g: v.tensor_tensor(
                out=xn[:, c * xn_cols + t * 512: c * xn_cols + (t + 1) * 512],
                in0=g[:, c * 512:(c + 1) * 512], in1=ps_bc[:, 512:1024], op=ALU.subtract),
                deps=[ta], tag=f"{pfx}_xb{t}_{c}")
            toks.append(tbb)
        bc_free = toks[-1]
        grp_free[0] = toks[-1]
        done_toks.append(toks[-1])
    return done_toks




def _pieces(a, end, bank=512):
    out = []
    off = a
    while off < end:
        n = min(bank - (off % bank), end - off)
        out.append((off, n))
        off += n
    return out

def mm_acc(s, psum_ap, pairs, deps, tag):
    def emit(te):
        insts = []
        n = len(pairs)
        for i, (l, r) in enumerate(pairs):
            insts.append(te.matmul(psum_ap, l, r, start=(i == 0), stop=(i == n - 1)))
        return insts
    return s.step("te", emit, deps=deps, tag=tag)



def build_attn_a(NH=8, stage='all'):
    T = 4096
    p = Prog("attn_a")
    s = p.s
    x_d = p.din("x", [1024, T])
    wqk_d = p.din("wqk", [1024, 1024])
    wv_d = p.din("wv", [1024, 512])
    slopes_d = p.din("slopes_bc", [128, 8])
    nslopes_d = p.din("nslopes", [1, 8])
    posrow_d = p.din("posrow", [1, T])
    poscol_d = p.din("poscol", [128, 32])
    dmask_d = p.din("dmask", [128, 128], BF16)
    ctx_d = p.dout("ctx", [512, T])

    ones_bf, ones_f, ctoks = emit_consts(p)

    P0 = p.ps("P0", [128, 1024], F32)
    P1 = p.ps("P1", [128, 1024], F32)
    P2 = p.ps("P2", [128, 1024], F32)
    P3 = p.ps("P3", [128, 1024], F32)

    xn = p.sb("xn", [128, 8 * T], BF16)
    ln_toks = emit_ln(p, x_d, T, xn, T, ones_bf, ones_f, ctoks,
                      P2[0:1, 0:512], P2[0:1, 512:1024], P3, pfx="ln")
    xn_done = ln_toks[-1]

    wqk = p.sb("wqk_sb", [128, 8 * 1024], BF16)
    wv = p.sb("wv_sb", [128, 8 * 512], BF16)
    wtok = []
    for c in range(8):
        wtok.append(s.step("gp", lambda g, c=c: g.dma_start(out=wqk[:, c * 1024:(c + 1) * 1024],
                                                            in_=wqk_d.ap()[c * 128:(c + 1) * 128, :]),
                           dma=True, tag=f"wqk{c}"))
        wtok.append(s.step("gp", lambda g, c=c: g.dma_start(out=wv[:, c * 512:(c + 1) * 512],
                                                            in_=wv_d.ap()[c * 128:(c + 1) * 128, :]),
                           dma=True, tag=f"wv{c}"))
    slopes = p.sb("slopes_sb2", [128, 8], F32)
    nslopes = p.sb("nslopes_sb2", [1, 8], F32)
    posrow = p.sb("posrow_sb2", [1, T], F32)
    poscol = p.sb("poscol_sb2", [128, 32], F32)
    dmask = p.sb("dmask_sb2", [128, 128], BF16)
    for nm, sb_, dd in (("sl", slopes, slopes_d), ("nsl", nslopes, nslopes_d), ("pr", posrow, posrow_d),
                        ("pc", poscol, poscol_d), ("dm", dmask, dmask_d)):
        wtok.append(s.step("sy", lambda y, sb_=sb_, dd=dd: y.dma_start(out=sb_[:], in_=dd.ap()),
                           dma=True, tag=f"c_{nm}"))

    psS = [P0, P1]
    psC = P2
    psX = P3
    if stage == 'w':
        dbg = p.sb("dbg_f32", [128, 512], F32)
        tdbg = s.step("ve", lambda v: v.tensor_copy(out=dbg[:], in_=wqk[:, 0:512]),
                      deps=[xn_done] + wtok, tag="dbg_cp")
        s.step("sy", lambda y: y.dma_start(out=ctx_d.ap()[0:128, 0:512], in_=dbg[:]),
               deps=[tdbg], dma=True, tag="dbg_st")
        return p.finish()

    qaug = p.sb("qaug", [128, T], BF16)
    kaug = p.sb("kaug", [128, T], BF16)
    qsq = p.sb("qsq", [64, 512], BF16)
    ksq = p.sb("ksq", [64, 512], BF16)
    q2row = p.sb("q2row", [1, T], F32)
    k2row = p.sb("k2row", [1, T], F32)
    maxk = p.sb("maxk", [1, 2], F32)
    vt = p.sb("vt", [128, 32 * 130], BF16)
    biascol = p.sb("biascol", [128, 32], F32)
    pt_sb = p.sb("pt_sb", [128, 1024], BF16)
    ctx_sb = p.sb("ctx_sb", [128, T], BF16)
    rrow = p.sb("rrow", [1, 1024], F32)
    st_f32 = p.sb("st_f32", [128, 512], F32)
    ctmp = p.sb("ctmp", [64, 1024], F32)

    def memset_vt_ones(v):
        i = None
        for j in range(32):
            i = v.memset(vt[:, j * 130 + 64: j * 130 + 65], 1.0)
            i = v.memset(vt[:, j * 130 + 129: j * 130 + 130], 1.0)
        return i
    t_vones = s.step("ve", memset_vt_ones, tag="vt_ones")

    ctx_store_toks = []
    v_done_pair = {}
    prev_head_f4 = []
    for h in range(NH):
        pr = h // 2
        sub = h % 2
        prev_evacs = None
        qk_evac_all = []
        for t in range(8):
            dep = ([xn_done] + wtok) if (t == 0 and h == 0) else []
            if t == 0 and h > 0:
                dep = [s.tok("ve"), s.tok("sc")]
            tm = mm_acc(s, psX[:, 0:512],
                        [(wqk[:, kc * 1024 + h * 128: kc * 1024 + (h + 1) * 128],
                          xn[:, kc * T + t * 512: kc * T + (t + 1) * 512]) for kc in range(8)],
                        deps=dep + (prev_evacs or []), tag=f"qk{h}_{t}")
            e1 = s.step("ve", lambda v, t=t: v.tensor_scalar_mul(
                qaug[0:64, t * 512:(t + 1) * 512], psX[0:64, 0:512], 0.125), deps=[tm], tag=f"qe{h}_{t}")
            e2 = s.step("ve", lambda v, t=t: v.tensor_copy(
                out=kaug[0:64, t * 512:(t + 1) * 512], in_=psX[64:128, 0:512]), deps=[tm], tag=f"ke{h}_{t}")
            if stage in ('qk_nosq',):
                prev_evacs = [e1, e2]
                qk_evac_all += prev_evacs
                continue
            if stage == 'qk_sqonly':
                sq1 = s.step("sc", lambda sc: sc.activation(out=qsq[:], in_=psX[0:64, 0:512],
                                                            func=AF.Square, scale=0.125),
                             deps=[tm, e1, e2], tag=f"qsqA{h}_{t}")
                sq2 = s.step("sc", lambda sc: sc.activation(out=ksq[:], in_=psX[64:128, 0:512],
                                                            func=AF.Square),
                             deps=[tm], tag=f"ksqA{h}_{t}")
                prev_evacs = [e1, e2, sq1, sq2]
                qk_evac_all += prev_evacs
                continue
            if stage == 'qk_m12':
                sq1 = s.step("sc", lambda sc: sc.activation(out=qsq[:], in_=psX[0:64, 0:512],
                                                            func=AF.Square, scale=0.125),
                             deps=[tm], tag=f"qsqB{h}_{t}")
                sq2 = s.step("sc", lambda sc: sc.activation(out=ksq[:], in_=psX[64:128, 0:512],
                                                            func=AF.Square),
                             deps=[tm], tag=f"ksqB{h}_{t}")
                m1 = mm_acc(s, psC[0:1, 0:512], [(ones_bf[0:64, 0:1], qsq[:])],
                            deps=[sq1, s.tok("ve")], tag=f"q2mB{h}_{t}")
                m2 = mm_acc(s, psC[0:1, 512:1024], [(ones_bf[0:64, 0:1], ksq[:])],
                            deps=[sq2], tag=f"k2mB{h}_{t}")
                prev_evacs = [e1, e2, sq1, sq2]
                qk_evac_all += prev_evacs
                continue
            sq1 = s.step("sc", lambda sc: sc.activation(out=qsq[:], in_=psX[0:64, 0:512],
                                                        func=AF.Square, scale=0.125),
                         deps=[tm, e1, e2], tag=f"qsq{h}_{t}")
            sq2 = s.step("sc", lambda sc: sc.activation(out=ksq[:], in_=psX[64:128, 0:512],
                                                        func=AF.Square),
                         deps=[tm], tag=f"ksq{h}_{t}")
            m1 = mm_acc(s, psC[0:1, 0:512], [(ones_bf[0:64, 0:1], qsq[:])],
                        deps=[sq1, s.tok("ve")], tag=f"q2m{h}_{t}")
            m2 = mm_acc(s, psC[0:1, 512:1024], [(ones_bf[0:64, 0:1], ksq[:])],
                        deps=[sq2], tag=f"k2m{h}_{t}")
            e3 = s.step("ve", lambda v, t=t: v.tensor_copy(out=q2row[0:1, t * 512:(t + 1) * 512],
                                                           in_=psC[0:1, 0:512]), deps=[m1], tag=f"q2e{h}_{t}")
            e4 = s.step("ve", lambda v, t=t: v.tensor_copy(out=k2row[0:1, t * 512:(t + 1) * 512],
                                                           in_=psC[0:1, 512:1024]), deps=[m2], tag=f"k2e{h}_{t}")
            prev_evacs = [e1, e2, sq1, sq2, e3, e4]
            qk_evac_all += prev_evacs
        if stage in ('qk_nosq', 'qk_non', 'qk_sqonly', 'qk_m12'):
            break
        n1 = s.step("sc", lambda sc: sc.activation(out=k2row[:], in_=k2row[:], func=AF.Sqrt),
                    deps=qk_evac_all[-6:], tag=f"n1{h}")
        n2 = s.step("ve", lambda v: v.reduce_max(maxk[:, 0:1], k2row[:], axis=AX.X), deps=[n1], tag=f"n2{h}")
        n3 = s.step("ve", lambda v: v.tensor_scalar_mul(maxk[:, 1:2], maxk[:, 0:1], -1.0), deps=[n2], tag=f"n3{h}")
        n4 = s.step("sc", lambda sc: sc.activation(out=q2row[:], in_=q2row[:], func=AF.Sqrt),
                    deps=[n1], tag=f"n4{h}")
        n5 = s.step("ve", lambda v: v.tensor_scalar_mul(q2row[:], q2row[:], maxk[:, 1:2]), deps=[n3, n4], tag=f"n5{h}")
        n6 = s.step("ve", lambda v, h=h: v.scalar_tensor_tensor(
            qaug[64:65, :], posrow[:], nslopes[0:1, h:h + 1], q2row[:], ALU.mult, ALU.add),
            deps=[n5], tag=f"n6{h}")
        n7 = s.step("ve", lambda v: v.memset(kaug[64:65, :], 1.0), deps=[], tag=f"n7{h}")
        n8 = s.step("ve", lambda v, h=h: v.tensor_scalar_mul(biascol[:], poscol[:], slopes[:, h:h + 1]),
                    deps=[], tag=f"n8{h}")
        qk_done = [n6, n7, n8, qk_evac_all[-6], qk_evac_all[-5]]
        if stage == 'qk2':
            tdg = s.step("ve", lambda v: v.tensor_copy(out=st_f32[:], in_=qaug[:, 0:512]),
                         deps=qk_done + [qk_evac_all[-6]], tag="dbgq")
            s.step("sy", lambda y: y.dma_start(out=ctx_d.ap()[0:128, 0:512], in_=st_f32[:]),
                   deps=[tdg], dma=True, tag="dbgq_st")
            break
        if stage == 'qk':
            break

        if sub == 0:
            v_toks = []
            for tt in range(32):
                dep = ([xn_done] + wtok) if (tt == 0 and pr == 0) else []
                tv = mm_acc(s, psX[:, 512:640],
                            [(xn[:, kc * T + tt * 128: kc * T + (tt + 1) * 128],
                              wv[:, kc * 512 + pr * 128: kc * 512 + (pr + 1) * 128]) for kc in range(8)],
                            deps=dep + [v_toks[-1] if v_toks else t_vones], tag=f"v{pr}_{tt}")
                ev = s.step("ve", lambda v, tt=tt: [
                    v.tensor_copy(out=vt[:, tt * 130: tt * 130 + 64], in_=psX[:, 512:576]),
                    v.tensor_copy(out=vt[:, tt * 130 + 65: tt * 130 + 129], in_=psX[:, 576:640])],
                    deps=[tv], tag=f"ve{pr}_{tt}")
                v_toks.append(ev)
            v_done_pair[pr] = v_toks[-1]
        v_done = v_done_pair[pr]
        if stage == 'v2':
            tdg = s.step("ve", lambda v: v.tensor_copy(out=st_f32[:], in_=vt[:, 0:512]),
                         deps=[v_done], tag="dbgv")
            s.step("sy", lambda y: y.dma_start(out=ctx_d.ap()[0:128, 0:512], in_=st_f32[:]),
                   deps=[tdg], dma=True, tag="dbgv_st")
            break
        if stage == 'v':
            break

        head_f4 = []
        for qc in range(4):
            Q0 = qc * 1024
            nkt = (Q0 + 1024) // 128
            exp_toks = []
            pv_toks = []
            for kt in range(nkt):
                K0 = kt * 128
                a = max(0, K0 - Q0)
                buf = psS[kt % 2]
                def s_emit(te, kt=kt, qc=qc, a=a, buf=buf):
                    insts = []
                    for off, n in _pieces(a, 1024):
                        insts.append(te.matmul(buf[:, off:off + n],
                                               kaug[0:65, kt * 128:(kt + 1) * 128],
                                               qaug[0:65, qc * 1024 + off: qc * 1024 + off + n],
                                               start=True, stop=True))
                    return insts
                tS = s.step("te", s_emit,
                            deps=qk_done + [exp_toks[-2] if len(exp_toks) >= 2 else None],
                            tag=f"S{h}_{qc}_{kt}")
                tE = s.step("sc", lambda sc, kt=kt, a=a, buf=buf: sc.activation(
                    out=pt_sb[:, a:1024], in_=buf[:, a:1024], func=AF.Exp,
                    bias=biascol[:, kt:kt + 1]),
                    deps=[tS, pv_toks[-1] if pv_toks else None], tag=f"E{h}_{qc}_{kt}")
                last_tok = tE
                if K0 >= Q0:
                    last_tok = s.step("ve", lambda v, a=a: v.tensor_tensor(
                        out=pt_sb[:, a:a + 128], in0=pt_sb[:, a:a + 128], in1=dmask[:], op=ALU.mult),
                        deps=[tE], tag=f"M{h}_{qc}_{kt}")
                exp_toks.append(last_tok)
                first = (kt == 0)
                last_kt_bank = {0: Q0 // 128 + 3, 1: nkt - 1}
                def pv_emit(te, kt=kt, a=a, sub=sub, first=first, last_kt_bank=last_kt_bank):
                    insts = []
                    for off, n in _pieces(a, 1024):
                        stop_p = (kt == last_kt_bank[off // 512])
                        insts.append(te.matmul(psC[0:65, off:off + n],
                                               vt[:, kt * 130 + sub * 65: kt * 130 + sub * 65 + 65],
                                               pt_sb[:, off:off + n],
                                               start=first, stop=stop_p))
                    return insts
                extra_dep = []
                if first:
                    extra_dep = head_f4[-1:] if head_f4 else (prev_head_f4[-1:] if prev_head_f4 else [])
                    if h == 0 and qc == 0:
                        extra_dep = [qk_evac_all[-2], qk_evac_all[-1]]
                tPV = s.step("te", pv_emit, deps=[last_tok, v_done] + extra_dep,
                             tag=f"PV{h}_{qc}_{kt}")
                pv_toks.append(tPV)
            if stage == 'pv':
                continue
            f1 = s.step("ve", lambda v: v.reciprocal(rrow[:], psC[64:65, :]), deps=[pv_toks[-1]], tag=f"f1{h}_{qc}")
            f2 = s.step("te", lambda te: [
                te.matmul(psX[0:64, 0:512], ones_f[0:1, 0:64], rrow[0:1, 0:512], start=True, stop=True),
                te.matmul(psX[0:64, 512:1024], ones_f[0:1, 0:64], rrow[0:1, 512:1024], start=True, stop=True)],
                deps=[f1, s.tok("ve"), s.tok("sc")], tag=f"f2{h}_{qc}")
            f3 = s.step("ve", lambda v: v.tensor_copy(out=ctmp[:], in_=psC[0:64, :]), deps=[f1], tag=f"f3{h}_{qc}")
            f4 = s.step("ve", lambda v, qc=qc, sub=sub: v.tensor_tensor(
                out=ctx_sb[sub * 64:(sub + 1) * 64, qc * 1024:(qc + 1) * 1024],
                in0=ctmp[:], in1=psX[0:64, 0:1024], op=ALU.mult),
                deps=[f2, f3] + (ctx_store_toks[-1:] if (qc == 0 and sub == 0) else []),
                tag=f"f4{h}_{qc}")
            head_f4.append(f4)
            if stage == 'attn1':
                break
        if stage == 'attn1':
            tdg = s.step("ve", lambda v: v.tensor_copy(out=st_f32[:], in_=ctx_sb[:, 0:512]),
                         deps=[head_f4[-1]], tag="dbga")
            s.step("sy", lambda y: y.dma_start(out=ctx_d.ap()[0:128, 0:512], in_=st_f32[:]),
                   deps=[tdg], dma=True, tag="dbga_st")
            break
        prev_head_f4 = head_f4
        if stage == 'pv':
            break
        if sub == 1:
            dep0 = s.tok("ve")
            for pc in range(8):
                tcp = s.step("ve", lambda v, pc=pc: v.tensor_copy(
                    out=st_f32[:], in_=ctx_sb[:, pc * 512:(pc + 1) * 512]),
                    deps=[dep0, ctx_store_toks[-1] if ctx_store_toks else None], tag=f"cst{pr}_{pc}")
                tst = s.step("sy", lambda y, pr=pr, pc=pc: y.dma_start(
                    out=ctx_d.ap()[pr * 128:(pr + 1) * 128, pc * 512:(pc + 1) * 512], in_=st_f32[:]),
                    deps=[tcp], dma=True, tag=f"ctx_st{pr}_{pc}")
                ctx_store_toks.append(tst)

    if stage in ('qk_nosq', 'qk_non', 'qk', 'v', 'pv', 'qk_sqonly', 'qk_m12'):
        tdg = s.step("ve", lambda v: v.tensor_copy(out=st_f32[:], in_=qaug[:, 0:512]),
                     deps=[s.tok("te"), s.tok("sc")], tag="dbgz")
        s.step("sy", lambda y: y.dma_start(out=ctx_d.ap()[0:128, 0:512], in_=st_f32[:]),
               deps=[tdg], dma=True, tag="dbgz_st")
    return p.finish()



def build_proj_res():
    T = 2048
    p = Prog("proj_res")
    s = p.s
    x_d = p.din("x", [1024, T])
    ctx_d = p.din("ctx", [1024, T])
    w_d = p.din("w", [1024, 1024])
    b_d = p.din("biascol", [128, 8])
    xo_d = p.dout("xo", [1024, T])

    wsb = p.sb("wsb", [128, 8 * 1024], BF16)
    cs = p.sb("cs", [128, 8 * T], BF16)
    bcol = p.sb("bcol", [128, 8], F32)
    xbuf = [p.sb("xb0", [128, 512], F32), p.sb("xb1", [128, 512], F32)]
    obuf = [p.sb("ob0", [128, 512], F32), p.sb("ob1", [128, 512], F32)]
    psum = [p.ps("pA", [128, 512], F32), p.ps("pB", [128, 512], F32)]

    wt = []
    for c in range(8):
        wt.append(s.step("gp", lambda g, c=c: g.dma_start(out=wsb[:, c * 1024:(c + 1) * 1024],
                                                          in_=w_d.ap()[c * 128:(c + 1) * 128, :]),
                         dma=True, tag=f"w{c}"))
        wt.append(s.step("gp", lambda g, c=c: g.dma_start(out=cs[:, c * T:(c + 1) * T],
                                                          in_=ctx_d.ap()[c * 128:(c + 1) * 128, :]),
                         dma=True, tag=f"c{c}"))
    wt.append(s.step("sy", lambda y: y.dma_start(out=bcol[:], in_=b_d.ap()), dma=True, tag="b"))

    i = 0
    prev = {}
    for mc in range(8):
        for tc in range(T // 512):
            k = i % 2
            ps_, xb, ob = psum[k], xbuf[k], obuf[k]
            tlx = s.step("sy", lambda y, mc=mc, tc=tc, xb=xb: y.dma_start(
                out=xb[:], in_=x_d.ap()[mc * 128:(mc + 1) * 128, tc * 512:(tc + 1) * 512]),
                deps=[prev.get(("x", k))], dma=True, tag=f"lx{mc}_{tc}")
            tm = mm_acc(s, ps_[:],
                        [(wsb[:, kc * 1024 + mc * 128: kc * 1024 + (mc + 1) * 128],
                          cs[:, kc * T + tc * 512: kc * T + (tc + 1) * 512]) for kc in range(8)],
                        deps=(wt if i == 0 else []) + [prev.get(("p", k))], tag=f"mm{mc}_{tc}")
            tev = s.step("ve", lambda v, mc=mc, ps_=ps_, xb=xb, ob=ob: v.scalar_tensor_tensor(
                ob[:], ps_[:], bcol[:, mc:mc + 1], xb[:], ALU.add, ALU.add),
                deps=[tm, tlx, prev.get(("o", k))] + (wt if i == 0 else []), tag=f"ev{mc}_{tc}")
            tst = s.step("sy", lambda y, mc=mc, tc=tc, ob=ob: y.dma_start(
                out=xo_d.ap()[mc * 128:(mc + 1) * 128, tc * 512:(tc + 1) * 512], in_=ob[:]),
                deps=[tev], dma=True, tag=f"st{mc}_{tc}")
            prev[("x", k)] = tev
            prev[("p", k)] = tev
            prev[("o", k)] = tst
            i += 1
    return p.finish()



_PROGS = {}
_BUILDERS = {}


def register(name):
    def deco(f):
        _BUILDERS[name] = f
        return f
    return deco


_BUILDERS["attn_a"] = build_attn_a
_BUILDERS["proj_res"] = build_proj_res


def get_prog(name):
    if name not in _PROGS:
        _PROGS[name] = _BUILDERS[name]()
    return _PROGS[name]


def _attn_a_consts():
    T, H = 4096, 16
    slopes = 2.0 ** (-8.0 * np.arange(1, H + 1, dtype=np.float32) / H)
    posrow = np.arange(T, dtype=np.float32).reshape(1, T)
    poscol = (np.arange(32, dtype=np.float32)[None, :] * 128
              + np.arange(128, dtype=np.float32)[:, None]).astype(np.float32)
    dmask = (np.arange(128)[:, None] <= np.arange(128)[None, :]).astype(BF)
    return slopes, posrow, poscol, dmask


def run_attn_a(xT, inp):
    prog = get_prog("attn_a")
    slopes, posrow, poscol, dmask = _attn_a_consts()
    Wq, Wk, Wv = inp["L0_attn_a_W_query"], inp["L0_attn_a_W_key"], inp["L0_attn_a_W_value"]
    in_maps = []
    for core in range(NCORES):
        b, hg = core // 2, core % 2
        hsl = slice(hg * 512, (hg + 1) * 512)
        wqk = np.empty((1024, 1024), np.float32)
        for hh in range(8):
            habs = hg * 8 + hh
            wqk[:, hh * 128: hh * 128 + 64] = Wq[:, habs * 64:(habs + 1) * 64]
            wqk[:, hh * 128 + 64: (hh + 1) * 128] = Wk[:, habs * 64:(habs + 1) * 64]
        sl = slopes[hg * 8:(hg + 1) * 8]
        in_maps.append({
            "x": np.ascontiguousarray(xT[b]),
            "wqk": wqk,
            "wv": np.ascontiguousarray(Wv[:, hsl]),
            "slopes_bc": np.broadcast_to(sl / 8.0, (128, 8)).astype(np.float32).copy(),
            "nslopes": (-sl / 8.0).reshape(1, 8).astype(np.float32),
            "posrow": posrow, "poscol": poscol, "dmask": dmask,
        })
    res = prog.run(in_maps)
    return [np.concatenate([res[2 * b]["ctx"], res[2 * b + 1]["ctx"]], axis=0) for b in range(4)]


def run_proj_res(xT, ctxT, W, bvec):
    prog = get_prog("proj_res")
    bcol = np.ascontiguousarray(bvec.reshape(8, 128).T).astype(np.float32)
    in_maps = []
    for core in range(NCORES):
        b, half = core // 2, core % 2
        tsl = slice(half * 2048, (half + 1) * 2048)
        in_maps.append({
            "x": np.ascontiguousarray(xT[b][:, tsl]),
            "ctx": np.ascontiguousarray(ctxT[b][:, tsl]),
            "w": np.ascontiguousarray(W),
            "biascol": bcol,
        })
    res = prog.run(in_maps)
    return [np.concatenate([res[2 * b]["xo"], res[2 * b + 1]["xo"]], axis=1) for b in range(4)]





def build_moe():
    T = 2048
    NE = 17
    p = Prog("moe")
    s = p.s
    x_d = p.din("x", [1024, T])
    gwT_d = p.din("gwT", [1024, 16])
    gb_d = p.din("gb", [1, 16])
    w13t_d = p.din("w13t", [NE * 1024, 1024])
    w2t_d = p.din("w2t", [NE * 512, 1024])
    idf_d = p.din("ident_f", [128, 128])
    xo_d = p.dout("xo", [1024, T])

    ones_bf, ones_f, ctoks = emit_consts(p)
    P0 = p.ps("P0", [128, 512], F32)
    P1 = p.ps("P1", [128, 512], F32)
    P2 = p.ps("P2", [128, 1024], F32)
    P3 = p.ps("P3", [128, 1024], F32)
    P4 = p.ps("P4", [128, 1024], F32)

    xn = p.sb("xn", [128, 8 * T], BF16)
    ln_toks = emit_ln(p, x_d, T, xn, T, ones_bf, ones_f, ctoks,
                      P2[0:1, 0:512], P2[0:1, 512:1024], P3, pfx="ln")
    xn_done = ln_toks[-1]

    gwT = p.sb("gwT_sb", [128, 8 * 16], BF16)
    gbbc = p.sb("gb_bc", [128, 16], F32)
    idf = p.sb("idf_sb", [128, 128], F32)
    scores = p.sb("scores", [128, 16], F32)
    biased = p.sb("biased", [128, 16], F32)
    maskt = p.sb("maskt", [128, 16], F32)
    tmpt = p.sb("tmpt", [128, 16], F32)
    comb = p.sb("comb", [128, 16], F32)
    mcol = p.sb("mcol", [128, 2], F32)
    combT = p.sb("combT", [16, T], F32)
    combT_d = p.dtmp("combT_d", [16, T], F32)

    tw1 = s.step("gp", lambda g: g.dma_start(
        out=gwT[:], in_=AP(gwT_d, 0, [[16, 128], [128 * 16, 8], [1, 16]])), dma=True, tag="gw")
    tw2 = s.step("sy", lambda y: y.dma_start(out=gbbc[:], in_=AP(gb_d, 0, [[0, 128], [1, 16]])),
                 dma=True, tag="gb")
    tw3 = s.step("sy", lambda y: y.dma_start(out=idf[:], in_=idf_d.ap()), dma=True, tag="idf")

    route_toks = []
    prev_tr = None
    for tt in range(T // 128):
        tg = mm_acc(s, P4[:, 0:16],
                    [(xn[:, kc * T + tt * 128:(kc * T + (tt + 1) * 128)], gwT[:, kc * 16:(kc + 1) * 16])
                     for kc in range(8)],
                    deps=[xn_done, tw1, prev_tr], tag=f"rg{tt}")
        t0 = s.step("sc", lambda sc: sc.activation(out=scores[:], in_=P4[:, 0:16], func=AF.Sigmoid),
                    deps=[tg], tag=f"rs{tt}")
        t1 = s.step("ve", lambda v: v.tensor_add(biased[:], scores[:], gbbc[:]), deps=[t0, tw2], tag=f"rb{tt}")
        t2 = s.step("ve", lambda v: v.memset(comb[:], 0.0), deps=[], tag=f"rc{tt}")
        tprev = [t1, t2]
        for j in range(4):
            m1 = s.step("ve", lambda v: v.reduce_max(mcol[:, 0:1], biased[:], axis=AX.X),
                        deps=tprev, tag=f"rm{tt}_{j}")
            m2 = s.step("ve", lambda v: v.tensor_scalar(maskt[:], biased[:], mcol[:, 0:1], None, ALU.is_ge),
                        deps=[m1], tag=f"rx{tt}_{j}")
            m3 = s.step("ve", lambda v: v.tensor_mul(tmpt[:], maskt[:], scores[:]), deps=[m2], tag=f"rt{tt}_{j}")
            m4 = s.step("ve", lambda v: v.tensor_add(comb[:], comb[:], tmpt[:]), deps=[m3], tag=f"ra{tt}_{j}")
            m5 = s.step("ve", lambda v: v.scalar_tensor_tensor(biased[:], maskt[:], -1e30, biased[:],
                                                               ALU.mult, ALU.add), deps=[m3], tag=f"rz{tt}_{j}")
            tprev = [m4, m5]
        r1 = s.step("ve", lambda v: v.tensor_reduce(mcol[:, 1:2], comb[:], AX.X, ALU.add), deps=tprev, tag=f"rr{tt}")
        r2 = s.step("ve", lambda v: v.reciprocal(mcol[:, 1:2], mcol[:, 1:2]), deps=[r1], tag=f"ri{tt}")
        r3 = s.step("ve", lambda v: v.tensor_scalar_mul(mcol[:, 1:2], mcol[:, 1:2], 2.5), deps=[r2], tag=f"rj{tt}")
        r4 = s.step("ve", lambda v: v.tensor_scalar_mul(comb[:], comb[:], mcol[:, 1:2]), deps=[r3], tag=f"rk{tt}")
        tr = s.step("te", lambda te: te.transpose(P0[0:16, 0:128], comb[:], idf[:]),
                    deps=[r4, tw3, prev_tr], tag=f"rT{tt}")
        cp = s.step("ve", lambda v, tt=tt: v.tensor_copy(out=combT[:, tt * 128:(tt + 1) * 128],
                                                         in_=P0[0:16, 0:128]), deps=[tr], tag=f"rC{tt}")
        prev_tr = cp
        route_toks.append(cp)
    tstc = s.step("sy", lambda y: y.dma_start(out=combT_d.ap(), in_=combT[:]),
                  deps=[route_toks[-1]], dma=True, tag="combT_st")

    w13sb = [p.sb("w13A", [128, 8 * 1024], BF16), p.sb("w13B", [128, 8 * 1024], BF16)]
    w2sb = [p.sb("w2A", [128, 4 * 1024], BF16)]
    cbc = [p.sb("cbcA", [128, T], F32)]
    hbuf = [p.sb("hA", [128, 4 * 512], BF16), p.sb("hB", [128, 4 * 512], BF16)]
    h1t = p.sb("h1t", [128, 512], F32)
    Y = p.sb("Y", [128, 8 * T], F32)
    xr = [p.sb("xrA", [128, 512], F32), p.sb("xrB", [128, 512], F32)]
    orr = [p.sb("orA", [128, 512], F32), p.sb("orB", [128, 512], F32)]

    wfree = {0: None, 1: None}
    cfree = {0: None}
    hk = 0
    hfree = {0: None, 1: None}
    ylast = {}
    pslast = {0: None, 1: None}
    prev_a2 = None
    for e in range(NE):
        k = e % 2
        dep_w = [wfree[k]] if wfree[k] is not None else []
        l1 = s.step("gp", lambda g, e=e, k=k: g.dma_start(
            out=w13sb[k][:], in_=AP(w13t_d, e * 1024 * 1024, [[1024, 128], [128 * 1024, 8], [1, 1024]])),
            deps=dep_w, dma=True, tag=f"lw13_{e}")
        l2 = s.step("gp", lambda g, e=e: g.dma_start(
            out=w2sb[0][:], in_=AP(w2t_d, e * 512 * 1024, [[1024, 128], [128 * 1024, 4], [1, 1024]])),
            deps=[wfree[0], wfree[1]], dma=True, tag=f"lw2_{e}")
        if e < 16:
            l3 = s.step("sy", lambda y, e=e: y.dma_start(
                out=cbc[0][:], in_=AP(combT_d, e * T, [[0, 128], [1, T]])),
                deps=[tstc, cfree[0]], dma=True, tag=f"lcb_{e}")
        else:
            l3 = None
        for tc in range(T // 512):
            hb = hbuf[hk % 2]
            myh = []
            for gc in range(4):
                tg = mm_acc(s, P0[:, 0:512],
                            [(w13sb[k][:, kc * 1024 + gc * 128: kc * 1024 + (gc + 1) * 128],
                              xn[:, kc * T + tc * 512:(kc * T + (tc + 1) * 512)]) for kc in range(8)],
                            deps=[xn_done, l1, s.tok("sc"), s.tok("ve")], tag=f"hg{e}_{tc}_{gc}")
                tu = mm_acc(s, P1[:, 0:512],
                            [(w13sb[k][:, kc * 1024 + 512 + gc * 128: kc * 1024 + 512 + (gc + 1) * 128],
                              xn[:, kc * T + tc * 512:(kc * T + (tc + 1) * 512)]) for kc in range(8)],
                            deps=[xn_done, l1, s.tok("ve")], tag=f"hu{e}_{tc}_{gc}")
                a1 = s.step("sc", lambda sc: sc.activation(out=h1t[:], in_=P0[:, 0:512], func=AF.Silu),
                            deps=[tg, hfree[hk % 2], prev_a2], tag=f"sl{e}_{tc}_{gc}")
                a2 = s.step("ve", lambda v, hb=hb, gc=gc: v.scalar_tensor_tensor(
                    hb[:, gc * 512:(gc + 1) * 512], P1[:, 0:512], 1.0, h1t[:], ALU.mult, ALU.mult),
                    deps=[tu, a1], tag=f"hm{e}_{tc}_{gc}")
                prev_a2 = a2
                if e < 16:
                    a3 = s.step("ve", lambda v, hb=hb, gc=gc, k=k, tc=tc: v.tensor_tensor(
                        out=hb[:, gc * 512:(gc + 1) * 512], in0=hb[:, gc * 512:(gc + 1) * 512],
                        in1=cbc[0][:, tc * 512:(tc + 1) * 512], op=ALU.mult),
                        deps=[a2, l3], tag=f"hc{e}_{tc}_{gc}")
                else:
                    a3 = a2
                myh.append(a3)
            for mc in range(8):
                py = [P2, P3][mc % 2]
                tmy = mm_acc(s, py[:, 0:512],
                             [(w2sb[0][:, kc * 1024 + mc * 128: kc * 1024 + (mc + 1) * 128],
                               hb[:, kc * 512:(kc + 1) * 512]) for kc in range(4)],
                             deps=myh + [l2, pslast[mc % 2]], tag=f"ym{e}_{tc}_{mc}")
                if e == 0:
                    ta = s.step("ve", lambda v, py=py, mc=mc, tc=tc: v.tensor_copy(
                        out=Y[:, mc * T + tc * 512: mc * T + (tc + 1) * 512], in_=py[:, 0:512]),
                        deps=[tmy], tag=f"ya{e}_{tc}_{mc}")
                else:
                    ta = s.step("ve", lambda v, py=py, mc=mc, tc=tc: v.tensor_tensor(
                        out=Y[:, mc * T + tc * 512: mc * T + (tc + 1) * 512],
                        in0=Y[:, mc * T + tc * 512: mc * T + (tc + 1) * 512],
                        in1=py[:, 0:512], op=ALU.add),
                        deps=[tmy], tag=f"ya{e}_{tc}_{mc}")
                ylast[(tc, mc)] = ta
                pslast[mc % 2] = ta
            hfree[hk % 2] = ylast[(tc, 7)]
            hk += 1
        wfree[k] = s.tok("te")
        cfree[0] = s.tok("ve")

    prevx = {0: None, 1: None}
    prevo = {0: None, 1: None}
    i = 0
    for mc in range(8):
        for tc in range(T // 512):
            kk = i % 2
            tlx = s.step("sy", lambda y, mc=mc, tc=tc, kk=kk: y.dma_start(
                out=xr[kk][:], in_=x_d.ap()[mc * 128:(mc + 1) * 128, tc * 512:(tc + 1) * 512]),
                deps=[prevx[kk]], dma=True, tag=f"ox{mc}_{tc}")
            ta = s.step("ve", lambda v, mc=mc, tc=tc, kk=kk: v.tensor_tensor(
                out=orr[kk][:], in0=xr[kk][:], in1=Y[:, mc * T + tc * 512: mc * T + (tc + 1) * 512],
                op=ALU.add),
                deps=[tlx, ylast[(tc, mc)], prevo[kk]], tag=f"oa{mc}_{tc}")
            tst = s.step("sy", lambda y, mc=mc, tc=tc, kk=kk: y.dma_start(
                out=xo_d.ap()[mc * 128:(mc + 1) * 128, tc * 512:(tc + 1) * 512], in_=orr[kk][:]),
                deps=[ta], dma=True, tag=f"os{mc}_{tc}")
            prevx[kk] = ta
            prevo[kk] = tst
            i += 1
    return p.finish()


_BUILDERS["moe"] = build_moe


def run_moe(xT, inp):
    prog = get_prog("moe")
    w13 = inp["L0_ffn_w13"]
    w2 = inp["L0_ffn_w2"]
    w13t = np.ascontiguousarray(np.concatenate(
        [w13.transpose(0, 2, 1).reshape(16 * 1024, 1024), inp["L0_ffn_w13_s"].T], axis=0))
    w2t = np.ascontiguousarray(np.concatenate(
        [w2.transpose(0, 2, 1).reshape(16 * 512, 1024), inp["L0_ffn_w2_s"].T], axis=0))
    gwT = np.ascontiguousarray(inp["L0_ffn_gate_w"].T)
    gb = inp["L0_ffn_gate_b"].reshape(1, 16).astype(np.float32)
    idf = np.eye(128, dtype=np.float32)
    in_maps = []
    for core in range(NCORES):
        b, half = core // 2, core % 2
        tsl = slice(half * 2048, (half + 1) * 2048)
        in_maps.append({
            "x": np.ascontiguousarray(xT[b][:, tsl]),
            "gwT": gwT, "gb": gb, "w13t": w13t, "w2t": w2t, "ident_f": idf,
        })
    res = prog.run(in_maps)
    return [np.concatenate([res[2 * b]["xo"], res[2 * b + 1]["xo"]], axis=1) for b in range(4)]



def build_attn_b():
    T = 2048
    p = Prog("attn_b")
    s = p.s
    x_d = p.din("x", [1024, T])
    wq_d = p.din("wq", [1024, 1024])
    wk_d = p.din("wk", [1024, 1024])
    wv_d = p.din("wv", [1024, 1024])
    m0_d = p.din("m0", [64, 8 * 64])
    z0_d = p.din("z0c", [64, 8])
    betas_d = p.din("betas_c", [64, 8])
    dmask_d = p.din("dmask", [128, 128], BF16)
    idb_d = p.din("ident_b", [128, 128], BF16)
    idf_d = p.din("ident_f", [128, 128])
    ctx_d = p.dout("ctx", [T, 1024])

    ones_bf, ones_f, ctoks = emit_consts(p)
    A = p.ps("A", [128, 512], F32)
    B = p.ps("B", [128, 512], F32)
    C = p.ps("C", [128, 512], F32)
    D = p.ps("D", [128, 512], F32)
    E = p.ps("E", [128, 1024], F32)
    F_ = p.ps("F", [128, 512], F32)
    G = p.ps("G", [64, 512], BF16)

    xn = p.sb("xn", [128, 8 * T], BF16)
    ln_toks = emit_ln(p, x_d, T, xn, T, ones_bf, ones_f, ctoks,
                      A[0:1, 0:512], B[0:1, 0:512], E, pfx="ln")
    xn_done = ln_toks[-1]

    wq = p.sb("wq_sb", [128, 8 * 1024], BF16)
    wk = p.sb("wk_sb", [128, 8 * 1024], BF16)
    wv = p.sb("wv_sb", [128, 8 * 1024], BF16)
    wtok = []
    for nm, sb_, dd in (("q", wq, wq_d), ("k", wk, wk_d), ("v", wv, wv_d)):
        wtok.append(s.step("gp", lambda g, sb_=sb_, dd=dd: g.dma_start(
            out=sb_[:], in_=AP(dd, 0, [[1024, 128], [128 * 1024, 8], [1, 1024]])),
            dma=True, tag=f"w{nm}"))
    m0 = p.sb("m0_sb", [64, 8 * 64], F32)
    z0c = p.sb("z0_sb", [64, 8], F32)
    betas = p.sb("betas_sb", [64, 8], F32)
    gcol = p.sb("gcol", [64, 8], F32)
    omg = p.sb("omg", [64, 8], F32)
    dmask = p.sb("dmask_sb", [128, 128], BF16)
    idb = p.sb("idb_sb", [128, 128], BF16)
    idf = p.sb("idf_sb", [128, 128], F32)
    for nm, sb_, dd in (("m0", m0, m0_d), ("z0", z0c, z0_d), ("be", betas, betas_d), ("dm", dmask, dmask_d),
                        ("ib", idb, idb_d), ("if", idf, idf_d)):
        wtok.append(s.step("sy", lambda y, sb_=sb_, dd=dd: y.dma_start(out=sb_[:], in_=dd.ap()),
                           dma=True, tag=f"c{nm}"))
    g1 = s.step("sc", lambda sc: sc.activation(out=gcol[:], in_=betas[:], func=AF.Sigmoid),
                deps=wtok, tag="gsig")
    g2 = s.step("ve", lambda v: v.scalar_tensor_tensor(omg[:], gcol[:], -1.0, ones_f[0:64, 0:8],
                                                       ALU.mult, ALU.add), deps=[g1] + list(ctoks), tag="gomg")

    M_f = p.sb("M_f", [64, 8 * 65], F32)
    M_b = p.sb("M_b", [64, 8 * 65], BF16)
    minit = {}
    for hh in range(8):
        t1 = s.step("ve", lambda v, hh=hh: v.tensor_copy(out=M_f[:, hh * 65: hh * 65 + 64],
                                                         in_=m0[:, hh * 64:(hh + 1) * 64]),
                    deps=wtok, tag=f"mi{hh}")
        t2 = s.step("ve", lambda v, hh=hh: v.tensor_copy(out=M_f[:, hh * 65 + 64: hh * 65 + 65],
                                                         in_=z0c[:, hh:hh + 1]),
                    deps=wtok, tag=f"zi{hh}")
        t3 = s.step("ve", lambda v, hh=hh: v.tensor_copy(out=M_b[:, hh * 65:(hh + 1) * 65],
                                                         in_=M_f[:, hh * 65:(hh + 1) * 65]),
                    deps=[t1, t2], tag=f"mb{hh}")
        minit[hh] = t3

    q_sc = p.dtmp("q_sc", [T, 1024], BF16)
    k_sc = p.dtmp("k_sc", [T, 1024], BF16)
    v_sc = p.dtmp("v_sc", [T, 1024], BF16)
    ev_bf = p.sb("ev_bf", [128, 1024], BF16)
    prev_ev = None
    prev_st = None
    for tt in range(T // 128):
        for wsb, dst in ((wq, q_sc), (wk, k_sc), (wv, v_sc)):
            for oc in range(2):
                ps_ = [A, B][oc]
                tm = mm_acc(s, ps_[:, 0:512],
                            [(xn[:, kc * T + tt * 128:(kc * T + (tt + 1) * 128)],
                              wsb[:, kc * 1024 + oc * 512: kc * 1024 + (oc + 1) * 512]) for kc in range(8)],
                            deps=[xn_done, prev_ev] + (wtok if tt == 0 else []), tag=f"pj{tt}_{oc}")
                prev_ev = s.step("ve", lambda v, oc=oc, ps_=ps_: v.tensor_copy(
                    out=ev_bf[:, oc * 512:(oc + 1) * 512], in_=ps_[:, 0:512]),
                    deps=[tm, prev_st], tag=f"pe{tt}_{oc}")
            tst = s.step("sy", lambda y, tt=tt, dst=dst: y.dma_start(
                out=dst.ap()[tt * 128:(tt + 1) * 128, :], in_=ev_bf[:]),
                deps=[prev_ev], dma=True, tag=f"pst{tt}")
            prev_st = tst
    proj_done = prev_st

    qf = p.sb("qf", [64, 512], BF16)
    qt4 = p.sb("qt4", [128, 4 * 64], BF16)
    kf = p.sb("kf", [128, 512], BF16)
    qaug = p.sb("qaug2", [128, 512], BF16)
    vt4 = p.sb("vt4", [128, 4 * 65], BF16)
    kt4 = p.sb("kt4", [128, 4 * 64], BF16)
    sk4 = p.sb("sk4", [128, 4 * 64], BF16)
    sqf = p.sb("sqf", [64, 512], BF16)
    elu1 = p.sb("elu1", [128, 512], F32)
    elu2 = p.sb("elu2", [128, 512], F32)
    q2s = p.sb("q2s", [1, 512], F32)
    k2s = p.sb("k2s", [1, 512], F32)
    qsq2 = p.sb("qsq2", [64, 512], BF16)
    maxk2 = p.sb("maxk2", [1, 2], F32)
    pt2 = p.sb("pt2", [128, 4 * 512], BF16)
    dotn = p.sb("dotn", [64, 512], F32)
    memn = p.sb("memn", [64, 512], F32)
    att_f = p.sb("att_f", [64, 512], F32)
    rrow2 = p.sb("rrow2", [1, 1024], F32)
    stg = p.sb("stg", [128, 4 * 64], F32)

    vt_ones = s.step("ve", lambda v: [v.memset(vt4[:, j * 65 + 64: j * 65 + 65], 1.0) for j in range(4)],
                     tag="vt4_ones")
    kf_ones = s.step("ve", lambda v: v.memset(kf[64:65, :], 1.0), tag="kf_ones")

    last_store = None
    last_exp = None
    last_pv = None
    for hh in range(8):
        m_prev = minit[hh]
        for sg in range(8):
            base = (sg * 256 + hh * 32) * 1024
            vetok = s.tok("ve")
            sctok = s.tok("sc")
            tetok = s.tok("te")
            lq = s.step("sy", lambda y, base=base: y.dma_start(
                out=qt4[:].rearrange("p (j c) -> p j c", j=4),
                in_=AP(q_sc, base, [[64, 128], [128 * 64, 4], [1, 64]])),
                deps=[proj_done, tetok], dma=True, tag=f"lq{hh}_{sg}")
            lv = s.step("gp", lambda g, base=base: g.dma_start(
                out=vt4[:].rearrange("p (j c) -> p j c", j=4)[:, :, 0:64],
                in_=AP(v_sc, base, [[64, 128], [128 * 64, 4], [1, 64]])),
                deps=[proj_done, vetok, vt_ones, tetok], dma=True, tag=f"lv{hh}_{sg}")
            lkt = s.step("gp", lambda g, base=base: g.dma_start(
                out=kt4[:].rearrange("p (j c) -> p j c", j=4),
                in_=AP(k_sc, base, [[64, 128], [128 * 64, 4], [1, 64]])),
                deps=[proj_done, vetok, sctok, tetok], dma=True, tag=f"lkt{hh}_{sg}")
            tq = s.step("te", lambda te: [te.transpose(G[:, j * 128:(j + 1) * 128],
                                                       qt4[:, j * 64:(j + 1) * 64], idb[:])
                                          for j in range(4)],
                        deps=[lq, vetok] + wtok, tag=f"tq{hh}_{sg}")
            eq = s.step("ve", lambda v: v.tensor_copy(out=qf[:], in_=G[0:64, :]), deps=[tq], tag=f"eq{hh}_{sg}")
            tk = s.step("te", lambda te: [te.transpose(G[:, j * 128:(j + 1) * 128],
                                                       kt4[:, j * 64:(j + 1) * 64], idb[:])
                                          for j in range(4)],
                        deps=[lkt, eq], tag=f"tk{hh}_{sg}")
            ek = s.step("ve", lambda v: v.tensor_copy(out=kf[0:64, :], in_=G[0:64, :]), deps=[tk], tag=f"ek{hh}_{sg}")
            lq = eq
            lk = ek
            n1 = s.step("sc", lambda sc: sc.activation(out=qsq2[:], in_=qf[:], func=AF.Square, scale=0.125),
                        deps=[lq], tag=f"nq{hh}_{sg}")
            n2 = mm_acc(s, A[0:1, 0:512], [(ones_bf[0:64, 0:1], qsq2[:])],
                        deps=[n1, s.tok("ve"), s.tok("sc")], tag=f"nqm{hh}_{sg}")
            n3 = s.step("sc", lambda sc: sc.activation(out=qsq2[:], in_=kf[0:64, :], func=AF.Square),
                        deps=[lk, n2], tag=f"nk{hh}_{sg}")
            n4 = mm_acc(s, B[0:1, 0:512], [(ones_bf[0:64, 0:1], qsq2[:])],
                        deps=[n3], tag=f"nkm{hh}_{sg}")
            n5 = s.step("ve", lambda v: v.tensor_copy(out=q2s[:], in_=A[0:1, 0:512]), deps=[n2], tag=f"nqe{hh}_{sg}")
            n6 = s.step("ve", lambda v: v.tensor_copy(out=k2s[:], in_=B[0:1, 0:512]), deps=[n4], tag=f"nke{hh}_{sg}")
            n7 = s.step("sc", lambda sc: [sc.activation(out=q2s[:], in_=q2s[:], func=AF.Sqrt),
                                          sc.activation(out=k2s[:], in_=k2s[:], func=AF.Sqrt)],
                        deps=[n5, n6], tag=f"nsq{hh}_{sg}")
            n8 = s.step("ve", lambda v: v.reduce_max(maxk2[:, 0:1], k2s[:], axis=AX.X), deps=[n7], tag=f"nmx{hh}_{sg}")
            n9 = s.step("ve", lambda v: v.tensor_scalar_mul(maxk2[:, 1:2], maxk2[:, 0:1], -1.0), deps=[n8], tag=f"nng{hh}_{sg}")
            n10 = s.step("ve", lambda v: v.tensor_scalar_mul(qaug[64:65, :], q2s[:], maxk2[:, 1:2]),
                         deps=[n9], tag=f"nc{hh}_{sg}")
            n11 = s.step("ve", lambda v: v.tensor_scalar_mul(qaug[0:64, :], qf[:], 0.125),
                         deps=[lq], tag=f"nqa{hh}_{sg}")
            qk_ready = [n10, n11, lk, kf_ones]
            e1 = s.step("ve", lambda v: v.tensor_scalar_min(elu1[0:64, :], qf[:], 0.0), deps=[lq], tag=f"e1{hh}_{sg}")
            e2 = s.step("sc", lambda sc: sc.activation(out=elu1[0:64, :], in_=elu1[0:64, :], func=AF.Exp),
                        deps=[e1], tag=f"e2{hh}_{sg}")
            e3 = s.step("ve", lambda v: v.tensor_scalar_max(elu2[0:64, :], qf[:], 0.0), deps=[lq], tag=f"e3{hh}_{sg}")
            e4 = s.step("ve", lambda v: v.tensor_add(sqf[:], elu1[0:64, :], elu2[0:64, :]), deps=[e2, e3], tag=f"e4{hh}_{sg}")
            f1 = s.step("ve", lambda v: v.tensor_scalar_min(elu1[:, 0:256], kt4[:], 0.0), deps=[lkt], tag=f"f1{hh}_{sg}")
            f2 = s.step("sc", lambda sc: sc.activation(out=elu1[:, 0:256], in_=elu1[:, 0:256], func=AF.Exp),
                        deps=[f1], tag=f"f2{hh}_{sg}")
            f3 = s.step("ve", lambda v: v.tensor_scalar_max(elu2[:, 0:256], kt4[:], 0.0), deps=[lkt], tag=f"f3{hh}_{sg}")
            f4 = s.step("ve", lambda v: v.tensor_add(sk4[:], elu1[:, 0:256], elu2[:, 0:256]), deps=[f2, f3], tag=f"f4{hh}_{sg}")
            pv_toks = []
            for kt in range(4):
                buf = [A, B][kt % 2]
                a = kt * 128
                tS = s.step("te", lambda te, kt=kt, a=a, buf=buf: te.matmul(
                    buf[:, a:512], kf[0:65, kt * 128:(kt + 1) * 128], qaug[0:65, a:512],
                    start=True, stop=True),
                    deps=qk_ready + [last_exp if kt >= 2 else None, s.tok("ve") if kt < 2 else None],
                    tag=f"S2{hh}_{sg}_{kt}")
                tE = s.step("sc", lambda sc, kt=kt, a=a, buf=buf: sc.activation(
                    out=pt2[:, kt * 512 + a: (kt + 1) * 512], in_=buf[:, a:512], func=AF.Exp),
                    deps=[tS, pv_toks[-1] if pv_toks else None], tag=f"E2{hh}_{sg}_{kt}")
                last_exp = tE
                tM = s.step("ve", lambda v, kt=kt, a=a: v.tensor_tensor(
                    out=pt2[:, kt * 512 + a: kt * 512 + a + 128],
                    in0=pt2[:, kt * 512 + a: kt * 512 + a + 128], in1=dmask[:], op=ALU.mult),
                    deps=[tE], tag=f"M2{hh}_{sg}_{kt}")
                tPV = s.step("te", lambda te, kt=kt, a=a: te.matmul(
                    C[0:65, a:512], vt4[:, kt * 65:(kt + 1) * 65], pt2[:, kt * 512 + a:(kt + 1) * 512],
                    start=(kt == 0), stop=(kt == 3)),
                    deps=[tM, lv, last_pv], tag=f"PV2{hh}_{sg}_{kt}")
                last_pv = None
                pv_toks.append(tPV)
            tMM = s.step("te", lambda te, hh=hh: te.matmul(
                D[0:65, 0:512], M_b[0:64, hh * 65:(hh + 1) * 65], sqf[:], start=True, stop=True),
                deps=[e4, m_prev, s.tok("ve")], tag=f"MM{hh}_{sg}")
            d1 = s.step("ve", lambda v: v.reciprocal(rrow2[0:1, 0:512], C[64:65, 0:512]),
                        deps=[pv_toks[-1]], tag=f"d1{hh}_{sg}")
            d2 = s.step("ve", lambda v: v.reciprocal(rrow2[0:1, 512:1024], D[64:65, 0:512]),
                        deps=[tMM, d1], tag=f"d2{hh}_{sg}")
            d3 = s.step("te", lambda te: [
                te.matmul(E[0:64, 0:512], ones_f[0:1, 0:64], rrow2[0:1, 0:512], start=True, stop=True),
                te.matmul(E[0:64, 512:1024], ones_f[0:1, 0:64], rrow2[0:1, 512:1024], start=True, stop=True)],
                deps=[d2, s.tok("ve")], tag=f"d3{hh}_{sg}")
            d4 = s.step("ve", lambda v: v.tensor_copy(out=dotn[:], in_=C[0:64, 0:512]), deps=[d1], tag=f"d4{hh}_{sg}")
            d5 = s.step("ve", lambda v: v.tensor_copy(out=memn[:], in_=D[0:64, 0:512]), deps=[d2], tag=f"d5{hh}_{sg}")
            d6 = s.step("ve", lambda v: v.tensor_tensor(out=dotn[:], in0=dotn[:], in1=E[0:64, 0:512],
                                                        op=ALU.mult), deps=[d3, d4], tag=f"d6{hh}_{sg}")
            d7 = s.step("ve", lambda v: v.tensor_tensor(out=memn[:], in0=memn[:], in1=E[0:64, 512:1024],
                                                        op=ALU.mult), deps=[d3, d5], tag=f"d7{hh}_{sg}")
            d8 = s.step("ve", lambda v, hh=hh: v.tensor_scalar_mul(dotn[:], dotn[:], omg[:, hh:hh + 1]),
                        deps=[d6, g2], tag=f"d8{hh}_{sg}")
            d9 = s.step("ve", lambda v, hh=hh: v.scalar_tensor_tensor(att_f[:], memn[:], gcol[:, hh:hh + 1],
                                                                      dotn[:], ALU.mult, ALU.add),
                        deps=[d7, d8], tag=f"d9{hh}_{sg}")
            tMU = s.step("te", lambda te: [te.matmul(F_[0:64, 0:65], sk4[:, kt * 64:(kt + 1) * 64],
                                                     vt4[:, kt * 65:(kt + 1) * 65],
                                                     start=(kt == 0), stop=(kt == 3)) for kt in range(4)],
                         deps=[f4, lv, d9, last_store], tag=f"MU{hh}_{sg}")
            m1_ = s.step("ve", lambda v, hh=hh: v.tensor_tensor(
                out=M_f[:, hh * 65:(hh + 1) * 65], in0=M_f[:, hh * 65:(hh + 1) * 65],
                in1=F_[0:64, 0:65], op=ALU.add), deps=[tMU, tMM], tag=f"mu{hh}_{sg}")
            m2_ = s.step("ve", lambda v, hh=hh: v.tensor_copy(
                out=M_b[:, hh * 65:(hh + 1) * 65], in_=M_f[:, hh * 65:(hh + 1) * 65]),
                deps=[m1_], tag=f"mc{hh}_{sg}")
            m_prev = m2_
            trs = []
            for j in range(4):
                tt_ = s.step("te", lambda te, j=j: te.transpose(
                    F_[:, j * 64:(j + 1) * 64], att_f[:, j * 128:(j + 1) * 128], idf[0:64, 0:64]),
                    deps=[d9, tMU, m1_], tag=f"tr{hh}_{sg}_{j}")
                trs.append(tt_)
            ev = s.step("ve", lambda v: v.tensor_copy(out=stg[:], in_=F_[:, 0:256]),
                        deps=trs + [last_store], tag=f"oe{hh}_{sg}")
            tst = s.step("sy", lambda y, base=base: y.dma_start(
                out=AP(ctx_d, base, [[64, 128], [8192, 4], [1, 64]]),
                in_=stg[:].rearrange("p (j c) -> p j c", j=4)),
                deps=[ev], dma=True, tag=f"ost{hh}_{sg}")
            last_store = tst
    return p.finish()


_BUILDERS["attn_b"] = build_attn_b


def run_attn_b(xT, inp):
    prog = get_prog("attn_b")
    dmask = (np.arange(128)[:, None] <= np.arange(128)[None, :]).astype(BF)
    mem0 = inp["L0_attn_b_memory0"][0]
    z0 = inp["L0_attn_b_z0"][0, :, 0, :]
    betas = inp["L0_attn_b_betas"][0, :, 0, :]
    in_maps = []
    for core in range(NCORES):
        b, hg = core // 2, core % 2
        hs = slice(hg * 8, (hg + 1) * 8)
        rows = np.concatenate([np.arange(sg * 512 + hg * 256, sg * 512 + hg * 256 + 256)
                               for sg in range(8)])
        xslab = np.ascontiguousarray(inp_x_slab(xT[b], rows))
        m0c = np.ascontiguousarray(mem0[hs].transpose(1, 0, 2).reshape(64, 8 * 64))
        in_maps.append({
            "x": xslab,
            "wq": inp["L0_attn_b_Wq"], "wk": inp["L0_attn_b_Wk"], "wv": inp["L0_attn_b_Wv"],
            "m0": m0c,
            "z0c": np.ascontiguousarray(z0[hs].T),
            "betas_c": np.ascontiguousarray(betas[hs].T),
            "dmask": dmask,
            "ident_b": np.eye(128, dtype=np.float32).astype(BF),
            "ident_f": np.eye(128, dtype=np.float32),
        })
    res = prog.run(in_maps)
    ctxT = []
    for b in range(4):
        full = np.empty((4096, 1024), np.float32)
        for hg in range(2):
            rows = np.concatenate([np.arange(sg * 512 + hg * 256, sg * 512 + hg * 256 + 256)
                                   for sg in range(8)])
            full[rows] = res[2 * b + hg]["ctx"]
        ctxT.append(np.ascontiguousarray(full.T))
    return ctxT


def inp_x_slab(xTb, rows):
    return xTb[:, rows]




def mm_acc2(s, psum_ap, pairs, deps, tag, start, stop):
    def emit(te):
        insts = []
        n = len(pairs)
        for i, (l, r) in enumerate(pairs):
            insts.append(te.matmul(psum_ap, l, r, start=(start and i == 0), stop=(stop and i == n - 1)))
        return insts
    return s.step("te", emit, deps=deps, tag=tag)


def build_m1_token():
    TE_ = 2560
    T = 2048
    p = Prog("m1_token")
    s = p.s
    x_d = p.din("x", [1024, TE_])
    inwT_d = p.din("inwT", [1024, 2048])
    inb_d = p.din("inb", [128, 16])
    convw_d = p.din("convw", [128, 16 * 3])
    convb_d = p.din("convb", [128, 16])
    hv_d = p.din("hv", [128, 2])
    xpjT_d = p.din("xpjT", [2048, 80])
    xpjb_d = p.din("xpjb", [80, 1])
    dtwT_d = p.din("dtwT", [64, 2048])
    dtb_d = p.din("dtb", [128, 16])
    dpar_d = p.din("dpar", [128, 16])
    delta_d = p.dout("delta", [2048, T])
    du_d = p.dout("du", [2048, T])
    uD_d = p.dout("uD", [2048, T])
    bm_d = p.dout("bm", [16, T])

    ones_bf, ones_f, ctoks = emit_consts(p)
    A = p.ps("A", [128, 512], F32)
    B = p.ps("B", [128, 512], F32)
    E = p.ps("E", [128, 1024], F32)
    X4 = [p.ps(f"X{i}", [80, 512], F32) for i in range(4)]

    xn = p.sb("xn", [128, 8 * TE_], BF16)
    ln_toks = emit_ln(p, x_d, TE_, xn, TE_, ones_bf, ones_f, ctoks,
                      A[0:1, 0:512], B[0:1, 0:512], E, pfx="ln")
    xn_done = ln_toks[-1]

    inwT = p.sb("inwT_sb", [128, 8 * 2048], BF16)
    xpjT = p.sb("xpjT_sb", [128, 16 * 80], BF16)
    dtwT = p.sb("dtwT_sb", [64, 2048], BF16)
    smalls = {}
    wtok = []
    wtok.append(s.step("gp", lambda g: g.dma_start(
        out=inwT[:], in_=AP(inwT_d, 0, [[2048, 128], [128 * 2048, 8], [1, 2048]])), dma=True, tag="winw"))
    wtok.append(s.step("gp", lambda g: g.dma_start(
        out=xpjT[:], in_=AP(xpjT_d, 0, [[80, 128], [128 * 80, 16], [1, 80]])), dma=True, tag="wxpj"))
    wtok.append(s.step("gp", lambda g: g.dma_start(out=dtwT[:], in_=dtwT_d.ap()), dma=True, tag="wdtw"))
    for nm, dd, shp in (("inb", inb_d, [128, 16]), ("convw", convw_d, [128, 48]),
                        ("convb", convb_d, [128, 16]), ("xpjb", xpjb_d, [80, 1]),
                        ("dtb", dtb_d, [128, 16]), ("dpar", dpar_d, [128, 16]),
                        ("hv", hv_d, [128, 2])):
        sb_ = p.sb(nm + "_sb", shp, F32)
        smalls[nm] = sb_
        wtok.append(s.step("sy", lambda y, sb_=sb_, dd=dd: y.dma_start(out=sb_[:], in_=dd.ap()),
                           dma=True, tag=f"c{nm}"))

    u_sc = p.dtmp("u_sc", [2048, T], F32)
    dlr = p.sb("dlr", [64, T], BF16)
    h_sb = p.sb("h_sb", [128, 2052], F32)
    u_f = p.sb("u_f", [128, T], F32)
    u_b = p.sb("u_b", [128, T], BF16)
    uD_sb = p.sb("uD_sb", [128, T], F32)
    t1sb = p.sb("t1sb", [128, 512], F32)
    t2sb = p.sb("t2sb", [128, 512], F32)
    t3sb = p.sb("t3sb", [128, 512], F32)
    xdbl = p.sb("xdbl", [80, T], F32)
    ut = [p.sb("utA", [128, 512], F32), p.sb("utB", [128, 512], F32)]

    prev_ev = None
    prev_u = None
    xdbl_last = [None] * 4
    for mc in range(16):
        ev_toks = []
        for tc in range(5):
            c0 = tc * 512
            n = min(512, 2052 - c0)
            tm = mm_acc(s, A[0:128, 0:n],
                        [(inwT[:, kc * 2048 + mc * 128: kc * 2048 + (mc + 1) * 128],
                          xn[:, kc * TE_ + c0: kc * TE_ + c0 + n]) for kc in range(8)],
                        deps=[xn_done, prev_ev] + (wtok if mc == 0 and tc == 0 else []),
                        tag=f"h{mc}_{tc}")
            pe = s.step("ve", lambda v, c0=c0, n=n, mc=mc: v.tensor_scalar(
                h_sb[:, c0:c0 + n], A[0:128, 0:n], smalls["inb"][:, mc:mc + 1], None, ALU.add),
                deps=[tm, prev_u], tag=f"he{mc}_{tc}")
            prev_ev = pe
            ev_toks.append(pe)
        hv1 = s.step("ve", lambda v: v.tensor_scalar(h_sb[:, 0:1], h_sb[:, 0:1],
                                                     smalls["hv"][:, 0:1], None, ALU.mult),
                     deps=ev_toks, tag=f"hv1_{mc}")
        hv2 = s.step("ve", lambda v: v.tensor_scalar(h_sb[:, 2049:2050], h_sb[:, 2049:2050],
                                                     smalls["hv"][:, 1:2], None, ALU.mult),
                     deps=ev_toks, tag=f"hv2_{mc}")
        c1 = s.step("ve", lambda v, mc=mc: v.tensor_scalar(
            u_f[:], h_sb[:, 0:T], smalls["convw"][:, mc * 3: mc * 3 + 1], None, ALU.mult),
            deps=ev_toks + [hv1, hv2], tag=f"cv1_{mc}")
        c2 = s.step("ve", lambda v, mc=mc: v.scalar_tensor_tensor(
            u_f[:], h_sb[:, 1:T + 1], smalls["convw"][:, mc * 3 + 1: mc * 3 + 2], u_f[:],
            ALU.mult, ALU.add), deps=[c1], tag=f"cv2_{mc}")
        c3 = s.step("ve", lambda v, mc=mc: v.scalar_tensor_tensor(
            u_f[:], h_sb[:, 2:T + 2], smalls["convw"][:, mc * 3 + 2: mc * 3 + 3], u_f[:],
            ALU.mult, ALU.add), deps=[c2], tag=f"cv3_{mc}")
        c4 = s.step("ve", lambda v, mc=mc: v.tensor_scalar(
            u_f[:], u_f[:], smalls["convb"][:, mc: mc + 1], None, ALU.add), deps=[c3], tag=f"cv4_{mc}")
        c5 = s.step("sc", lambda sc: sc.activation(out=u_f[:], in_=u_f[:], func=AF.Silu),
                    deps=[c4], tag=f"cv5_{mc}")
        c6 = s.step("ve", lambda v: v.tensor_copy(out=u_b[:], in_=u_f[:]),
                    deps=[c5, s.tok("te")], tag=f"cv6_{mc}")
        c7 = s.step("ve", lambda v, mc=mc: v.tensor_scalar(
            uD_sb[:], u_f[:], smalls["dpar"][:, mc:mc + 1], None, ALU.mult), deps=[c5], tag=f"cv7_{mc}")
        st1 = s.step("sy", lambda y, mc=mc: y.dma_start(
            out=u_sc.ap()[mc * 128:(mc + 1) * 128, :], in_=u_f[:]), deps=[c5], dma=True, tag=f"su{mc}")
        st2 = s.step("sy", lambda y, mc=mc: y.dma_start(
            out=uD_d.ap()[mc * 128:(mc + 1) * 128, :], in_=uD_sb[:]), deps=[c7], dma=True, tag=f"suD{mc}")
        prev_u = st2
        for tc in range(4):
            xdbl_last[tc] = mm_acc2(s, X4[tc][:, 0:512],
                                    [(xpjT[:, mc * 80:(mc + 1) * 80],
                                      u_b[:, tc * 512:(tc + 1) * 512])],
                                    deps=[c6, xdbl_last[tc]],
                                    tag=f"xd{mc}_{tc}", start=(mc == 0), stop=(mc == 15))

    bm_toks = []
    dlr_toks = []
    for tc in range(4):
        xe = s.step("ve", lambda v, tc=tc: v.tensor_scalar(
            xdbl[:, tc * 512:(tc + 1) * 512], X4[tc][:, 0:512], smalls["xpjb"][:, 0:1], None, ALU.add),
            deps=[xdbl_last[tc]], tag=f"xe{tc}")
        dl = s.step("ve", lambda v, tc=tc: v.tensor_copy(out=dlr[:, tc * 512:(tc + 1) * 512],
                                                         in_=xdbl[0:64, tc * 512:(tc + 1) * 512]),
                    deps=[xe], tag=f"dl{tc}")
        dlr_toks.append(dl)
        bm_toks.append(xe)
    stbm = s.step("sy", lambda y: y.dma_start(out=bm_d.ap(), in_=xdbl[64:80, :]),
                  deps=bm_toks, dma=True, tag="stbm")

    prev = {}
    i = 0
    for mc in range(16):
        for tc in range(4):
            k = i % 2
            ps_ = [A, B][k]
            tm = mm_acc(s, ps_[:, 0:512],
                        [(dtwT[:, mc * 128:(mc + 1) * 128], dlr[:, tc * 512:(tc + 1) * 512])],
                        deps=[dlr_toks[tc], prev.get(("p", k))], tag=f"dm{mc}_{tc}")
            lu = s.step("sy", lambda y, mc=mc, tc=tc, k=k: y.dma_start(
                out=ut[k][:], in_=u_sc.ap()[mc * 128:(mc + 1) * 128, tc * 512:(tc + 1) * 512]),
                deps=[prev.get(("u", k)), s.tok("sy") if i == 0 else None], dma=True, tag=f"lu{mc}_{tc}")
            s1 = s.step("ve", lambda v, mc=mc, ps_=ps_: v.tensor_scalar(
                t1sb[:], ps_[:, 0:512], smalls["dtb"][:, mc:mc + 1], None, ALU.add),
                deps=[tm, prev.get("std"), prev.get("stu")], tag=f"s1{mc}_{tc}")
            s2 = s.step("ve", lambda v: v.tensor_scalar_max(t2sb[:], t1sb[:], 0.0), deps=[s1], tag=f"s2{mc}_{tc}")
            s3 = s.step("ve", lambda v: v.tensor_scalar_mul(t3sb[:], t1sb[:], -1.0),
                        deps=[s1, prev.get("s6")], tag=f"s3{mc}_{tc}")
            s4 = s.step("ve", lambda v: v.tensor_tensor(t3sb[:], t1sb[:], t3sb[:], ALU.min), deps=[s3], tag=f"s4{mc}_{tc}")
            s5 = s.step("sc", lambda sc: sc.activation(out=t3sb[:], in_=t3sb[:], func=AF.Exp), deps=[s4], tag=f"s5{mc}_{tc}")
            s6 = s.step("sc", lambda sc: sc.activation(out=t3sb[:], in_=t3sb[:], func=AF.Ln, bias=1.0), deps=[s5], tag=f"s6{mc}_{tc}")
            s7 = s.step("ve", lambda v: v.tensor_add(t1sb[:], t2sb[:], t3sb[:]), deps=[s6, s2], tag=f"s7{mc}_{tc}")
            std = s.step("sy", lambda y, mc=mc, tc=tc: y.dma_start(
                out=delta_d.ap()[mc * 128:(mc + 1) * 128, tc * 512:(tc + 1) * 512], in_=t1sb[:]),
                deps=[s7], dma=True, tag=f"std{mc}_{tc}")
            s8 = s.step("ve", lambda v, k=k: v.tensor_tensor(t2sb[:], t1sb[:], ut[k][:], ALU.mult),
                        deps=[s7, lu], tag=f"s8{mc}_{tc}")
            stu = s.step("sy", lambda y, mc=mc, tc=tc: y.dma_start(
                out=du_d.ap()[mc * 128:(mc + 1) * 128, tc * 512:(tc + 1) * 512], in_=t2sb[:]),
                deps=[s8], dma=True, tag=f"stu{mc}_{tc}")
            prev[("p", k)] = s1
            prev[("u", k)] = s8
            prev["std"] = std
            prev["stu"] = stu
            prev["s6"] = s6
            i += 1
    return p.finish()


def build_m1_scan():
    T = 4096
    p = Prog("m1_scan")
    s = p.s
    de_d = p.din("delta", [1024, T])
    du_d = p.din("du", [1024, T])
    b_d = p.din("bmat", [64, T])
    alog_d = p.din("alog", [128, 2 * 16])
    y_d = p.dout("y", [1024, T])

    Acol = p.sb("Acol", [128, 32], F32)
    dsb = [p.sb(f"dsb{i}", [128, T], F32) for i in range(2)]
    usb = [p.sb(f"usb{i}", [128, T], F32) for i in range(2)]
    ysb = [p.sb(f"ysb{i}", [128, T], F32) for i in range(2)]
    bbc = p.sb("bbc", [128, T], F32)
    dA = p.sb("dA", [128, T], F32)
    dBu = p.sb("dBu", [128, T], F32)
    sco = p.sb("sco", [128, T], F32)

    la = s.step("sy", lambda y: y.dma_start(out=Acol[:], in_=alog_d.ap()), dma=True, tag="la")
    ta = s.step("sc", lambda sc: sc.activation(out=Acol[:], in_=Acol[:], func=AF.Exp), deps=[la], tag="aexp")
    tn = s.step("ve", lambda v: v.tensor_scalar_mul(Acol[:], Acol[:], -1.0), deps=[ta], tag="aneg")

    prev_ld = {}
    prev_y = {}
    for b in range(4):
        lds = []
        for ct in range(2):
            l1 = s.step("sy", lambda y, b=b, ct=ct: y.dma_start(
                out=dsb[ct][:], in_=de_d.ap()[b * 256 + ct * 128: b * 256 + (ct + 1) * 128, :]),
                deps=[prev_ld.get(("d", ct))], dma=True, tag=f"ld{b}_{ct}")
            l2 = s.step("gp", lambda g, b=b, ct=ct: g.dma_start(
                out=usb[ct][:], in_=du_d.ap()[b * 256 + ct * 128: b * 256 + (ct + 1) * 128, :]),
                deps=[prev_ld.get(("u", ct))], dma=True, tag=f"lu{b}_{ct}")
            lds.append((l1, l2))
        ytoks = {0: [], 1: []}
        for n in range(16):
            lb = s.step("sy", lambda y, b=b, n=n: y.dma_start(
                out=bbc[:], in_=AP(b_d, (b * 16 + n) * T, [[0, 128], [1, T]])),
                deps=[s.tok("ve")], dma=True, tag=f"lb{b}_{n}")
            for ct in range(2):
                l1, l2 = lds[ct]
                d1 = s.step("sc", lambda sc, n=n, ct=ct: sc.activation(out=dA[:], in_=dsb[ct][:], func=AF.Exp,
                                                                scale=Acol[:, ct * 16 + n: ct * 16 + n + 1]),
                            deps=[l1, tn, s.tok("ve")], tag=f"da{b}_{n}_{ct}")
                d2 = s.step("ve", lambda v, ct=ct: v.tensor_mul(dBu[:], usb[ct][:], bbc[:]),
                            deps=[l2, lb], tag=f"db{b}_{n}_{ct}")
                d3 = s.step("ve", lambda v: v.tensor_tensor_scan(sco[:], dA[:], dBu[:], 0.0,
                                                                 ALU.mult, ALU.add),
                            deps=[d1, d2], tag=f"sc{b}_{n}_{ct}")
                if n == 0:
                    d4 = s.step("ve", lambda v, ct=ct: v.tensor_copy(out=ysb[ct][:], in_=sco[:]),
                                deps=[d3, prev_y.get(ct)], tag=f"ya{b}_{n}_{ct}")
                else:
                    d4 = s.step("ve", lambda v, ct=ct: v.tensor_add(ysb[ct][:], ysb[ct][:], sco[:]),
                                deps=[d3], tag=f"ya{b}_{n}_{ct}")
                ytoks[ct].append(d4)
        for ct in range(2):
            ty = s.step("gp", lambda g, b=b, ct=ct: g.dma_start(
                out=y_d.ap()[b * 256 + ct * 128: b * 256 + (ct + 1) * 128, :], in_=ysb[ct][:]),
                deps=[ytoks[ct][-1]], dma=True, tag=f"sy{b}_{ct}")
            prev_y[ct] = ty
            prev_ld[("d", ct)] = s.tok("sc")
            prev_ld[("u", ct)] = s.tok("ve")
    return p.finish()


def build_m1_out():
    T = 2048
    p = Prog("m1_out")
    s = p.s
    x_d = p.din("x", [1024, T])
    y_din = p.din("y", [2048, T])
    uD_d = p.din("uD", [2048, T])
    w_d = p.din("outwT", [2048, 1024])
    b_d = p.din("biascol", [128, 8])
    xo_d = p.dout("xo", [1024, T])

    wsb = p.sb("wsb", [128, 16 * 1024], BF16)
    yb = p.sb("yb", [128, 16 * T], BF16)
    bcol = p.sb("bcol", [128, 8], F32)
    ybuf = [p.sb("yb0", [128, T], F32), p.sb("yb1", [128, T], F32)]
    ubuf = [p.sb("ub0", [128, T], F32), p.sb("ub1", [128, T], F32)]
    xbuf = [p.sb("xb0", [128, 512], F32), p.sb("xb1", [128, 512], F32)]
    obuf = [p.sb("ob0", [128, 512], F32), p.sb("ob1", [128, 512], F32)]
    psum = [p.ps("pA", [128, 512], F32), p.ps("pB", [128, 512], F32)]

    wt = [s.step("gp", lambda g: g.dma_start(
        out=wsb[:], in_=AP(w_d, 0, [[1024, 128], [128 * 1024, 16], [1, 1024]])), dma=True, tag="w"),
        s.step("sy", lambda y: y.dma_start(out=bcol[:], in_=b_d.ap()), dma=True, tag="b")]
    add_toks = []
    prev = {}
    for kc in range(16):
        k = kc % 2
        l1 = s.step("sy", lambda y, kc=kc, k=k: y.dma_start(
            out=ybuf[k][:], in_=y_din.ap()[kc * 128:(kc + 1) * 128, :]),
            deps=[prev.get(("y", k))], dma=True, tag=f"ly{kc}")
        l2 = s.step("gp", lambda g, kc=kc, k=k: g.dma_start(
            out=ubuf[k][:], in_=uD_d.ap()[kc * 128:(kc + 1) * 128, :]),
            deps=[prev.get(("u", k))], dma=True, tag=f"lud{kc}")
        tadd = s.step("ve", lambda v, kc=kc, k=k: v.tensor_tensor(
            out=yb[:, kc * T:(kc + 1) * T], in0=ybuf[k][:], in1=ubuf[k][:], op=ALU.add),
            deps=[l1, l2], tag=f"ad{kc}")
        prev[("y", k)] = tadd
        prev[("u", k)] = tadd
        add_toks.append(tadd)
    yb_done = add_toks[-1]

    i = 0
    prev2 = {}
    for mc in range(8):
        for tc in range(T // 512):
            k = i % 2
            ps_, xb, ob = psum[k], xbuf[k], obuf[k]
            tlx = s.step("sy", lambda y, mc=mc, tc=tc, xb=xb: y.dma_start(
                out=xb[:], in_=x_d.ap()[mc * 128:(mc + 1) * 128, tc * 512:(tc + 1) * 512]),
                deps=[prev2.get(("x", k))], dma=True, tag=f"lx{mc}_{tc}")
            tm = mm_acc(s, ps_[:],
                        [(wsb[:, kc * 1024 + mc * 128: kc * 1024 + (mc + 1) * 128],
                          yb[:, kc * T + tc * 512: kc * T + (tc + 1) * 512]) for kc in range(16)],
                        deps=([yb_done] + wt if i == 0 else []) + [prev2.get(("p", k))], tag=f"mm{mc}_{tc}")
            tev = s.step("ve", lambda v, mc=mc, ps_=ps_, xb=xb, ob=ob: v.scalar_tensor_tensor(
                ob[:], ps_[:], bcol[:, mc:mc + 1], xb[:], ALU.add, ALU.add),
                deps=[tm, tlx, prev2.get(("o", k))] + (wt if i == 0 else []), tag=f"ev{mc}_{tc}")
            tst = s.step("sy", lambda y, mc=mc, tc=tc, ob=ob: y.dma_start(
                out=xo_d.ap()[mc * 128:(mc + 1) * 128, tc * 512:(tc + 1) * 512], in_=ob[:]),
                deps=[tev], dma=True, tag=f"st{mc}_{tc}")
            prev2[("x", k)] = tev
            prev2[("p", k)] = tev
            prev2[("o", k)] = tst
            i += 1
    return p.finish()


_BUILDERS["m1_token"] = build_m1_token
_BUILDERS["m1_scan"] = build_m1_scan
_BUILDERS["m1_out"] = build_m1_out


def run_mamba1(xT, inp):
    prog = get_prog("m1_token")
    inwT = np.ascontiguousarray(inp["L1_attn_a_in_w"].T)
    cw = inp["L1_attn_a_conv_w"][:, 0, :]
    convw = np.ascontiguousarray(cw.reshape(16, 128, 3).transpose(1, 0, 2).reshape(128, 48))
    col16 = lambda v: np.ascontiguousarray(v.reshape(16, 128).T)
    xpjT = np.ascontiguousarray(inp["L1_attn_a_x_proj_w"].T)
    dtwT = np.ascontiguousarray(inp["L1_attn_a_dt_w"].T)
    in_maps = []
    for core in range(NCORES):
        b, half = core // 2, core % 2
        t0 = half * 2048
        xe = np.zeros((1024, 2560), np.float32)
        xe[:, 1:2049] = xT[b][:, t0:t0 + 2048]
        hv = np.zeros((128, 2), np.float32)
        if half == 1:
            xe[:, 0] = xT[b][:, 2047]
            hv[:, 0] = 1.0
        if half == 0:
            xe[:, 2049] = xT[b][:, 2048]
            hv[:, 1] = 1.0
        in_maps.append({
            "x": xe, "inwT": inwT, "inb": col16(inp["L1_attn_a_in_b"]),
            "convw": convw, "convb": col16(inp["L1_attn_a_conv_b"]), "hv": hv,
            "xpjT": xpjT, "xpjb": inp["L1_attn_a_x_proj_b"].reshape(80, 1),
            "dtwT": dtwT, "dtb": col16(inp["L1_attn_a_dt_b"]),
            "dpar": col16(inp["L1_attn_a_D_param"]),
        })
    res = prog.run(in_maps)
    delta = [np.concatenate([res[2 * b]["delta"], res[2 * b + 1]["delta"]], axis=1) for b in range(4)]
    du = [np.concatenate([res[2 * b]["du"], res[2 * b + 1]["du"]], axis=1) for b in range(4)]
    uD = [np.concatenate([res[2 * b]["uD"], res[2 * b + 1]["uD"]], axis=1) for b in range(4)]
    bm = [np.concatenate([res[2 * b]["bm"], res[2 * b + 1]["bm"]], axis=1) for b in range(4)]

    prog2 = get_prog("m1_scan")
    bmat_all = np.concatenate(bm, axis=0)
    alog = inp["L1_attn_a_A_log"]
    in_maps2 = []
    for core in range(NCORES):
        ch0 = core * 256
        de_c = np.concatenate([delta[b][ch0:ch0 + 256] for b in range(4)], axis=0)
        du_c = np.concatenate([du[b][ch0:ch0 + 256] for b in range(4)], axis=0)
        al = alog[ch0:ch0 + 256].reshape(2, 128, 16).transpose(1, 0, 2).reshape(128, 32)
        in_maps2.append({
            "delta": np.ascontiguousarray(de_c), "du": np.ascontiguousarray(du_c),
            "bmat": np.ascontiguousarray(bmat_all),
            "alog": np.ascontiguousarray(al),
        })
    res2 = prog2.run(in_maps2)
    y_full = []
    for b in range(4):
        y_full.append(np.concatenate([res2[c]["y"][b * 256:(b + 1) * 256] for c in range(NCORES)], axis=0))

    prog3 = get_prog("m1_out")
    outwT = np.ascontiguousarray(inp["L1_attn_a_out_w"].T)
    bcol = np.ascontiguousarray(inp["L1_attn_a_out_b"].reshape(8, 128).T)
    in_maps3 = []
    for core in range(NCORES):
        b, half = core // 2, core % 2
        tsl = slice(half * 2048, (half + 1) * 2048)
        in_maps3.append({
            "x": np.ascontiguousarray(xT[b][:, tsl]),
            "y": np.ascontiguousarray(y_full[b][:, tsl]),
            "uD": np.ascontiguousarray(uD[b][:, tsl]),
            "outwT": outwT, "biascol": bcol,
        })
    res3 = prog3.run(in_maps3)
    return [np.concatenate([res3[2 * b]["xo"], res3[2 * b + 1]["xo"]], axis=1) for b in range(4)]





def build_mamba2():
    T = 4096
    NC_ = 32
    p = Prog("mamba2")
    s = p.s
    x_d = p.din("x", [1024, T])
    normw_d = p.din("normw", [128, 8])
    inwT_d = p.din("inwT", [512, 2320])
    convw_d = p.din("convw", [128, 40])
    convb_d = p.din("convb", [128, 10])
    dtb_d = p.din("dtbias", [16, 1])
    alog_d = p.din("alog", [16, 1])
    dbc_d = p.din("dbc", [64, 16])
    gnw_d = p.din("gnw", [128, 8])
    outpwT_d = p.din("outpwT", [1024, 512])
    maskf_d = p.din("maskf", [128, 128])
    idb_d = p.din("ident_b", [128, 128], BF16)
    idf_d = p.din("ident_f", [128, 128])
    out_d = p.dout("outh", [T, 512])

    z_sc = p.dtmp("z_sc", [1024, T])
    xf_sc = p.dtmp("xf_sc", [1024, T])
    xbf_sc = p.dtmp("xbf_sc", [1024, T], BF16)
    yg_sc = p.dtmp("yg_sc", [1024, T])
    yh_sc = p.dtmp("yh_sc", [1024, T], BF16)
    Y_sc = p.dtmp("Y_sc", [1024, T])
    sg_sc = p.dtmp("sg_sc", [16, T])
    daf_sc = p.dtmp("daf_sc", [1, 512])

    ones_bf, ones_f, ctoks = emit_consts(p)
    A = p.ps("A", [128, 512], F32)
    B = p.ps("B", [128, 512], F32)
    C2 = p.ps("C2", [128, 512], F32)
    D2 = p.ps("D2", [128, 512], F32)
    E2 = p.ps("E2", [128, 1024], F32)
    Gb = p.ps("Gb", [128, 256], BF16)

    outpwT = p.sb("outpwT_sb", [128, 8 * 512], BF16)
    smalls = {}
    wtok = [s.step("gp", lambda g: g.dma_start(
        out=outpwT[:], in_=AP(outpwT_d, 0, [[512, 128], [128 * 512, 8], [1, 512]])), dma=True, tag="wout")]
    for nm, dd, shp in (("normw", normw_d, [128, 8]), ("convw", convw_d, [128, 40]),
                        ("convb", convb_d, [128, 10]), ("dtbias", dtb_d, [16, 1]),
                        ("alog", alog_d, [16, 1]), ("dbc", dbc_d, [64, 16]),
                        ("gnw", gnw_d, [128, 8]), ("maskf", maskf_d, [128, 128])):
        sb_ = p.sb(nm + "_sb", shp, F32)
        smalls[nm] = sb_
        wtok.append(s.step("sy", lambda y, sb_=sb_, dd=dd: y.dma_start(out=sb_[:], in_=dd.ap()),
                           dma=True, tag=f"c{nm}"))
    idb = p.sb("idb_sb", [128, 128], BF16)
    idf = p.sb("idf_sb", [128, 128], F32)
    wtok.append(s.step("sy", lambda y: y.dma_start(out=idb[:], in_=idb_d.ap()), dma=True, tag="cidb"))
    wtok.append(s.step("sy", lambda y: y.dma_start(out=idf[:], in_=idf_d.ap()), dma=True, tag="cidf"))

    xnh = p.sb("xnh", [128, 4 * T], BF16)
    xt2 = [p.sb("m2xtA", [128, 512], F32), p.sb("m2xtB", [128, 512], F32)]
    sqb = p.sb("m2sq", [128, 512], BF16)
    rqr = p.sb("m2rq", [1, 512], F32)
    xd = x_d.ap()
    mm_last = None
    bc_free = None
    prev_x = {}
    rms_last = None
    for t in range(8):
        for c in range(8):
            k = c % 2
            ld = s.step("sy", lambda y, c=c, t=t, k=k: y.dma_start(
                out=xt2[k][:], in_=xd[c * 128:(c + 1) * 128, t * 512:(t + 1) * 512]),
                deps=[prev_x.get(k)], dma=True, tag=f"rm_ld{t}_{c}")
            tsq = s.step("sc", lambda sc, k=k: sc.activation(out=sqb[:], in_=xt2[k][:], func=AF.Square),
                         deps=[ld, mm_last], tag=f"rm_sq{t}_{c}")
            mm_last = s.step("te", lambda te, c=c: te.matmul(
                A[0:1, 0:512], ones_bf[:, 0:1], sqb[:], start=(c == 0), stop=(c == 7)),
                deps=[tsq] + (list(ctoks) if t == 0 and c == 0 else []), tag=f"rm_mm{t}_{c}")
            prev_x[k] = tsq
        t1 = s.step("ve", lambda v: v.tensor_scalar(rqr[:], A[0:1, 0:512], 1.0 / 1024.0, 1e-5,
                                                    ALU.mult, ALU.add), deps=[mm_last, bc_free], tag=f"rm_s1{t}")
        t2 = s.step("sc", lambda sc: sc.activation(out=rqr[:], in_=rqr[:], func=AF.Sqrt), deps=[t1], tag=f"rm_s2{t}")
        t3 = s.step("ve", lambda v: v.reciprocal(rqr[:], rqr[:]), deps=[t2], tag=f"rm_s3{t}")
        t4 = s.step("te", lambda te: te.matmul(E2[:, 0:512], ones_f[0:1, 0:128], rqr[:],
                                               start=True, stop=True), deps=[t3, bc_free], tag=f"rm_bc{t}")
        toks = []
        for c in range(4):
            k = c % 2
            ld = s.step("sy", lambda y, c=c, t=t, k=k: y.dma_start(
                out=xt2[k][:], in_=xd[c * 128:(c + 1) * 128, t * 512:(t + 1) * 512]),
                deps=[prev_x.get(k), mm_last], dma=True, tag=f"rm_l2{t}_{c}")
            tk = s.step("ve", lambda v, c=c, t=t, k=k: v.scalar_tensor_tensor(
                xnh[:, c * T + t * 512: c * T + (t + 1) * 512], xt2[k][:],
                smalls["normw"][:, c:c + 1], E2[:, 0:512], ALU.mult, ALU.mult),
                deps=[t4, ld] + wtok, tag=f"rm_x{t}_{c}")
            prev_x[k] = tk
            toks.append(tk)
        bc_free = toks[-1]
        rms_last = toks[-1]
    xn_done = rms_last

    inwt2 = [p.sb("m2iwA", [128, 4 * 128], BF16), p.sb("m2iwB", [128, 4 * 128], BF16)]
    Bf_bf = p.sb("m2B", [128, T], BF16)
    Cf_bf = p.sb("m2C", [128, T], BF16)
    pk1 = p.sb("m2pk1", [128, T], F32)
    pk2 = p.sb("m2pk2", [128, T], F32)
    hc = p.sb("m2hc", [128, T + 3], F32)
    cv = p.sb("m2cv", [128, T], F32)
    dtsb = pk1[0:16, :]
    dtA = pk2[0:16, :]
    Sg = hc[0:16, 0:T]
    nSg = pk1[32:48, :]
    t16 = cv[0:16, 0:T]
    ones16 = t16
    ev32 = p.sb("m2ev", [128, 512], F32)
    hz = s.step("ve", lambda v: v.memset(hc[:, 0:3], 0.0), tag="m2hz")
    prev_ev = None
    prev_st = None
    prev_w = {}
    dt_ready = None
    conv_toks = []
    for m in range(19):
        nrow = 128 if m < 18 else 16
        k = m % 2
        lw = s.step("gp", lambda g, m=m, k=k, nrow=nrow: g.dma_start(
            out=inwt2[k][:, 0:4 * nrow].rearrange("p (kc c) -> p kc c", kc=4),
            in_=AP(inwT_d, m * 128, [[2320, 128], [128 * 2320, 4], [1, nrow]])),
            deps=[prev_w.get(k)], dma=True, tag=f"lw{m}")
        ev_toks = []
        for tc in range(8):
            tm = mm_acc(s, A[0:nrow, 0:512],
                        [(inwt2[k][:, kc * nrow: (kc + 1) * nrow],
                          xnh[:, kc * T + tc * 512: kc * T + (tc + 1) * 512]) for kc in range(4)],
                        deps=[xn_done, prev_ev, lw] + (wtok if m == 0 and tc == 0 else []), tag=f"pj{m}_{tc}")
            if m < 8:
                pe = s.step("ve", lambda v, tc=tc: v.tensor_copy(out=ev32[:], in_=A[0:128, 0:512]),
                            deps=[tm, prev_st], tag=f"pz{m}_{tc}")
                st = s.step("sy", lambda y, m=m, tc=tc: y.dma_start(
                    out=z_sc.ap()[m * 128:(m + 1) * 128, tc * 512:(tc + 1) * 512], in_=ev32[:]),
                    deps=[pe], dma=True, tag=f"pzs{m}_{tc}")
                prev_st = st
                prev_ev = pe
            elif m < 18:
                pe = s.step("ve", lambda v, tc=tc: v.tensor_copy(out=hc[:, 3 + tc * 512: 3 + (tc + 1) * 512],
                                                                 in_=A[0:128, 0:512]),
                            deps=[tm], tag=f"px{m}_{tc}")
                prev_ev = pe
                ev_toks.append(pe)
            else:
                pe = s.step("ve", lambda v, tc=tc: v.tensor_copy(out=dtsb[:, tc * 512:(tc + 1) * 512],
                                                                 in_=A[0:16, 0:512]),
                            deps=[tm], tag=f"pd{m}_{tc}")
                prev_ev = pe
                ev_toks.append(pe)
        prev_w[k] = s.tok("te")
        if 8 <= m < 18:
            mc = m - 8
            c1 = s.step("ve", lambda v, mc=mc: v.tensor_scalar(
                cv[:], hc[:, 0:T], smalls["convw"][:, mc * 4: mc * 4 + 1], None, ALU.mult),
                deps=ev_toks + [hz, s.tok("gp"), s.tok("sy")], tag=f"cva{m}")
            c2 = s.step("ve", lambda v, mc=mc: v.scalar_tensor_tensor(
                cv[:], hc[:, 1:T + 1], smalls["convw"][:, mc * 4 + 1: mc * 4 + 2], cv[:],
                ALU.mult, ALU.add), deps=[c1], tag=f"cvb{m}")
            c3 = s.step("ve", lambda v, mc=mc: v.scalar_tensor_tensor(
                cv[:], hc[:, 2:T + 2], smalls["convw"][:, mc * 4 + 2: mc * 4 + 3], cv[:],
                ALU.mult, ALU.add), deps=[c2], tag=f"cvc{m}")
            c4 = s.step("ve", lambda v, mc=mc: v.scalar_tensor_tensor(
                cv[:], hc[:, 3:T + 3], smalls["convw"][:, mc * 4 + 3: mc * 4 + 4], cv[:],
                ALU.mult, ALU.add), deps=[c3], tag=f"cvd{m}")
            c5 = s.step("ve", lambda v, mc=mc: v.tensor_scalar(
                cv[:], cv[:], smalls["convb"][:, mc:mc + 1], None, ALU.add), deps=[c4], tag=f"cve{m}")
            c6 = s.step("sc", lambda sc: sc.activation(out=cv[:], in_=cv[:], func=AF.Silu),
                        deps=[c5], tag=f"cvf{m}")
            if mc < 8:
                c8 = s.step("gp", lambda g, mc=mc: g.dma_start(out=xf_sc.ap()[mc * 128:(mc + 1) * 128, :],
                                                              in_=cv[:]),
                            deps=[c6], dma=True, tag=f"cvh{m}")
                c9 = s.step("gp", lambda g, mc=mc: g.dma_start(out=xbf_sc.ap()[mc * 128:(mc + 1) * 128, :],
                                                               in_=cv[:]),
                            deps=[c6], dma=True, tag=f"cvi{m}")
                conv_toks += [c8, c9]
                prev_ev = s.tok("ve")
            elif mc == 8:
                c7 = s.step("ve", lambda v: v.tensor_copy(out=Bf_bf[:], in_=cv[:]), deps=[c6], tag=f"cvg{m}")
                conv_toks.append(c7)
                prev_ev = c7
            else:
                c7 = s.step("ve", lambda v: v.tensor_copy(out=Cf_bf[:], in_=cv[:]), deps=[c6], tag=f"cvg{m}")
                conv_toks.append(c7)
                prev_ev = c7
        if m == 18:
            dt_ready = ev_toks[-1]

    negSgT = p.sb("m2negSgT", [128, 32 * 16], F32)
    dtT = p.sb("m2dtT", [128, 32 * 16], F32)
    acol = p.sb("m2acol", [16, 2], F32)
    dafr = p.sb("m2dafr", [16, 32], F32)
    dafbc = p.sb("m2dafbc", [128, 512], F32)

    q1 = s.step("ve", lambda v: v.tensor_scalar(dtsb[:], dtsb[:], smalls["dtbias"][:, 0:1], None, ALU.add),
                deps=[dt_ready] + wtok, tag="m2q1")
    q2 = s.step("ve", lambda v: v.tensor_scalar_max(t16[:], dtsb[:], 0.0), deps=[q1], tag="m2q2")
    q3 = s.step("ve", lambda v: v.tensor_scalar_mul(dtA[:], dtsb[:], -1.0), deps=[q1], tag="m2q3")
    q4 = s.step("ve", lambda v: v.tensor_tensor(dtA[:], dtsb[:], dtA[:], ALU.min), deps=[q3], tag="m2q4")
    q5 = s.step("sc", lambda sc: sc.activation(out=dtA[:], in_=dtA[:], func=AF.Exp), deps=[q4], tag="m2q5")
    q6 = s.step("sc", lambda sc: sc.activation(out=dtA[:], in_=dtA[:], func=AF.Ln, bias=1.0), deps=[q5], tag="m2q6")
    q7 = s.step("ve", lambda v: v.tensor_add(dtsb[:], t16[:], dtA[:]), deps=[q6, q2], tag="m2q7")
    o1 = s.step("ve", lambda v: v.memset(ones16[:], 1.0), deps=[q7], tag="m2o16")
    a1 = s.step("sc", lambda sc: sc.activation(out=acol[:, 0:1], in_=smalls["alog"][:, 0:1], func=AF.Exp),
                deps=wtok + [q6], tag="m2a1")
    a2 = s.step("ve", lambda v: v.tensor_scalar_mul(acol[:, 0:1], acol[:, 0:1], -1.0), deps=[a1], tag="m2a2")
    q8 = s.step("ve", lambda v: v.tensor_scalar(dtA[:], dtsb[:], acol[:, 0:1], None, ALU.mult),
                deps=[q7, a2], tag="m2q8")
    q9 = s.step("ve", lambda v: v.tensor_tensor_scan(Sg[:], ones16[:], dtA[:], 0.0, ALU.mult, ALU.add),
                deps=[q8, o1], tag="m2q9")
    q10 = s.step("ve", lambda v: v.tensor_scalar_mul(nSg[:], Sg[:], -1.0), deps=[q9], tag="m2q10")
    q11 = s.step("sy", lambda y: y.dma_start(out=sg_sc.ap(), in_=Sg[:]), deps=[q9], dma=True, tag="m2q11")
    q12 = s.step("ve", lambda v: v.tensor_copy(out=dafr[:], in_=dtA[:, ::128]), deps=[q8], tag="m2q12")
    q13 = s.step("sc", lambda sc: sc.activation(out=dafr[:], in_=dafr[:], func=AF.Exp), deps=[q12], tag="m2q13")
    q14 = s.step("sy", lambda y: y.dma_start(out=AP(daf_sc, 0, [[32, 16], [1, 32]]), in_=dafr[:]),
                 deps=[q13], dma=True, tag="m2q14")
    q15 = s.step("sy", lambda y: y.dma_start(out=dafbc[:], in_=AP(daf_sc, 0, [[0, 128], [1, 512]])),
                 deps=[q14], dma=True, tag="m2q15")
    prev_tr = None
    for c in range(NC_):
        ta_ = s.step("te", lambda te, c=c: [
            te.transpose(E2[0:128, 0:16], nSg[:, c * 128:(c + 1) * 128], idf[32:48, 32:48]),
            te.transpose(E2[0:128, 16:32], dtsb[:, c * 128:(c + 1) * 128], idf[0:16, 0:16])],
            deps=[q10, q7, prev_tr] + wtok, tag=f"m2tr{c}")
        tb_ = s.step("ve", lambda v, c=c: [
            v.tensor_copy(out=negSgT[:, c * 16:(c + 1) * 16], in_=E2[0:128, 0:16]),
            v.tensor_copy(out=dtT[:, c * 16:(c + 1) * 16], in_=E2[0:128, 16:32])],
            deps=[ta_], tag=f"m2te{c}")
        prev_tr = tb_
    tr_done = prev_tr

    sgbc = p.sb("m2sgbc", [128, T], F32)
    xhf = p.sb("m2xhf", [64, T], F32)
    xhb = p.sb("m2xhb", [64, T], BF16)
    Yh = p.sb("m2Yh", [64, T], F32)
    Esb = p.sb("m2Esb", [128, 128], F32)
    Gt_f = p.sb("m2Gtf", [128, 128], F32)
    Gt_b = p.sb("m2Gtb", [128, 128], BF16)
    Xs_b = p.sb("m2Xs", [128, 64], BF16)
    bc_b = p.sb("m2bcb", [128, 128], BF16)
    Cs_b = p.sb("m2Cs", [128, 128], BF16)
    Bw_b = p.sb("m2Bw", [128, 128], BF16)
    Et = p.sb("m2Et", [128, 128], F32)
    dec_c = p.sb("m2dec", [128, 1], F32)
    hst_f = p.sb("m2hstf", [128, 64], F32)
    hst_b = p.sb("m2hstb", [128, 64], BF16)

    prev_head = {}
    for h in range(16):
        lsg = s.step("sy", lambda y, h=h: y.dma_start(
            out=sgbc[:], in_=AP(sg_sc, h * T, [[0, 128], [1, T]])),
            deps=[q11, prev_head.get("sg")], dma=True, tag=f"m2sg{h}")
        lxh = s.step("sy", lambda y, h=h: y.dma_start(
            out=xhf[:], in_=xf_sc.ap()[h * 64:(h + 1) * 64, :]),
            deps=conv_toks + [prev_head.get("xh")], dma=True, tag=f"m2xh{h}")
        lxb = s.step("gp", lambda g, h=h: g.dma_start(
            out=xhb[:], in_=xbf_sc.ap()[h * 64:(h + 1) * 64, :]),
            deps=conv_toks + [prev_head.get("xb")], dma=True, tag=f"m2xb{h}")
        hz1 = s.step("ve", lambda v: v.memset(hst_f[:], 0.0), deps=[prev_head.get("hst")], tag=f"m2hz{h}")
        hz2 = s.step("ve", lambda v: v.memset(hst_b[:], 0.0), deps=[prev_head.get("hstb")], tag=f"m2hzb{h}")
        prev_c = {}
        for c in range(NC_):
            e0 = s.step("ve", lambda v, c=c, h=h: v.tensor_scalar(
                Et[:], sgbc[:, c * 128:(c + 1) * 128],
                negSgT[:, c * 16 + h: c * 16 + h + 1], 0.0, ALU.add, ALU.min),
                deps=[lsg, tr_done, prev_c.get("E")], tag=f"mE0{h}_{c}")
            e1 = s.step("sc", lambda sc: sc.activation(out=Esb[:], in_=Et[:], func=AF.Exp),
                deps=[e0], tag=f"mE{h}_{c}")
            m1 = s.step("te", lambda te, c=c: te.matmul(
                A[0:128, 0:128], Bf_bf[:, c * 128:(c + 1) * 128], Cf_bf[:, c * 128:(c + 1) * 128],
                start=True, stop=True),
                deps=conv_toks + [prev_c.get("CB")], tag=f"mCB{h}_{c}")
            g1 = s.step("ve", lambda v: v.tensor_tensor(Gt_f[:], Esb[:], A[0:128, 0:128], ALU.mult),
                        deps=[e1, m1], tag=f"mG1{h}_{c}")
            g2 = s.step("ve", lambda v: v.tensor_tensor(Gt_b[:], Gt_f[:], smalls["maskf"][:], ALU.mult),
                        deps=[g1], tag=f"mG2{h}_{c}")
            m2 = s.step("te", lambda te, c=c: te.transpose(
                Gb[:, 0:64], xhb[0:64, c * 128:(c + 1) * 128], idb[0:64, 0:64]),
                deps=[lxb, prev_c.get("Xs")] + wtok, tag=f"mXt{h}_{c}")
            x1 = s.step("ve", lambda v, c=c, h=h: v.tensor_scalar(
                Xs_b[:], Gb[:, 0:64], dtT[:, c * 16 + h: c * 16 + h + 1], None, ALU.mult),
                deps=[m2, tr_done], tag=f"mXs{h}_{c}")
            m3 = s.step("te", lambda te: te.matmul(B[0:128, 0:128], ones_f[0:1, 0:128], Esb[0:1, 0:128],
                                                   start=True, stop=True),
                        deps=[e1, prev_c.get("bc")], tag=f"mbc{h}_{c}")
            d1 = s.step("ve", lambda v: v.tensor_copy(out=dec_c[:], in_=B[0:128, 127:128]), deps=[m3], tag=f"mdc{h}_{c}")
            cb1 = s.step("ve", lambda v: v.tensor_copy(out=bc_b[:], in_=B[0:128, 0:128]), deps=[m3], tag=f"mcb{h}_{c}")
            c1_ = s.step("ve", lambda v, c=c: v.tensor_tensor(Cs_b[:], Cf_bf[:, c * 128:(c + 1) * 128],
                                                              bc_b[:], ALU.mult),
                         deps=[cb1] + conv_toks, tag=f"mCs{h}_{c}")
            def ymm(te, c=c, first=(c == 0)):
                insts = [te.matmul(C2[0:64, 0:128], Xs_b[:], Gt_b[:], start=True, stop=first)]
                if not first:
                    insts.append(te.matmul(C2[0:64, 0:128], hst_b[:], Cs_b[:], start=False, stop=True))
                return insts
            m4 = s.step("te", ymm, deps=[g2, x1, c1_, prev_c.get("Y"), prev_c.get("hstb_w")], tag=f"mY{h}_{c}")
            y1 = s.step("ve", lambda v, c=c, h=h: v.scalar_tensor_tensor(
                Yh[:, c * 128:(c + 1) * 128], xhf[:, c * 128:(c + 1) * 128],
                smalls["dbc"][:, h:h + 1], C2[0:64, 0:128], ALU.mult, ALU.add),
                deps=[m4, lxh] + ([prev_head.get("Yst")] if c == 0 else []), tag=f"mYe{h}_{c}")
            m5 = s.step("te", lambda te, c=c: te.transpose(
                Gb[:, 64:192], Bf_bf[:, c * 128:(c + 1) * 128], idb[:]),
                deps=conv_toks + [prev_c.get("Bw")] + wtok, tag=f"mBt{h}_{c}")
            b1 = s.step("ve", lambda v: v.tensor_scalar(Bw_b[:], Gb[:, 64:192], Esb[:, 127:128], None, ALU.mult),
                        deps=[m5, e1], tag=f"mBw{h}_{c}")
            m6 = s.step("te", lambda te: te.matmul(D2[0:128, 0:64], Bw_b[:], Xs_b[:], start=True, stop=True),
                        deps=[b1, x1, prev_c.get("U")], tag=f"mU{h}_{c}")
            v1 = s.step("ve", lambda v: v.scalar_tensor_tensor(hst_f[:], hst_f[:], dec_c[:, 0:1],
                                                               D2[0:128, 0:64], ALU.mult, ALU.add),
                        deps=[m6, d1, hz1], tag=f"mh1{h}_{c}")
            if c < NC_ - 1:
                v2 = s.step("ve", lambda v, c=c, h=h: v.tensor_scalar(
                    hst_f[:], hst_f[:], dafbc[:, h * 32 + c + 1: h * 32 + c + 2], None, ALU.mult),
                    deps=[v1, q15], tag=f"mh2{h}_{c}")
            else:
                v2 = v1
            v3 = s.step("ve", lambda v: v.tensor_copy(out=hst_b[:], in_=hst_f[:]), deps=[v2, hz2], tag=f"mh3{h}_{c}")
            prev_c = {"E": e1, "CB": g1, "Xs": x1, "bc": c1_, "Y": y1,
                      "Bw": b1, "U": v1, "hstb_w": v3}
        sy1 = s.step("sy", lambda y, h=h: y.dma_start(out=Y_sc.ap()[h * 64:(h + 1) * 64, :], in_=Yh[:]),
                     deps=[prev_c["Y"], prev_head.get("Yst")], dma=True, tag=f"m2Yst{h}")
        prev_head = {"sg": s.tok("sc"), "xh": s.tok("ve"), "xb": s.tok("te"),
                     "hst": v2, "hstb": v3, "Yst": sy1}
    head_done = s.tok("sy")

    ygA = [p.sb("m2ygA", [128, 512], F32), p.sb("m2ygB", [128, 512], F32)]
    zA = [p.sb("m2zA", [128, 512], F32), p.sb("m2zB", [128, 512], F32)]
    rq2 = pk2[64:65, :]
    gst_prev = {}
    mmq = None
    for tc in range(8):
        for c in range(8):
            k = c % 2
            l1 = s.step("sy", lambda y, tc=tc, c=c, k=k: y.dma_start(
                out=ygA[k][:], in_=Y_sc.ap()[c * 128:(c + 1) * 128, tc * 512:(tc + 1) * 512]),
                deps=[head_done, gst_prev.get(("y", k))], dma=True, tag=f"gy{tc}_{c}")
            l2 = s.step("gp", lambda g, tc=tc, c=c, k=k: g.dma_start(
                out=zA[k][:], in_=z_sc.ap()[c * 128:(c + 1) * 128, tc * 512:(tc + 1) * 512]),
                deps=[gst_prev.get(("z", k))], dma=True, tag=f"gz{tc}_{c}")
            s1 = s.step("sc", lambda sc, k=k: sc.activation(out=zA[k][:], in_=zA[k][:], func=AF.Silu),
                        deps=[l2], tag=f"gs{tc}_{c}")
            s2 = s.step("ve", lambda v, k=k: v.tensor_tensor(ygA[k][:], ygA[k][:], zA[k][:], ALU.mult),
                        deps=[l1, s1], tag=f"gm{tc}_{c}")
            s3 = s.step("sc", lambda sc, k=k: sc.activation(out=sqb[:], in_=ygA[k][:], func=AF.Square),
                        deps=[s2, mmq], tag=f"gq{tc}_{c}")
            mmq = s.step("te", lambda te, c=c: te.matmul(A[0:1, 0:512], ones_bf[:, 0:1], sqb[:],
                                                         start=(c == 0), stop=(c == 7)),
                         deps=[s3], tag=f"gmm{tc}_{c}")
            s4 = s.step("sy", lambda y, tc=tc, c=c, k=k: y.dma_start(
                out=yg_sc.ap()[c * 128:(c + 1) * 128, tc * 512:(tc + 1) * 512], in_=ygA[k][:]),
                deps=[s2], dma=True, tag=f"gw{tc}_{c}")
            gst_prev[("y", k)] = s2
            gst_prev[("z", k)] = s2
            gst_prev[("st", k)] = s4
        r1 = s.step("ve", lambda v, tc=tc: v.tensor_scalar(rq2[0:1, tc * 512:(tc + 1) * 512],
                                                           A[0:1, 0:512], 1.0 / 1024.0, 1e-5,
                                                           ALU.mult, ALU.add), deps=[mmq], tag=f"gr1{tc}")
        r2 = s.step("sc", lambda sc, tc=tc: sc.activation(out=rq2[0:1, tc * 512:(tc + 1) * 512],
                                                          in_=rq2[0:1, tc * 512:(tc + 1) * 512], func=AF.Sqrt),
                    deps=[r1], tag=f"gr2{tc}")
        r3 = s.step("ve", lambda v, tc=tc: v.reciprocal(rq2[0:1, tc * 512:(tc + 1) * 512],
                                                        rq2[0:1, tc * 512:(tc + 1) * 512]), deps=[r2], tag=f"gr3{tc}")
        mmq = r3
    prevn = {}
    last_n = None
    sc_all = s.tok("sc")
    for tc in range(8):
        t4 = s.step("te", lambda te, tc=tc: te.matmul(E2[:, 0:512], ones_f[64:65, 0:128],
                                                      rq2[0:1, tc * 512:(tc + 1) * 512], start=True, stop=True),
                    deps=[s.tok("ve"), prevn.get("bc")], tag=f"nbc{tc}")
        for c in range(8):
            k = c % 2
            l1 = s.step("sy", lambda y, tc=tc, c=c, k=k: y.dma_start(
                out=ygA[k][:], in_=yg_sc.ap()[c * 128:(c + 1) * 128, tc * 512:(tc + 1) * 512]),
                deps=[gst_prev.get(("st", 0)), gst_prev.get(("st", 1)), prevn.get(("y", k)), sc_all],
                dma=True, tag=f"ny{tc}_{c}")
            n1 = s.step("ve", lambda v, c=c, k=k: v.scalar_tensor_tensor(
                zA[k][:], ygA[k][:], smalls["gnw"][:, c:c + 1], E2[:, 0:512], ALU.mult, ALU.mult),
                deps=[l1, t4, prevn.get(("z", k))], tag=f"nn{tc}_{c}")
            n2 = s.step("gp", lambda g, tc=tc, c=c, k=k: g.dma_start(
                out=yh_sc.ap()[c * 128:(c + 1) * 128, tc * 512:(tc + 1) * 512], in_=zA[k][:]),
                deps=[n1], dma=True, tag=f"ns{tc}_{c}")
            prevn[("y", k)] = n1
            prevn[("z", k)] = n2
            last_n = n2
        prevn["bc"] = last_n
    yhat_done = last_n

    ybk = hc[:, 0:2048].bitcast(BF16)
    ost = [ygA[0], ygA[1]]
    prevo = {}
    for tg in range(8):
        lb = s.step("sy", lambda y, tg=tg: y.dma_start(
            out=ybk[:].rearrange("p (kc n) -> p kc n", kc=8),
            in_=AP(yh_sc, tg * 512, [[T, 128], [128 * T, 8], [1, 512]])),
            deps=[yhat_done, prevo.get("ybk")], dma=True, tag=f"fl{tg}")
        for ti in range(4):
            tt = tg * 4 + ti
            k = tt % 2
            psf = [A, B][k]
            tm = mm_acc(s, psf[0:128, 0:512],
                        [(ybk[:, kc * 512 + ti * 128: kc * 512 + (ti + 1) * 128],
                          outpwT[:, kc * 512:(kc + 1) * 512]) for kc in range(8)],
                        deps=[lb, prevo.get(("p", k))] + (wtok if tt == 0 else []), tag=f"om{tt}")
            ev = s.step("ve", lambda v, k=k, psf=psf: v.tensor_copy(out=ost[k][:], in_=psf[0:128, 0:512]),
                        deps=[tm, prevo.get(("s", k))], tag=f"oe{tt}")
            st = s.step("sy", lambda y, tt=tt, k=k: y.dma_start(
                out=out_d.ap()[tt * 128:(tt + 1) * 128, :], in_=ost[k][:]),
                deps=[ev], dma=True, tag=f"os{tt}")
            prevo[("p", k)] = ev
            prevo[("s", k)] = st
        prevo["ybk"] = s.tok("te")
    return p.finish()


_BUILDERS["mamba2"] = build_mamba2


def run_mamba2(xT, inp):
    prog = get_prog("mamba2")
    maskf = (np.arange(128)[:, None] <= np.arange(128)[None, :]).astype(np.float32)
    idb = np.eye(128, dtype=np.float32).astype(BF)
    idf = np.eye(128, dtype=np.float32)
    nw = inp["L1_attn_b_norm_w"]
    in_maps = []
    for core in range(NCORES):
        b, dr = core // 2, core % 2
        if dr == 0:
            keys = {"in_w": "L1_attn_b_f_in_w", "conv_w": "L1_attn_b_f_conv_w",
                    "conv_b": "L1_attn_b_f_conv_b", "dt_bias": "L1_attn_b_f_dt_bias",
                    "A_log": "L1_attn_b_f_A_log", "D": "L1_attn_b_f_D",
                    "gnorm_w": "L1_attn_b_f_gnorm_w", "outp_w": "L1_attn_b_f_outp_w"}
        else:
            keys = {"in_w": "L1_attn_b_b_in_w", "conv_w": "L1_attn_b_b_conv_w",
                    "conv_b": "L1_attn_b_b_conv_b", "dt_bias": "L1_attn_b_b_dt_bias",
                    "A_log": "L1_attn_b_b_A_log", "D": "L1_attn_b_b_D",
                    "gnorm_w": "L1_attn_b_b_gnorm_w", "outp_w": "L1_attn_b_b_outp_w"}
        g = lambda k: inp[keys[k]]
        xb = xT[b]
        if dr == 0:
            xp = xb
            nwp = nw
        else:
            xp = np.concatenate([xb[512:], xb[:512]], axis=0)[:, ::-1]
            nwp = np.concatenate([nw[512:], nw[:512]])
        convw = np.ascontiguousarray(g("conv_w").reshape(10, 128, 4).transpose(1, 0, 2).reshape(128, 40))
        convb = np.ascontiguousarray(g("conv_b").reshape(10, 128).T)
        in_maps.append({
            "x": np.ascontiguousarray(xp),
            "normw": np.ascontiguousarray(nwp.reshape(8, 128).T),
            "inwT": np.ascontiguousarray(g("in_w").T),
            "convw": convw, "convb": convb,
            "dtbias": g("dt_bias").reshape(16, 1),
            "alog": g("A_log").reshape(16, 1),
            "dbc": np.ascontiguousarray(np.broadcast_to(g("D").reshape(1, 16), (64, 16))).copy(),
            "gnw": np.ascontiguousarray(g("gnorm_w").reshape(8, 128).T),
            "outpwT": np.ascontiguousarray(g("outp_w").T),
            "maskf": maskf, "ident_b": idb, "ident_f": idf,
        })
    res = prog.run(in_maps)
    ctxT = []
    for b in range(4):
        outf = res[2 * b]["outh"]
        outb = res[2 * b + 1]["outh"][::-1]
        full = np.concatenate([outf, outb], axis=1)
        ctxT.append(np.ascontiguousarray(full.T))
    return ctxT


def build_ln_out():
    T = 2048
    p = Prog("ln_out")
    s = p.s
    x_d = p.din("x", [1024, T])
    o_d = p.dout("xn", [1024, T])
    ones_bf, ones_f, ctoks = emit_consts(p)
    P2 = p.ps("P2", [128, 1024], F32)
    P3 = p.ps("P3", [128, 1024], F32)
    xn = p.sb("xn_sb", [128, 8 * T], F32)
    toks = emit_ln(p, x_d, T, xn, T, ones_bf, ones_f, ctoks,
                   P2[0:1, 0:512], P2[0:1, 512:1024], P3, pfx="ln")
    for c in range(8):
        s.step("sy", lambda y, c=c: y.dma_start(out=o_d.ap()[c * 128:(c + 1) * 128, :],
                                                in_=xn[:, c * T:(c + 1) * T]),
               deps=[toks[-1]], dma=True, tag=f"st{c}")
    return p.finish()


def build_proj_res2():
    T = 2048
    p = Prog("proj_res2")
    s = p.s
    x_d = p.din("x", [1024, T])
    x2_d = p.din("x2", [1024, T])
    ctx_d = p.din("ctx", [1024, T])
    w_d = p.din("w", [1024, 1024])
    b_d = p.din("biascol", [128, 8])
    xo_d = p.dout("xo", [1024, T])

    wsb = p.sb("wsb", [128, 8 * 1024], BF16)
    cs = p.sb("cs", [128, 8 * T], BF16)
    bcol = p.sb("bcol", [128, 8], F32)
    xbuf = [p.sb("xb0", [128, 512], F32), p.sb("xb1", [128, 512], F32)]
    x2buf = [p.sb("xc0", [128, 512], F32), p.sb("xc1", [128, 512], F32)]
    obuf = [p.sb("ob0", [128, 512], F32), p.sb("ob1", [128, 512], F32)]
    psum = [p.ps("pA", [128, 512], F32), p.ps("pB", [128, 512], F32)]

    wt = []
    for c in range(8):
        wt.append(s.step("gp", lambda g, c=c: g.dma_start(out=wsb[:, c * 1024:(c + 1) * 1024],
                                                          in_=w_d.ap()[c * 128:(c + 1) * 128, :]),
                         dma=True, tag=f"w{c}"))
        wt.append(s.step("gp", lambda g, c=c: g.dma_start(out=cs[:, c * T:(c + 1) * T],
                                                          in_=ctx_d.ap()[c * 128:(c + 1) * 128, :]),
                         dma=True, tag=f"c{c}"))
    wt.append(s.step("sy", lambda y: y.dma_start(out=bcol[:], in_=b_d.ap()), dma=True, tag="b"))

    i = 0
    prev = {}
    for mc in range(8):
        for tc in range(T // 512):
            k = i % 2
            ps_, xb, x2b, ob = psum[k], xbuf[k], x2buf[k], obuf[k]
            tlx = s.step("sy", lambda y, mc=mc, tc=tc, xb=xb: y.dma_start(
                out=xb[:], in_=x_d.ap()[mc * 128:(mc + 1) * 128, tc * 512:(tc + 1) * 512]),
                deps=[prev.get(("x", k))], dma=True, tag=f"lx{mc}_{tc}")
            tlx2 = s.step("gp", lambda g, mc=mc, tc=tc, x2b=x2b: g.dma_start(
                out=x2b[:], in_=x2_d.ap()[mc * 128:(mc + 1) * 128, tc * 512:(tc + 1) * 512]),
                deps=[prev.get(("x2", k))], dma=True, tag=f"l2{mc}_{tc}")
            tm = mm_acc(s, ps_[:],
                        [(wsb[:, kc * 1024 + mc * 128: kc * 1024 + (mc + 1) * 128],
                          cs[:, kc * T + tc * 512: kc * T + (tc + 1) * 512]) for kc in range(8)],
                        deps=(wt if i == 0 else []) + [prev.get(("p", k))], tag=f"mm{mc}_{tc}")
            ta = s.step("ve", lambda v, xb=xb, x2b=x2b, ob=ob: v.tensor_add(ob[:], xb[:], x2b[:]),
                        deps=[tlx, tlx2, prev.get(("o", k))], tag=f"ta{mc}_{tc}")
            tev = s.step("ve", lambda v, mc=mc, ps_=ps_, ob=ob: v.scalar_tensor_tensor(
                ob[:], ps_[:], bcol[:, mc:mc + 1], ob[:], ALU.add, ALU.add),
                deps=[tm, ta] + (wt if i == 0 else []), tag=f"ev{mc}_{tc}")
            tst = s.step("sy", lambda y, mc=mc, tc=tc, ob=ob: y.dma_start(
                out=xo_d.ap()[mc * 128:(mc + 1) * 128, tc * 512:(tc + 1) * 512], in_=ob[:]),
                deps=[tev], dma=True, tag=f"st{mc}_{tc}")
            prev[("x", k)] = ta
            prev[("x2", k)] = ta
            prev[("p", k)] = tev
            prev[("o", k)] = tst
            i += 1
    return p.finish()


_BUILDERS["ln_out"] = build_ln_out
_BUILDERS["proj_res2"] = build_proj_res2


def run_ln_out(xT):
    prog = get_prog("ln_out")
    in_maps = []
    for core in range(NCORES):
        b, half = core // 2, core % 2
        tsl = slice(half * 2048, (half + 1) * 2048)
        in_maps.append({"x": np.ascontiguousarray(xT[b][:, tsl])})
    res = prog.run(in_maps)
    return [np.concatenate([res[2 * b]["xn"], res[2 * b + 1]["xn"]], axis=1) for b in range(4)]


def run_proj_res2(xT, x2T, ctxT, W, bvec):
    prog = get_prog("proj_res2")
    bcol = np.ascontiguousarray(bvec.reshape(8, 128).T).astype(np.float32)
    in_maps = []
    for core in range(NCORES):
        b, half = core // 2, core % 2
        tsl = slice(half * 2048, (half + 1) * 2048)
        in_maps.append({
            "x": np.ascontiguousarray(xT[b][:, tsl]),
            "x2": np.ascontiguousarray(x2T[b][:, tsl]),
            "ctx": np.ascontiguousarray(ctxT[b][:, tsl]),
            "w": np.ascontiguousarray(W), "biascol": bcol,
        })
    res = prog.run(in_maps)
    return [np.concatenate([res[2 * b]["xo"], res[2 * b + 1]["xo"]], axis=1) for b in range(4)]



def build_l1f_attn():
    T = 4096
    p = Prog("l1f_attn")
    s = p.s
    x_d = p.din("x", [1024, T])
    wqk_d = p.din("wqk", [1024, 1024])
    wv_d = p.din("wv", [1024, 512])
    qs_d = p.din("qs", [64, 2])
    csq_d = p.din("csq", [64, T])
    ssq_d = p.din("ssq", [64, T])
    csk_d = p.din("csk", [64, T])
    ssk_d = p.din("ssk", [64, T])
    mks_d = p.din("mks", [128, 3 * 384])
    idb_d = p.din("ident_b", [128, 128], BF16)
    ctx_d = p.dout("ctx", [512, T], BF16)

    ones_bf, ones_f, ctoks = emit_consts(p)
    A = p.ps("A", [128, 512], F32)
    B = p.ps("B", [128, 512], F32)
    C2 = p.ps("C2", [128, 512], F32)
    D2 = p.ps("D2", [128, 512], F32)
    Gb = p.ps("Gb", [128, 512], BF16)

    xsh = p.sb("xsh", [128, 8 * T], BF16)
    xt2 = [p.sb("xtA", [128, 512], F32), p.sb("xtB", [128, 512], F32)]
    prev_x = {}
    for c in range(8):
        for t in range(8):
            k = (c * 8 + t) % 2
            ld = s.step("sy", lambda y, c=c, t=t, k=k: y.dma_start(
                out=xt2[k][:], in_=x_d.ap()[c * 128:(c + 1) * 128, t * 512:(t + 1) * 512]),
                deps=[prev_x.get(k)], dma=True, tag=f"shl{c}_{t}")
            if c < 4:
                tk = s.step("ve", lambda v, c=c, t=t, k=k: v.tensor_copy(
                    out=xsh[:, c * T + t * 512: c * T + (t + 1) * 512], in_=xt2[k][:]),
                    deps=[ld], tag=f"shc{c}_{t}")
            elif t < 7:
                tk = s.step("ve", lambda v, c=c, t=t, k=k: v.tensor_copy(
                    out=xsh[:, c * T + t * 512 + 1: c * T + (t + 1) * 512 + 1], in_=xt2[k][:]),
                    deps=[ld], tag=f"shc{c}_{t}")
            else:
                tk = s.step("ve", lambda v, c=c, t=t, k=k: v.tensor_copy(
                    out=xsh[:, c * T + t * 512 + 1: (c + 1) * T], in_=xt2[k][:, 0:511]),
                    deps=[ld], tag=f"shc{c}_{t}")
            prev_x[k] = tk
    for c in range(4, 8):
        s.step("ve", lambda v, c=c: v.memset(xsh[:, c * T: c * T + 1], 0.0), tag=f"shz{c}")
    xsh_done = s.tok("ve")

    wqk = p.sb("wqk_sb", [128, 8 * 1024], BF16)
    wv = p.sb("wv_sb", [128, 8 * 512], BF16)
    wtok = [s.step("gp", lambda g: g.dma_start(
        out=wqk[:], in_=AP(wqk_d, 0, [[1024, 128], [128 * 1024, 8], [1, 1024]])), dma=True, tag="wqk"),
        s.step("gp", lambda g: g.dma_start(
            out=wv[:], in_=AP(wv_d, 0, [[512, 128], [128 * 512, 8], [1, 512]])), dma=True, tag="wv")]
    qs_sb = p.sb("qs_sb", [64, 2], F32)
    csq = p.sb("csq_sb", [64, T], BF16)
    ssq = p.sb("ssq_sb", [64, T], BF16)
    csk = p.sb("csk_sb", [64, T], BF16)
    ssk = p.sb("ssk_sb", [64, T], BF16)
    mks = p.sb("mks_sb", [128, 3 * 384], F32)
    idb = p.sb("idb_l", [128, 128], BF16)
    for nm, sb_, dd in (("cq", csq, csq_d), ("sq", ssq, ssq_d),
                        ("ck", csk, csk_d), ("sk", ssk, ssk_d)):
        wtok.append(s.step("gp", lambda g, sb_=sb_, dd=dd: g.dma_start(out=sb_[:], in_=dd.ap()),
                           dma=True, tag=f"c{nm}"))
    for nm, sb_, dd in (("qs", qs_sb, qs_d), ("mk", mks, mks_d), ("ib", idb, idb_d)):
        wtok.append(s.step("sy", lambda y, sb_=sb_, dd=dd: y.dma_start(out=sb_[:], in_=dd.ap()),
                           dma=True, tag=f"c{nm}"))

    qfp = p.sb("qfp_l", [128, T], F32)
    qf = qfp[0:64, :]
    n2r = qfp[64:65, :]
    kf = p.sb("kf_l", [64, T], F32)
    qnb = p.sb("qnb_l", [64, T], BF16)
    rt = p.sb("rt_l", [64, T], BF16)
    qp = p.sb("qp_l", [64, T], BF16)
    kp = p.sb("kp_l", [64, T], BF16)
    sqt = p.sb("sqt_l", [64, 512], BF16)
    vt = p.sb("vt_l", [128, 32 * 130], BF16)
    sm = p.sb("sm_l", [128, 384], F32)
    mcol = p.sb("mcol_l", [128, 2], F32)
    pbf = p.sb("pbf_l", [128, 384], BF16)
    ptb = p.sb("ptb_l", [128, 384], BF16)
    rrow = xt2[1][0:1, 0:512]
    ctmp = xt2[0][0:64, :]
    ctx_sb = p.sb("ctx_sb_l", [64, 512], BF16)

    def memset_vt(v):
        i = None
        for j in range(32):
            i = v.memset(vt[:, j * 130 + 64: j * 130 + 65], 1.0)
            i = v.memset(vt[:, j * 130 + 129: j * 130 + 130], 1.0)
        return i
    t_vones = s.step("ve", memset_vt, tag="vt_ones")

    qs8 = p.sb("qs8_l", [64, 1], F32)
    t_qs8 = s.step("ve", lambda v: v.tensor_scalar_mul(qs8[:], qs_sb[:, 0:1], 8.0),
                   deps=wtok, tag="qs8")
    ctx_store = []
    v_done_pair = {}
    rot_free = ()
    fin_dep = {}
    pv_tok = {}
    prev_T = None
    prev_E = None
    for h in range(8):
        pr, sub = h // 2, h % 2
        ev_prev = None
        for t in range(8):
            dep = ([xsh_done] + wtok) if (t == 0 and h == 0) else []
            if t == 0 and h > 0:
                dep = [s.tok("ve"), s.tok("sc")]
            tm = mm_acc(s, B[0:128, 0:512],
                        [(wqk[:, kc * 1024 + h * 128: kc * 1024 + (h + 1) * 128],
                          xsh[:, kc * T + t * 512: kc * T + (t + 1) * 512]) for kc in range(8)],
                        deps=dep + ([ev_prev] if ev_prev else []), tag=f"lqk{h}_{t}")
            e1 = s.step("ve", lambda v, t=t: v.tensor_copy(out=qf[:, t * 512:(t + 1) * 512],
                                                           in_=B[0:64, 0:512]), deps=[tm], tag=f"lqe{h}_{t}")
            e2 = s.step("ve", lambda v, t=t: v.tensor_copy(out=kf[:, t * 512:(t + 1) * 512],
                                                           in_=B[64:128, 0:512]), deps=[tm], tag=f"lke{h}_{t}")
            ev_prev = e2

        def qkprep(src, dst, cs_t, ss_t, col, tagp):
            toks = []
            for t in range(8):
                sq1 = s.step("sc", lambda sc, t=t, src=src: sc.activation(
                    out=sqt[:], in_=src[:, t * 512:(t + 1) * 512], func=AF.Square),
                    deps=[s.tok("ve"), s.tok("te")], tag=f"{tagp}sq{t}")
                m1 = mm_acc(s, A[0:1, 0:512], [(ones_bf[0:64, 0:1], sqt[:])],
                            deps=[sq1, s.tok("ve")], tag=f"{tagp}m{t}")
                e3 = s.step("ve", lambda v, t=t: v.tensor_copy(out=n2r[0:1, t * 512:(t + 1) * 512],
                                                               in_=A[0:1, 0:512]), deps=[m1], tag=f"{tagp}e{t}")
                toks.append(e3)
            s1 = s.step("sc", lambda sc: sc.activation(out=n2r[:], in_=n2r[:], func=AF.Sqrt),
                        deps=toks, tag=f"{tagp}s1")
            s2 = s.step("ve", lambda v: v.tensor_scalar_max(n2r[:], n2r[:], 1e-12), deps=[s1], tag=f"{tagp}s2")
            s3 = s.step("ve", lambda v: v.reciprocal(n2r[:], n2r[:]), deps=[s2], tag=f"{tagp}s3")
            qn_toks = []
            for t in range(8):
                b1 = s.step("te", lambda te, t=t: te.matmul(B[0:64, 0:512], ones_f[64:65, 0:64],
                                                            n2r[0:1, t * 512:(t + 1) * 512],
                                                            start=True, stop=True),
                            deps=[s3, s.tok("ve")], tag=f"{tagp}b{t}")
                v1 = s.step("ve", lambda v, t=t, src=src: v.tensor_tensor(
                    src[:, t * 512:(t + 1) * 512], src[:, t * 512:(t + 1) * 512],
                    B[0:64, 0:512], ALU.mult), deps=[b1], tag=f"{tagp}v{t}")
                v2 = s.step("ve", lambda v, t=t, src=src, col=col: v.tensor_scalar(
                    qnb[:, t * 512:(t + 1) * 512], src[:, t * 512:(t + 1) * 512],
                    qs8[:, 0:1] if col == 0 else qs_sb[:, 1:2], None,
                    ALU.mult), deps=[v1, t_qs8], tag=f"{tagp}w{t}")
                qn_toks.append(v2)
            r1 = s.step("gp", lambda g: g.dma_start(out=rt[0:32, :], in_=qnb[32:64, :]),
                        deps=qn_toks + list(rot_free), dma=True, tag=f"{tagp}r1")
            r2 = s.step("gp", lambda g: g.dma_start(out=rt[32:64, :], in_=qnb[0:32, :]),
                        deps=qn_toks, dma=True, tag=f"{tagp}r2")
            d_toks = []
            for t in range(8):
                d1 = s.step("ve", lambda v, t=t, dst=dst, cs_t=cs_t: v.tensor_tensor(
                    dst[:, t * 512:(t + 1) * 512], qnb[:, t * 512:(t + 1) * 512],
                    cs_t[:, t * 512:(t + 1) * 512], ALU.mult),
                    deps=[r1, r2] + list(rot_free), tag=f"{tagp}d1{t}")
                d2 = s.step("ve", lambda v, t=t, ss_t=ss_t: v.tensor_tensor(
                    rt[:, t * 512:(t + 1) * 512], rt[:, t * 512:(t + 1) * 512],
                    ss_t[:, t * 512:(t + 1) * 512], ALU.mult), deps=[r1, r2], tag=f"{tagp}d2{t}")
                d3 = s.step("ve", lambda v, t=t, dst=dst: v.tensor_add(
                    dst[:, t * 512:(t + 1) * 512], dst[:, t * 512:(t + 1) * 512],
                    rt[:, t * 512:(t + 1) * 512]), deps=[d1, d2], tag=f"{tagp}d3{t}")
                d_toks.append(d3)
            return d_toks

        qd = qkprep(qf, qp, csq, ssq, 0, f"q{h}_")
        rot_free = (s.tok("ve"), s.tok("te"))
        kd = qkprep(kf, kp, csk, ssk, 1, f"k{h}_")
        rot_free = (s.tok("ve"), s.tok("te"))

        if sub == 0:
            v_toks = []
            for tt in range(32):
                dep = ([xsh_done] + wtok) if (tt == 0 and pr == 0) else []
                tv = mm_acc(s, D2[:, 0:128],
                            [(xsh[:, kc * T + tt * 128: kc * T + (tt + 1) * 128],
                              wv[:, kc * 512 + pr * 128: kc * 512 + (pr + 1) * 128]) for kc in range(8)],
                            deps=dep + [v_toks[-1] if v_toks else t_vones], tag=f"lv{pr}_{tt}")
                ev = s.step("ve", lambda v, tt=tt: [
                    v.tensor_copy(out=vt[:, tt * 130: tt * 130 + 64], in_=D2[:, 0:64]),
                    v.tensor_copy(out=vt[:, tt * 130 + 65: tt * 130 + 129], in_=D2[:, 64:128])],
                    deps=[tv], tag=f"lve{pr}_{tt}")
                v_toks.append(ev)
            v_done_pair[pr] = v_toks[-1]
        v_done = v_done_pair[pr]

        for qt in range(32):
            q0 = qt * 128
            kstart = max(q0 - 256, 0)
            kw = q0 + 128 - kstart
            mi = 0 if qt >= 2 else (1 if qt == 0 else 2)
            tS = s.step("te", lambda te, qt=qt, kstart=kstart, kw=kw: te.matmul(
                A[0:128, 0:kw], qp[:, qt * 128:(qt + 1) * 128], kp[:, kstart:kstart + kw],
                start=True, stop=True),
                deps=qd + kd + [s.tok("ve") if qt == 0 else None], tag=f"S{h}_{qt}")
            t1 = s.step("ve", lambda v, mi=mi, kw=kw: v.tensor_tensor(
                sm[:, 0:kw], A[0:128, 0:kw], mks[:, mi * 384: mi * 384 + kw], ALU.add),
                deps=[tS, prev_E], tag=f"M{h}_{qt}")
            t2 = s.step("ve", lambda v, kw=kw: v.reduce_max(mcol[:, 0:1], sm[:, 0:kw], axis=AX.X),
                        deps=[t1], tag=f"R{h}_{qt}")
            t3 = s.step("ve", lambda v: v.tensor_scalar_mul(mcol[:, 1:2], mcol[:, 0:1], -1.0),
                        deps=[t2], tag=f"N{h}_{qt}")
            t4 = s.step("sc", lambda sc, kw=kw: sc.activation(out=pbf[:, 0:kw], in_=sm[:, 0:kw],
                                                              func=AF.Exp, bias=mcol[:, 1:2]),
                        deps=[t3, prev_T], tag=f"E{h}_{qt}")
            prev_E = t4
            nkt = kw // 128
            tt_ = s.step("te", lambda te, nkt=nkt: [te.transpose(
                Gb[:, j * 128:(j + 1) * 128], pbf[:, j * 128:(j + 1) * 128], idb[:])
                for j in range(nkt)],
                deps=[t4, prev_T] + wtok, tag=f"T{h}_{qt}")
            tev = s.step("ve", lambda v, nkt=nkt: v.tensor_copy(out=ptb[:, 0:nkt * 128],
                                                                in_=Gb[:, 0:nkt * 128]),
                         deps=[tt_], tag=f"TE{h}_{qt}")
            prev_T = tev
            oc = (qt % 4) * 128
            def pv_emit(te, kstart=kstart, nkt=nkt, sub=sub, oc=oc):
                insts = []
                for j in range(nkt):
                    ktile = kstart // 128 + j
                    insts.append(te.matmul(C2[0:65, oc:oc + 128],
                                           vt[:, ktile * 130 + sub * 65: ktile * 130 + sub * 65 + 65],
                                           ptb[:, j * 128:(j + 1) * 128],
                                           start=(j == 0), stop=(j == nkt - 1)))
                return insts
            tPV = s.step("te", pv_emit, deps=[tev, v_done, fin_dep.get(qt % 4)], tag=f"PV{h}_{qt}")
            pv_tok[qt % 4] = tPV
            if qt % 4 == 3:
                qb = qt - 3
                f1 = s.step("ve", lambda v: v.reciprocal(rrow[:], C2[64:65, 0:512]),
                            deps=[pv_tok[j] for j in range(4)], tag=f"f1{h}_{qt}")
                f2 = s.step("te", lambda te: te.matmul(B[0:64, 0:512], ones_f[0:1, 0:64], rrow[:],
                                                       start=True, stop=True),
                            deps=[f1, s.tok("ve")], tag=f"f2{h}_{qt}")
                f3 = s.step("ve", lambda v: v.tensor_copy(out=ctmp[:], in_=C2[0:64, 0:512]),
                            deps=[f1], tag=f"f3{h}_{qt}")
                f4 = s.step("ve", lambda v: v.tensor_tensor(
                    out=ctx_sb[:], in0=ctmp[:], in1=B[0:64, 0:512], op=ALU.mult),
                    deps=[f2, f3] + ctx_store[-1:], tag=f"f4{h}_{qt}")
                tst8 = s.step("sy", lambda y, h=h, qb=qb: y.dma_start(
                    out=ctx_d.ap()[h * 64:(h + 1) * 64, qb * 128:(qb + 4) * 128], in_=ctx_sb[:]),
                    deps=[f4], dma=True, tag=f"cx{h}_{qt}")
                ctx_store.append(tst8)
                for j in range(4):
                    fin_dep[j] = f4
    return p.finish()


def build_l1f_ffn():
    T = 2048
    TH = T + 1
    p = Prog("l1f_ffn")
    s = p.s
    x_d = p.din("x", [1024, TH])
    ctx_d = p.din("ctx", [1024, TH], BF16)
    wo_d = p.din("wo", [1024, 1024])
    lng_d = p.din("lng", [128, 8])
    lnb_d = p.din("lnb", [128, 8])
    w1_d = p.din("w1", [1024, 5632])
    b1a_d = p.din("b1a", [128, 22])
    b1g_d = p.din("b1g", [128, 22])
    w2_d = p.din("w2", [2816, 1024])
    b2_d = p.din("b2col", [128, 8])
    xo_d = p.dout("xo", [1024, T])
    x1_sc = p.dtmp("x1_sc", [1024, TH])

    ones_bf, ones_f, ctoks = emit_consts(p)
    A = p.ps("A", [128, 512], F32)
    B = p.ps("B", [128, 512], F32)
    A2 = p.ps("Ax", [128, 512], F32)
    B2 = p.ps("Bx", [128, 512], F32)
    An = [A, A2]
    Bn = [B, B2]
    E2 = p.ps("E2", [128, 1024], F32)
    P2 = p.ps("P2", [128, 1024], F32)

    wosb = p.sb("wosb", [128, 8 * 1024], BF16)
    ctxb = [p.sb("fcxA", [128, 8 * 544], BF16), p.sb("fcxB", [128, 8 * 544], BF16)]
    smalls = {}
    wtok = [s.step("gp", lambda g: g.dma_start(
        out=wosb[:], in_=AP(wo_d, 0, [[1024, 128], [128 * 1024, 8], [1, 1024]])), dma=True, tag="wo")]
    for nm, dd, shp in (("lng", lng_d, [128, 8]), ("lnb", lnb_d, [128, 8]),
                        ("b1a", b1a_d, [128, 22]), ("b1g", b1g_d, [128, 22]),
                        ("b2", b2_d, [128, 8])):
        sb_ = p.sb(nm + "_sb", shp, F32)
        smalls[nm] = sb_
        wtok.append(s.step("sy", lambda y, sb_=sb_, dd=dd: y.dma_start(out=sb_[:], in_=dd.ap()),
                           dma=True, tag=f"c{nm}"))

    xbuf = [p.sb("fxA", [128, 544], F32), p.sb("fxB", [128, 544], F32)]
    o1b = [p.sb("foA1", [128, 544], F32), p.sb("foB1", [128, 544], F32)]
    prev = {}
    tcs = [(0, 513), (513, 512), (1025, 512), (1537, 512)]
    x1_last = None
    for tci, (c0, n) in enumerate(tcs):
        k = tci % 2
        lc = s.step("gp", lambda g, c0=c0, n=n, k=k: g.dma_start(
            out=ctxb[k][:, 0:8 * n].rearrange("p (c m) -> p c m", c=8),
            in_=AP(ctx_d, c0, [[TH, 128], [128 * TH, 8], [1, n]])),
            deps=[prev.get(("c", k))], dma=True, tag=f"p1c{tci}")
        for mc in range(8):
            kk2 = (tci * 8 + mc) % 2
            tlx = s.step("sy", lambda y, mc=mc, c0=c0, n=n, k=k: y.dma_start(
                out=xbuf[k][:, 0:n], in_=x_d.ap()[mc * 128:(mc + 1) * 128, c0:c0 + n]),
                deps=[prev.get(("x", k))], dma=True, tag=f"p1x{tci}_{mc}")
            def mme(te, mc=mc, n=n, k=k, kk2=kk2):
                insts = []
                for off, nn in _pieces(0, n):
                    ps_ = An[kk2] if off == 0 else Bn[kk2]
                    for kc in range(8):
                        insts.append(te.matmul(ps_[0:128, 0:nn],
                                               wosb[:, kc * 1024 + mc * 128: kc * 1024 + (mc + 1) * 128],
                                               ctxb[k][:, kc * n + off: kc * n + off + nn],
                                               start=(kc == 0), stop=(kc == 7)))
                return insts
            tm = s.step("te", mme, deps=[lc, prev.get(("p", kk2))] + (wtok if tci == 0 and mc == 0 else []),
                        tag=f"p1m{tci}_{mc}")
            def eve(v, mc=mc, n=n, k=k, kk2=kk2):
                insts = [v.tensor_tensor(o1b[k][:, 0:min(n, 512)],
                                         xbuf[k][:, 0:min(n, 512)], An[kk2][0:128, 0:min(n, 512)], ALU.add)]
                if n == 513:
                    insts.append(v.tensor_tensor(o1b[k][:, 512:513],
                                                 xbuf[k][:, 512:513], Bn[kk2][0:128, 0:1], ALU.add))
                return insts
            tev = s.step("ve", eve, deps=[tm, tlx, prev.get(("e", k))], tag=f"p1e{tci}_{mc}")
            tst = s.step("sy", lambda y, mc=mc, c0=c0, n=n, k=k: y.dma_start(
                out=x1_sc.ap()[mc * 128:(mc + 1) * 128, c0:c0 + n], in_=o1b[k][:, 0:n]),
                deps=[tev], dma=True, tag=f"p1s{tci}_{mc}")
            prev[("x", k)] = tev
            prev[("p", kk2)] = tev
            prev[("e", k)] = tst
            x1_last = tst
        prev[("c", k)] = s.tok("te")
    x1_done = x1_last

    xn = p.sb("xnf", [128, 8 * T], BF16)
    xg = p.sb("fxg", [128, 8 * 512], F32)
    cast_bf = p.sb("fcast", [128, 512], BF16)
    sq_bf = p.sb("fsq", [128, 512], BF16)
    sm_tmp = p.sb("fsm", [128, 512], F32)
    row_v = p.sb("frv", [1, 512], F32)
    row_r = p.sb("frr", [1, 512], F32)
    row_m = p.sb("frm", [1, 512], F32)

    mm_last_m = mm_last_q = None
    row_free = None
    bc_free = None
    grp_free = None
    for t in range(4):
        ld = []
        for c in range(8):
            off = (1 + t * 512) if c < 4 else (t * 512)
            ld.append(s.step("sy", lambda y, c=c, off=off: y.dma_start(
                out=xg[:, c * 512:(c + 1) * 512], in_=x1_sc.ap()[c * 128:(c + 1) * 128, off:off + 512]),
                deps=[x1_done if t == 0 and c == 0 else None,
                      grp_free if c == 0 else None], dma=True, tag=f"l2d{t}_{c}"))
        for c in range(8):
            tcast = s.step("ve", lambda v, c=c: v.tensor_copy(out=cast_bf[:], in_=xg[:, c * 512:(c + 1) * 512]),
                           deps=[ld[c], mm_last_m] + list(ctoks), tag=f"lca{t}_{c}")
            tsq = s.step("sc", lambda sc, c=c: sc.activation(out=sq_bf[:], in_=xg[:, c * 512:(c + 1) * 512],
                                                             func=AF.Square),
                         deps=[ld[c], mm_last_q], tag=f"lsq{t}_{c}")
            mm_last_m = s.step("te", lambda te, c=c: te.matmul(
                A[0:1, 0:512], ones_bf[:, 0:1], cast_bf[:], start=(c == 0), stop=(c == 7)),
                deps=[tcast, row_free if c == 0 else None], tag=f"lnm{t}_{c}")
            mm_last_q = s.step("te", lambda te, c=c: te.matmul(
                B[0:1, 0:512], ones_bf[:, 0:1], sq_bf[:], start=(c == 0), stop=(c == 7)),
                deps=[tsq, row_free if c == 0 else None], tag=f"lnq{t}_{c}")
        t1 = s.step("ve", lambda v: v.tensor_scalar_mul(row_m[:], A[0:1, 0:512], 1.0 / 1024.0),
                    deps=[mm_last_m, row_free], tag=f"ls1{t}")
        t2 = s.step("ve", lambda v: v.tensor_mul(row_v[:], row_m[:], row_m[:]), deps=[t1], tag=f"ls2{t}")
        t3 = s.step("ve", lambda v: v.scalar_tensor_tensor(row_v[:], B[0:1, 0:512], 1.0 / 1024.0, row_v[:],
                                                           ALU.mult, ALU.subtract),
                    deps=[t2, mm_last_q], tag=f"ls3{t}")
        t4 = s.step("ve", lambda v: v.tensor_scalar_add(row_v[:], row_v[:], 1e-5), deps=[t3], tag=f"ls4{t}")
        t5 = s.step("sc", lambda sc: sc.activation(out=row_r[:], in_=row_v[:], func=AF.Sqrt),
                    deps=[t4, bc_free], tag=f"ls5{t}")
        t6 = s.step("ve", lambda v: v.reciprocal(row_r[:], row_r[:]), deps=[t5], tag=f"ls6{t}")
        t7 = s.step("ve", lambda v: v.tensor_mul(row_m[:], row_m[:], row_r[:]), deps=[t6], tag=f"ls7{t}")
        tb = s.step("te", lambda te: [
            te.matmul(E2[:, 0:512], ones_f[0:1, 0:128], row_r[:], start=True, stop=True),
            te.matmul(E2[:, 512:1024], ones_f[0:1, 0:128], row_m[:], start=True, stop=True)],
            deps=[t6, t7, bc_free], tag=f"lbc{t}")
        row_free = tb
        toks = []
        for c in range(8):
            ta = s.step("ve", lambda v, c=c: v.tensor_tensor(
                out=sm_tmp[:], in0=xg[:, c * 512:(c + 1) * 512], in1=E2[:, 512:1024], op=ALU.subtract),
                deps=[tb], tag=f"lxa{t}_{c}")
            tbb = s.step("ve", lambda v: v.tensor_tensor(
                out=sm_tmp[:], in0=sm_tmp[:], in1=E2[:, 0:512], op=ALU.mult),
                deps=[ta], tag=f"lxb{t}_{c}")
            tcc = s.step("ve", lambda v, c=c, t=t: v.tensor_scalar(
                xn[:, c * T + t * 512: c * T + (t + 1) * 512], sm_tmp[:],
                smalls["lng"][:, c:c + 1], smalls["lnb"][:, c:c + 1], ALU.mult, ALU.add),
                deps=[tbb], tag=f"lxc{t}_{c}")
            toks.append(tcc)
        bc_free = toks[-1]
        grp_free = toks[-1]
    xn_done = bc_free

    w1sb = [p.sb("w1A", [128, 8 * 128], BF16), p.sb("w1B", [128, 8 * 128], BF16)]
    w1gb = [p.sb("w1gA", [128, 8 * 128], BF16), p.sb("w1gB", [128, 8 * 128], BF16)]
    w2all = p.sb("w2all", [128, 22 * 1024], BF16)
    hall = p.sb("hall", [128, 22 * 512], BF16)
    gltb = [p.sb("glt", [128, 512], BF16), p.sb("glt2", [128, 512], BF16)]
    obuf = [p.sb("foA", [128, 512], F32), p.sb("foB", [128, 512], F32)]
    x3b = [p.sb("fx3A", [128, 512], F32), p.sb("fx3B", [128, 512], F32)]
    w2toks = []
    for j in range(22):
        w2toks.append(s.step("gp", lambda g, j=j: g.dma_start(
            out=w2all[:, j * 1024:(j + 1) * 1024],
            in_=AP(w2_d, j * 128 * 1024, [[1024, 128], [1, 1024]])),
            deps=[], dma=True, tag=f"lw2_{j}"))
    prev_w = {}
    pslast = {}
    prevo = {}
    prevx3 = {}
    for tc in range(4):
        h_toks = []
        for j in range(22):
            kw = j % 2
            lw1 = s.step("gp", lambda g, j=j, kw=kw: g.dma_start(
                out=w1sb[kw][:].rearrange("p (kc c) -> p kc c", kc=8),
                in_=AP(w1_d, j * 128, [[5632, 128], [128 * 5632, 8], [1, 128]])),
                deps=[prev_w.get(("a", kw))], dma=True, tag=f"lw1a{tc}_{j}")
            lw2 = s.step("gp", lambda g, j=j, kw=kw: g.dma_start(
                out=w1gb[kw][:].rearrange("p (kc c) -> p kc c", kc=8),
                in_=AP(w1_d, 2816 + j * 128, [[5632, 128], [128 * 5632, 8], [1, 128]])),
                deps=[prev_w.get(("g", kw))], dma=True, tag=f"lw1g{tc}_{j}")
            tma = mm_acc(s, A[0:128, 0:512],
                         [(w1sb[kw][:, kc * 128:(kc + 1) * 128],
                           xn[:, kc * T + tc * 512: kc * T + (tc + 1) * 512]) for kc in range(8)],
                         deps=[xn_done, lw1, s.tok("ve")], tag=f"f3a{tc}_{j}")
            tmg = mm_acc(s, B[0:128, 0:512],
                         [(w1gb[kw][:, kc * 128:(kc + 1) * 128],
                           xn[:, kc * T + tc * 512: kc * T + (tc + 1) * 512]) for kc in range(8)],
                         deps=[xn_done, lw2, s.tok("sc")], tag=f"f3g{tc}_{j}")
            prev_w[("a", kw)] = tma
            prev_w[("g", kw)] = tmg
            a1 = s.step("sc", lambda sc, j=j, kw=kw: sc.activation(out=gltb[kw][:], in_=B[0:128, 0:512],
                                                                   func=AF.Gelu, bias=smalls["b1g"][:, j:j + 1]),
                        deps=[tmg, prev_w.get(("h", kw))], tag=f"f3s{tc}_{j}")
            a2 = s.step("ve", lambda v, j=j, kw=kw: v.scalar_tensor_tensor(
                hall[:, j * 512:(j + 1) * 512], A[0:128, 0:512], smalls["b1a"][:, j:j + 1], gltb[kw][:],
                ALU.add, ALU.mult), deps=[tma, a1], tag=f"f3h{tc}_{j}")
            prev_w[("h", kw)] = a2
            h_toks.append(a2)
        for mc in range(8):
            kk = mc % 2
            py = [E2[:, 0:512], P2[:, 0:512]][kk]
            lx3 = s.step("sy", lambda y, mc=mc, tc=tc, kk=kk: y.dma_start(
                out=x3b[kk][:], in_=x1_sc.ap()[mc * 128:(mc + 1) * 128, 1 + tc * 512: 1 + (tc + 1) * 512]),
                deps=[x1_done, prevx3.get(kk)], dma=True, tag=f"f3x{tc}_{mc}")
            tmy = mm_acc(s, py,
                         [(w2all[:, j * 1024 + mc * 128: j * 1024 + (mc + 1) * 128],
                           hall[:, j * 512:(j + 1) * 512]) for j in range(22)],
                         deps=h_toks + w2toks + [pslast.get(kk)], tag=f"f3y{tc}_{mc}")
            tev = s.step("ve", lambda v, mc=mc, py=py, kk=kk: v.scalar_tensor_tensor(
                obuf[kk][:], py, smalls["b2"][:, mc:mc + 1], x3b[kk][:], ALU.add, ALU.add),
                deps=[tmy, lx3, prevo.get(kk)], tag=f"f3e{tc}_{mc}")
            tst = s.step("sy", lambda y, mc=mc, tc=tc, kk=kk: y.dma_start(
                out=xo_d.ap()[mc * 128:(mc + 1) * 128, tc * 512:(tc + 1) * 512], in_=obuf[kk][:]),
                deps=[tev], dma=True, tag=f"f3st{tc}_{mc}")
            pslast[kk] = tev
            prevo[kk] = tst
            prevx3[kk] = tev
    return p.finish()


def build_add2():
    T = 2048
    p = Prog("add2")
    s = p.s
    x_d = p.din("x", [1024, T])
    x2_d = p.din("x2", [1024, T])
    xo_d = p.dout("xo", [1024, T])
    b1 = [p.sb("aA", [128, 512], F32), p.sb("aB", [128, 512], F32)]
    b2 = [p.sb("bA", [128, 512], F32), p.sb("bB", [128, 512], F32)]
    b3 = [p.sb("cA", [128, 512], F32), p.sb("cB", [128, 512], F32)]
    prev = {}
    i = 0
    for mc in range(8):
        for tc in range(4):
            k = i % 2
            l1 = s.step("sy", lambda y, mc=mc, tc=tc, k=k: y.dma_start(
                out=b1[k][:], in_=x_d.ap()[mc * 128:(mc + 1) * 128, tc * 512:(tc + 1) * 512]),
                deps=[prev.get(("1", k))], dma=True, tag=f"l1{mc}_{tc}")
            l2 = s.step("gp", lambda g, mc=mc, tc=tc, k=k: g.dma_start(
                out=b2[k][:], in_=x2_d.ap()[mc * 128:(mc + 1) * 128, tc * 512:(tc + 1) * 512]),
                deps=[prev.get(("2", k))], dma=True, tag=f"l2{mc}_{tc}")
            ta = s.step("ve", lambda v, k=k: v.tensor_add(b3[k][:], b1[k][:], b2[k][:]),
                        deps=[l1, l2, prev.get(("3", k))], tag=f"a{mc}_{tc}")
            tst = s.step("sy", lambda y, mc=mc, tc=tc, k=k: y.dma_start(
                out=xo_d.ap()[mc * 128:(mc + 1) * 128, tc * 512:(tc + 1) * 512], in_=b3[k][:]),
                deps=[ta], dma=True, tag=f"s{mc}_{tc}")
            prev[("1", k)] = ta
            prev[("2", k)] = ta
            prev[("3", k)] = tst
            i += 1
    return p.finish()


_BUILDERS["l1f_attn"] = build_l1f_attn
_BUILDERS["l1f_ffn"] = build_l1f_ffn
_BUILDERS["add2"] = build_add2


def _l1f_tables():
    T, DH, W = 4096, 64, 256
    half = np.arange(0, DH, 2, dtype=np.float32)
    inv_freq = (1.0 / 10000.0 ** (half / DH)).astype(np.float32)
    t_ = np.arange(T, dtype=np.float32)
    fr = t_[:, None] * inv_freq[None, :]
    fr = np.concatenate([fr, fr], -1)
    base = (half + 0.4 * DH) / (1.4 * DH)
    power = (t_ - T // 2) / (W // 2)
    sc = (base[None, :] ** power[:, None]).astype(np.float32)
    sc = np.concatenate([sc, sc], -1)
    cos, sin = np.cos(fr), np.sin(fr)
    sign = np.concatenate([-np.ones(32), np.ones(32)]).astype(np.float32)
    csq = np.ascontiguousarray((cos * sc).T)
    ssq = np.ascontiguousarray((sign[None, :] * sin * sc).T)
    csk = np.ascontiguousarray((cos / sc).T)
    ssk = np.ascontiguousarray((sign[None, :] * sin / sc).T)
    ql = np.arange(128)[:, None]
    kl = np.arange(384)[None, :]
    m_std = np.where((kl >= ql) & (kl <= ql + 256), 0.0, -1e30).astype(np.float32)
    m_q0 = np.full((128, 384), -1e30, np.float32)
    m_q0[:, :128] = np.where(kl[:, :128] <= ql, 0.0, -1e30)
    m_q1 = np.full((128, 384), -1e30, np.float32)
    m_q1[:, :256] = np.where(kl[:, :256] <= 128 + ql, 0.0, -1e30)
    mks = np.concatenate([m_std, m_q0, m_q1], axis=1)
    return csq, ssq, csk, ssk, mks


def run_l1f_attn(xT, inp, li):
    prog = get_prog("l1f_attn")
    csq, ssq, csk, ssk, mks = _l1f_tables()
    Wqkv = inp["L1_ffn_Wqkv"][li]
    qsc = inp["L1_ffn_q_scale"][li]
    ksc = inp["L1_ffn_k_scale"][li]
    idb = np.eye(128, dtype=np.float32).astype(BF)
    in_maps = []
    for core in range(NCORES):
        b, hg = core // 2, core % 2
        wqk = np.empty((1024, 1024), np.float32)
        for hh in range(8):
            habs = hg * 8 + hh
            wqk[:, hh * 128: hh * 128 + 64] = Wqkv[:, habs * 64:(habs + 1) * 64]
            wqk[:, hh * 128 + 64: (hh + 1) * 128] = Wqkv[:, 1024 + habs * 64: 1024 + (habs + 1) * 64]
        wv = np.ascontiguousarray(Wqkv[:, 2048 + hg * 512: 2048 + (hg + 1) * 512])
        in_maps.append({
            "x": np.ascontiguousarray(xT[b]),
            "wqk": wqk, "wv": wv,
            "qs": np.ascontiguousarray(np.stack([qsc, ksc], axis=1)),
            "csq": csq, "ssq": ssq, "csk": csk, "ssk": ssk, "mks": mks,
            "ident_b": idb,
        })
    res = prog.run(in_maps)
    return [np.concatenate([res[2 * b]["ctx"], res[2 * b + 1]["ctx"]], axis=0) for b in range(4)]


def run_l1f_ffn(xT, ctxT, inp, li):
    prog = get_prog("l1f_ffn")
    Wo = inp["L1_ffn_Wo"][li]
    W1 = inp["L1_ffn_W1"][li]
    b1 = inp["L1_ffn_b1"][li]
    W2 = inp["L1_ffn_W2"][li]
    b2 = inp["L1_ffn_b2"][li]
    lng = inp["L1_ffn_ln_g"][li]
    lnb = inp["L1_ffn_ln_b"][li]
    w1p = np.zeros((1024, 5632), np.float32)
    w1p[:, 0:2730] = W1[:, 0:2730]
    w1p[:, 2816:2816 + 2730] = W1[:, 2730:5460]
    b1a = np.zeros((2816,), np.float32)
    b1a[0:2730] = b1[0:2730]
    b1g = np.zeros((2816,), np.float32)
    b1g[0:2730] = b1[2730:5460]
    w2p = np.zeros((2816, 1024), np.float32)
    w2p[0:2730] = W2
    col = lambda v, n: np.ascontiguousarray(v.reshape(n, 128).T)
    in_maps = []
    for core in range(NCORES):
        b, half = core // 2, core % 2
        t0 = half * 2048
        xh = np.zeros((1024, 2049), np.float32)
        ch = np.zeros((1024, 2049), BF)
        xh[:, 1:] = xT[b][:, t0:t0 + 2048]
        ch[:, 1:] = ctxT[b][:, t0:t0 + 2048]
        if half == 1:
            xh[:, 0] = xT[b][:, 2047]
            ch[:, 0] = ctxT[b][:, 2047]
        in_maps.append({
            "x": xh, "ctx": ch, "wo": np.ascontiguousarray(Wo),
            "lng": col(lng, 8), "lnb": col(lnb, 8),
            "w1": w1p, "b1a": col(b1a, 22), "b1g": col(b1g, 22),
            "w2": w2p, "b2col": col(b2, 8),
        })
    res = prog.run(in_maps)
    return [np.concatenate([res[2 * b]["xo"], res[2 * b + 1]["xo"]], axis=1) for b in range(4)]


def run_add2(xT, x2T):
    prog = get_prog("add2")
    in_maps = []
    for core in range(NCORES):
        b, half = core // 2, core % 2
        tsl = slice(half * 2048, (half + 1) * 2048)
        in_maps.append({"x": np.ascontiguousarray(xT[b][:, tsl]),
                        "x2": np.ascontiguousarray(x2T[b][:, tsl])})
    res = prog.run(in_maps)
    return [np.concatenate([res[2 * b]["xo"], res[2 * b + 1]["xo"]], axis=1) for b in range(4)]


def run_l1ffn_block(xT, inp):
    inner = run_ln_out(xT)
    for li in range(6):
        ctxT = run_l1f_attn(inner, inp, li)
        inner = run_l1f_ffn(inner, ctxT, inp, li)
    return run_add2(xT, inner)


def kernel(**inputs):
    _ = inputs.get("L0_ffn_input_ids")
    xT = [np.ascontiguousarray(inputs["L0_attn_a_x"][b].T) for b in range(4)]
    ctxT = run_attn_a(xT, inputs)
    xT = run_proj_res(xT, ctxT, inputs["L0_attn_a_W_proj"], inputs["L0_attn_a_b_proj"])
    xT = run_moe(xT, inputs)
    ctxT = run_attn_b(xT, inputs)
    xT = run_proj_res(xT, ctxT, inputs["L0_attn_b_Wo"], np.zeros(1024, np.float32))
    xT = run_moe(xT, inputs)
    xT = run_mamba1(xT, inputs)
    xT = run_l1ffn_block(xT, inputs)
    xnT = run_ln_out(xT)
    ctxT = run_mamba2(xnT, inputs)
    xT = run_proj_res2(xT, xnT, ctxT,
                       np.ascontiguousarray(inputs["L1_attn_b_op_w"].T), inputs["L1_attn_b_op_b"])
    xT = run_l1ffn_block(xT, inputs)
    out = np.stack([xT[b].T for b in range(4)], axis=0).astype(np.float32)
    return out


# version 03
# speedup vs baseline: 1.3247x; 1.3247x over previous
import contextlib
import numpy as np
import ml_dtypes
import concourse.bass as bass
import concourse.mybir as mybir
from concourse.ap import AP
from concourse.bass_utils import run_bass_kernel_spmd

F32 = mybir.dt.float32
BF16 = mybir.dt.bfloat16
AF = mybir.ActivationFunctionType
ALU = mybir.AluOpType
AX = mybir.AxisListType
BF = ml_dtypes.bfloat16

NCORES = 8

ENGINES = ("sy", "gp", "ve", "sc", "te")
_ENG_ATTR = {"sy": "sync", "gp": "gpsimd", "ve": "vector", "sc": "scalar", "te": "tensor"}


class Sched:
    def __init__(self, nc, es):
        self.nc = nc
        self.sem = {e: es.enter_context(nc.semaphore(f"sem_{e}")) for e in ENGINES}
        self.steps = {e: [] for e in ENGINES}
        self.count = {e: 0 for e in ENGINES}

    def step(self, eng, emit, deps=(), dma=False, tag=""):
        inc = 16 if dma else 1
        dd = []
        for d in deps:
            if d is None or d[0] == eng:
                continue
            dd.append(d)
        self.steps[eng].append((emit, tuple(dd), inc, tag))
        self.count[eng] += inc
        return (eng, self.count[eng])

    def tok(self, eng):
        return (eng, self.count[eng])

    def _simulate(self):
        idx = {e: 0 for e in ENGINES}
        done = {e: 0 for e in ENGINES}
        total = sum(len(v) for v in self.steps.values())
        ndone = 0
        while ndone < total:
            progressed = False
            for e in ENGINES:
                while idx[e] < len(self.steps[e]):
                    emit, deps, inc, tag = self.steps[e][idx[e]]
                    if all(done[de] >= dv for de, dv in deps):
                        done[e] += inc
                        idx[e] += 1
                        ndone += 1
                        progressed = True
                    else:
                        break
            if not progressed:
                heads = {e: (idx[e], len(self.steps[e]),
                             self.steps[e][idx[e]][3] if idx[e] < len(self.steps[e]) else None)
                         for e in ENGINES}
                raise RuntimeError(f"sched deadlock: done={done} heads={heads}")

    def finalize(self, block):
        self._simulate()
        final_counts = dict(self.count)
        sems = self.sem

        def mk_body(eng):
            steps = self.steps[eng]
            is_sy = eng == "sy"

            auto_chain = eng in ("ve", "sc", "gp")

            def body(engobj):
                hw = {e: 0 for e in ENGINES}
                cum = 0
                any_dma_before = False
                first = True
                for emit, deps, inc, tag in steps:
                    if not first and (auto_chain or (inc == 16 and any_dma_before)):
                        engobj.wait_ge(sems[eng], cum)
                    for de, dv in deps:
                        if dv > hw[de]:
                            engobj.wait_ge(sems[de], dv)
                            hw[de] = dv
                    inst = emit(engobj)
                    if isinstance(inst, (list, tuple)):
                        inst = inst[-1]
                    inst.then_inc(sems[eng], inc)
                    cum += inc
                    first = False
                    if inc == 16:
                        any_dma_before = True
                if is_sy:
                    for e in ENGINES:
                        if final_counts[e] > 0:
                            engobj.wait_ge(sems[e], final_counts[e])
            return body

        for eng in ENGINES:
            if self.steps[eng] or eng == "sy":
                getattr(block, _ENG_ATTR[eng])(mk_body(eng))


class Prog:
    def __init__(self, name):
        self.name = name
        self.nc = bass.Bass("TRN2", target_bir_lowering=False, debug=False)
        self.es = contextlib.ExitStack()
        self.s = Sched(self.nc, self.es)
        self.inputs = []
        self.outputs = []

    def din(self, name, shape, dtype=F32):
        self.inputs.append(name)
        return self.nc.dram_tensor(name, list(shape), dtype, kind="ExternalInput")

    def dout(self, name, shape, dtype=F32):
        self.outputs.append(name)
        return self.nc.dram_tensor(name, list(shape), dtype, kind="ExternalOutput")

    def dtmp(self, name, shape, dtype=F32):
        return self.nc.dram_tensor(name, list(shape), dtype, kind="Internal")

    def sb(self, name, shape, dtype=F32):
        return self.es.enter_context(self.nc.sbuf_tensor(name, list(shape), dtype))

    def ps(self, name, shape, dtype=F32):
        return self.es.enter_context(self.nc.psum_tensor(name, list(shape), dtype))

    def finish(self):
        block = self.es.enter_context(self.nc.Block())
        self.s.finalize(block)
        self.es.close()
        return self

    def run(self, in_maps):
        res = run_bass_kernel_spmd(self.nc, in_maps, core_ids=list(range(NCORES)))
        return res.results



def emit_consts(p):
    ones_bf = p.sb("ones_bf", [128, 2], BF16)
    ones_f = p.sb("ones_f", [128, 128], F32)
    t1 = p.s.step("ve", lambda v: v.memset(ones_bf[:], 1.0), tag="c1")
    t2 = p.s.step("ve", lambda v: v.memset(ones_f[:], 1.0), tag="c2")
    return ones_bf, ones_f, (t1, t2)


def emit_ln(p, x_dram, T, xn, xn_cols, ones_bf, ones_f, const_toks,
            ps_m, ps_q, ps_bc, pfx="ln"):
    s = p.s
    NT = T // 512
    xg = [p.sb(f"{pfx}_xg0", [128, 8 * 512], F32)]
    cast_bf = p.sb(f"{pfx}_cast", [128, 512], BF16)
    sq_bf = p.sb(f"{pfx}_sq", [128, 512], BF16)
    row_v = p.sb(f"{pfx}_rv", [1, 512], F32)
    row_r = p.sb(f"{pfx}_rr", [1, 512], F32)
    row_ms = p.sb(f"{pfx}_rms", [1, 512], F32)
    xd = x_dram.ap()

    done_toks = []
    grp_free = [None, None]
    row_free = None
    bc_free = None
    mm_last_m = mm_last_q = None
    for t in range(NT):
        g = xg[0]
        ld = []
        for c in range(8):
            dep = [grp_free[0]] if (c == 0 and grp_free[0] is not None) else []
            ld.append(s.step("sy", lambda y, c=c, t=t, g=g: y.dma_start(
                out=g[:, c * 512:(c + 1) * 512], in_=xd[c * 128:(c + 1) * 128, t * 512:(t + 1) * 512]),
                deps=dep, dma=True, tag=f"{pfx}_ld{t}_{c}"))
        for c in range(8):
            tcast = s.step("ve", lambda v, c=c, g=g: v.tensor_copy(out=cast_bf[:], in_=g[:, c * 512:(c + 1) * 512]),
                           deps=[ld[c], mm_last_m] + list(const_toks), tag=f"{pfx}_cast{t}_{c}")
            tsq = s.step("sc", lambda sc, c=c, g=g: sc.activation(out=sq_bf[:], in_=g[:, c * 512:(c + 1) * 512],
                                                                  func=AF.Square),
                         deps=[ld[c], mm_last_q], tag=f"{pfx}_sq{t}_{c}")
            mm_last_m = s.step("te", lambda te, c=c: te.matmul(
                ps_m, ones_bf[:, 0:1], cast_bf[:], start=(c == 0), stop=(c == 7)),
                deps=[tcast, row_free if c == 0 else None], tag=f"{pfx}_mm{t}_{c}")
            mm_last_q = s.step("te", lambda te, c=c: te.matmul(
                ps_q, ones_bf[:, 0:1], sq_bf[:], start=(c == 0), stop=(c == 7)),
                deps=[tsq, row_free if c == 0 else None], tag=f"{pfx}_mq{t}_{c}")
        t1 = s.step("ve", lambda v: v.tensor_scalar_mul(row_ms[:], ps_m, 1.0 / 1024.0),
                    deps=[mm_last_m, row_free], tag=f"{pfx}_st1_{t}")
        t2 = s.step("ve", lambda v: v.tensor_mul(row_v[:], row_ms[:], row_ms[:]),
                    deps=[t1], tag=f"{pfx}_st2_{t}")
        t3 = s.step("ve", lambda v: v.scalar_tensor_tensor(row_v[:], ps_q, 1.0 / 1024.0, row_v[:],
                                                           ALU.mult, ALU.subtract),
                    deps=[t2, mm_last_q], tag=f"{pfx}_st3_{t}")
        t4 = s.step("ve", lambda v: v.tensor_scalar_add(row_v[:], row_v[:], 1e-5),
                    deps=[t3], tag=f"{pfx}_st4_{t}")
        t5 = s.step("sc", lambda sc: sc.activation(out=row_r[:], in_=row_v[:], func=AF.Sqrt),
                    deps=[t4, bc_free], tag=f"{pfx}_st5_{t}")
        t6 = s.step("ve", lambda v: v.reciprocal(row_r[:], row_r[:]),
                    deps=[t5], tag=f"{pfx}_st6_{t}")
        t7 = s.step("ve", lambda v: v.tensor_mul(row_ms[:], row_ms[:], row_r[:]),
                    deps=[t6], tag=f"{pfx}_st7_{t}")
        tb = s.step("te", lambda te: [
            te.matmul(ps_bc[:, 0:512], ones_f[0:1, 0:128], row_r[:], start=True, stop=True),
            te.matmul(ps_bc[:, 512:1024], ones_f[0:1, 0:128], row_ms[:], start=True, stop=True)],
            deps=[t6, t7, bc_free], tag=f"{pfx}_bc_{t}")
        row_free = tb
        toks = []
        for c in range(8):
            ta = s.step("ve", lambda v, c=c, g=g: v.tensor_tensor(
                out=g[:, c * 512:(c + 1) * 512], in0=g[:, c * 512:(c + 1) * 512],
                in1=ps_bc[:, 0:512], op=ALU.mult),
                deps=[tb], tag=f"{pfx}_xa{t}_{c}")
            tbb = s.step("ve", lambda v, c=c, t=t, g=g: v.tensor_tensor(
                out=xn[:, c * xn_cols + t * 512: c * xn_cols + (t + 1) * 512],
                in0=g[:, c * 512:(c + 1) * 512], in1=ps_bc[:, 512:1024], op=ALU.subtract),
                deps=[ta], tag=f"{pfx}_xb{t}_{c}")
            toks.append(tbb)
        bc_free = toks[-1]
        grp_free[0] = toks[-1]
        done_toks.append(toks[-1])
    return done_toks




def _pieces(a, end, bank=512):
    out = []
    off = a
    while off < end:
        n = min(bank - (off % bank), end - off)
        out.append((off, n))
        off += n
    return out

def mm_acc(s, psum_ap, pairs, deps, tag):
    def emit(te):
        insts = []
        n = len(pairs)
        for i, (l, r) in enumerate(pairs):
            insts.append(te.matmul(psum_ap, l, r, start=(i == 0), stop=(i == n - 1)))
        return insts
    return s.step("te", emit, deps=deps, tag=tag)



def build_attn_a(NH=8, stage='all'):
    T = 4096
    p = Prog("attn_a")
    s = p.s
    x_d = p.din("x", [1024, T])
    wqk_d = p.din("wqk", [1024, 1024])
    wv_d = p.din("wv", [1024, 512])
    slopes_d = p.din("slopes_bc", [128, 8])
    nslopes_d = p.din("nslopes", [1, 8])
    posrow_d = p.din("posrow", [1, T])
    poscol_d = p.din("poscol", [128, 32])
    dmask_d = p.din("dmask", [128, 128], BF16)
    ctx_d = p.dout("ctx", [512, T])

    ones_bf, ones_f, ctoks = emit_consts(p)

    P0 = p.ps("P0", [128, 1024], F32)
    P1 = p.ps("P1", [128, 1024], F32)
    P2 = p.ps("P2", [128, 1024], F32)
    P3 = p.ps("P3", [128, 1024], F32)

    xn = p.sb("xn", [128, 8 * T], BF16)
    ln_toks = emit_ln(p, x_d, T, xn, T, ones_bf, ones_f, ctoks,
                      P2[0:1, 0:512], P2[0:1, 512:1024], P3, pfx="ln")
    xn_done = ln_toks[-1]

    wqk = p.sb("wqk_sb", [128, 8 * 1024], BF16)
    wv = p.sb("wv_sb", [128, 8 * 512], BF16)
    wtok = []
    for c in range(8):
        wtok.append(s.step("gp", lambda g, c=c: g.dma_start(out=wqk[:, c * 1024:(c + 1) * 1024],
                                                            in_=wqk_d.ap()[c * 128:(c + 1) * 128, :]),
                           dma=True, tag=f"wqk{c}"))
        wtok.append(s.step("gp", lambda g, c=c: g.dma_start(out=wv[:, c * 512:(c + 1) * 512],
                                                            in_=wv_d.ap()[c * 128:(c + 1) * 128, :]),
                           dma=True, tag=f"wv{c}"))
    slopes = p.sb("slopes_sb2", [128, 8], F32)
    nslopes = p.sb("nslopes_sb2", [1, 8], F32)
    posrow = p.sb("posrow_sb2", [1, T], F32)
    poscol = p.sb("poscol_sb2", [128, 32], F32)
    dmask = p.sb("dmask_sb2", [128, 128], BF16)
    for nm, sb_, dd in (("sl", slopes, slopes_d), ("nsl", nslopes, nslopes_d), ("pr", posrow, posrow_d),
                        ("pc", poscol, poscol_d), ("dm", dmask, dmask_d)):
        wtok.append(s.step("sy", lambda y, sb_=sb_, dd=dd: y.dma_start(out=sb_[:], in_=dd.ap()),
                           dma=True, tag=f"c_{nm}"))

    psS = [P0, P1]
    psC = P2
    psX = P3
    if stage == 'w':
        dbg = p.sb("dbg_f32", [128, 512], F32)
        tdbg = s.step("ve", lambda v: v.tensor_copy(out=dbg[:], in_=wqk[:, 0:512]),
                      deps=[xn_done] + wtok, tag="dbg_cp")
        s.step("sy", lambda y: y.dma_start(out=ctx_d.ap()[0:128, 0:512], in_=dbg[:]),
               deps=[tdbg], dma=True, tag="dbg_st")
        return p.finish()

    qaug = p.sb("qaug", [128, T], BF16)
    kaug = p.sb("kaug", [128, T], BF16)
    qsq = p.sb("qsq", [64, 512], BF16)
    ksq = p.sb("ksq", [64, 512], BF16)
    q2row = p.sb("q2row", [1, T], F32)
    k2row = p.sb("k2row", [1, T], F32)
    maxk = p.sb("maxk", [1, 2], F32)
    vt = p.sb("vt", [128, 32 * 130], BF16)
    biascol = p.sb("biascol", [128, 32], F32)
    pt_sb = p.sb("pt_sb", [128, 1024], BF16)
    ctx_sb = p.sb("ctx_sb", [128, T], BF16)
    rrow = p.sb("rrow", [1, 1024], F32)
    st_f32 = p.sb("st_f32", [128, 512], F32)
    ctmp = p.sb("ctmp", [64, 1024], F32)

    def memset_vt_ones(v):
        i = None
        for j in range(32):
            i = v.memset(vt[:, j * 130 + 64: j * 130 + 65], 1.0)
            i = v.memset(vt[:, j * 130 + 129: j * 130 + 130], 1.0)
        return i
    t_vones = s.step("ve", memset_vt_ones, tag="vt_ones")

    ctx_store_toks = []
    v_done_pair = {}
    prev_head_f4 = []
    for h in range(NH):
        pr = h // 2
        sub = h % 2
        prev_evacs = None
        qk_evac_all = []
        for t in range(8):
            dep = ([xn_done] + wtok) if (t == 0 and h == 0) else []
            if t == 0 and h > 0:
                dep = [s.tok("ve"), s.tok("sc")]
            tm = mm_acc(s, psX[:, 0:512],
                        [(wqk[:, kc * 1024 + h * 128: kc * 1024 + (h + 1) * 128],
                          xn[:, kc * T + t * 512: kc * T + (t + 1) * 512]) for kc in range(8)],
                        deps=dep + (prev_evacs or []), tag=f"qk{h}_{t}")
            e1 = s.step("ve", lambda v, t=t: v.tensor_scalar_mul(
                qaug[0:64, t * 512:(t + 1) * 512], psX[0:64, 0:512], 0.125), deps=[tm], tag=f"qe{h}_{t}")
            e2 = s.step("ve", lambda v, t=t: v.tensor_copy(
                out=kaug[0:64, t * 512:(t + 1) * 512], in_=psX[64:128, 0:512]), deps=[tm], tag=f"ke{h}_{t}")
            if stage in ('qk_nosq',):
                prev_evacs = [e1, e2]
                qk_evac_all += prev_evacs
                continue
            if stage == 'qk_sqonly':
                sq1 = s.step("sc", lambda sc: sc.activation(out=qsq[:], in_=psX[0:64, 0:512],
                                                            func=AF.Square, scale=0.125),
                             deps=[tm, e1, e2], tag=f"qsqA{h}_{t}")
                sq2 = s.step("sc", lambda sc: sc.activation(out=ksq[:], in_=psX[64:128, 0:512],
                                                            func=AF.Square),
                             deps=[tm], tag=f"ksqA{h}_{t}")
                prev_evacs = [e1, e2, sq1, sq2]
                qk_evac_all += prev_evacs
                continue
            if stage == 'qk_m12':
                sq1 = s.step("sc", lambda sc: sc.activation(out=qsq[:], in_=psX[0:64, 0:512],
                                                            func=AF.Square, scale=0.125),
                             deps=[tm], tag=f"qsqB{h}_{t}")
                sq2 = s.step("sc", lambda sc: sc.activation(out=ksq[:], in_=psX[64:128, 0:512],
                                                            func=AF.Square),
                             deps=[tm], tag=f"ksqB{h}_{t}")
                m1 = mm_acc(s, psC[0:1, 0:512], [(ones_bf[0:64, 0:1], qsq[:])],
                            deps=[sq1, s.tok("ve")], tag=f"q2mB{h}_{t}")
                m2 = mm_acc(s, psC[0:1, 512:1024], [(ones_bf[0:64, 0:1], ksq[:])],
                            deps=[sq2], tag=f"k2mB{h}_{t}")
                prev_evacs = [e1, e2, sq1, sq2]
                qk_evac_all += prev_evacs
                continue
            sq1 = s.step("sc", lambda sc: sc.activation(out=qsq[:], in_=psX[0:64, 0:512],
                                                        func=AF.Square, scale=0.125),
                         deps=[tm, e1, e2], tag=f"qsq{h}_{t}")
            sq2 = s.step("sc", lambda sc: sc.activation(out=ksq[:], in_=psX[64:128, 0:512],
                                                        func=AF.Square),
                         deps=[tm], tag=f"ksq{h}_{t}")
            m1 = mm_acc(s, psC[0:1, 0:512], [(ones_bf[0:64, 0:1], qsq[:])],
                        deps=[sq1, s.tok("ve")], tag=f"q2m{h}_{t}")
            m2 = mm_acc(s, psC[0:1, 512:1024], [(ones_bf[0:64, 0:1], ksq[:])],
                        deps=[sq2], tag=f"k2m{h}_{t}")
            e3 = s.step("ve", lambda v, t=t: v.tensor_copy(out=q2row[0:1, t * 512:(t + 1) * 512],
                                                           in_=psC[0:1, 0:512]), deps=[m1], tag=f"q2e{h}_{t}")
            e4 = s.step("ve", lambda v, t=t: v.tensor_copy(out=k2row[0:1, t * 512:(t + 1) * 512],
                                                           in_=psC[0:1, 512:1024]), deps=[m2], tag=f"k2e{h}_{t}")
            prev_evacs = [e1, e2, sq1, sq2, e3, e4]
            qk_evac_all += prev_evacs
        if stage in ('qk_nosq', 'qk_non', 'qk_sqonly', 'qk_m12'):
            break
        n1 = s.step("sc", lambda sc: sc.activation(out=k2row[:], in_=k2row[:], func=AF.Sqrt),
                    deps=qk_evac_all[-6:], tag=f"n1{h}")
        n2 = s.step("ve", lambda v: v.reduce_max(maxk[:, 0:1], k2row[:], axis=AX.X), deps=[n1], tag=f"n2{h}")
        n3 = s.step("ve", lambda v: v.tensor_scalar_mul(maxk[:, 1:2], maxk[:, 0:1], -1.0), deps=[n2], tag=f"n3{h}")
        n4 = s.step("sc", lambda sc: sc.activation(out=q2row[:], in_=q2row[:], func=AF.Sqrt),
                    deps=[n1], tag=f"n4{h}")
        n5 = s.step("ve", lambda v: v.tensor_scalar_mul(q2row[:], q2row[:], maxk[:, 1:2]), deps=[n3, n4], tag=f"n5{h}")
        n6 = s.step("ve", lambda v, h=h: v.scalar_tensor_tensor(
            qaug[64:65, :], posrow[:], nslopes[0:1, h:h + 1], q2row[:], ALU.mult, ALU.add),
            deps=[n5], tag=f"n6{h}")
        n7 = s.step("ve", lambda v: v.memset(kaug[64:65, :], 1.0), deps=[], tag=f"n7{h}")
        n8 = s.step("ve", lambda v, h=h: v.tensor_scalar_mul(biascol[:], poscol[:], slopes[:, h:h + 1]),
                    deps=[], tag=f"n8{h}")
        qk_done = [n6, n7, n8, qk_evac_all[-6], qk_evac_all[-5]]
        if stage == 'qk2':
            tdg = s.step("ve", lambda v: v.tensor_copy(out=st_f32[:], in_=qaug[:, 0:512]),
                         deps=qk_done + [qk_evac_all[-6]], tag="dbgq")
            s.step("sy", lambda y: y.dma_start(out=ctx_d.ap()[0:128, 0:512], in_=st_f32[:]),
                   deps=[tdg], dma=True, tag="dbgq_st")
            break
        if stage == 'qk':
            break

        if sub == 0:
            v_toks = []
            for tt in range(32):
                dep = ([xn_done] + wtok) if (tt == 0 and pr == 0) else []
                tv = mm_acc(s, psX[:, 512:640],
                            [(xn[:, kc * T + tt * 128: kc * T + (tt + 1) * 128],
                              wv[:, kc * 512 + pr * 128: kc * 512 + (pr + 1) * 128]) for kc in range(8)],
                            deps=dep + [v_toks[-1] if v_toks else t_vones], tag=f"v{pr}_{tt}")
                ev = s.step("ve", lambda v, tt=tt: [
                    v.tensor_copy(out=vt[:, tt * 130: tt * 130 + 64], in_=psX[:, 512:576]),
                    v.tensor_copy(out=vt[:, tt * 130 + 65: tt * 130 + 129], in_=psX[:, 576:640])],
                    deps=[tv], tag=f"ve{pr}_{tt}")
                v_toks.append(ev)
            v_done_pair[pr] = v_toks[-1]
        v_done = v_done_pair[pr]
        if stage == 'v2':
            tdg = s.step("ve", lambda v: v.tensor_copy(out=st_f32[:], in_=vt[:, 0:512]),
                         deps=[v_done], tag="dbgv")
            s.step("sy", lambda y: y.dma_start(out=ctx_d.ap()[0:128, 0:512], in_=st_f32[:]),
                   deps=[tdg], dma=True, tag="dbgv_st")
            break
        if stage == 'v':
            break

        head_f4 = []
        for qc in range(4):
            Q0 = qc * 1024
            nkt = (Q0 + 1024) // 128
            exp_toks = []
            pv_toks = []
            for kt in range(nkt):
                K0 = kt * 128
                a = max(0, K0 - Q0)
                buf = psS[kt % 2]
                def s_emit(te, kt=kt, qc=qc, a=a, buf=buf):
                    insts = []
                    for off, n in _pieces(a, 1024):
                        insts.append(te.matmul(buf[:, off:off + n],
                                               kaug[0:65, kt * 128:(kt + 1) * 128],
                                               qaug[0:65, qc * 1024 + off: qc * 1024 + off + n],
                                               start=True, stop=True))
                    return insts
                tS = s.step("te", s_emit,
                            deps=qk_done + [exp_toks[-2] if len(exp_toks) >= 2 else None],
                            tag=f"S{h}_{qc}_{kt}")
                tE = s.step("sc", lambda sc, kt=kt, a=a, buf=buf: sc.activation(
                    out=pt_sb[:, a:1024], in_=buf[:, a:1024], func=AF.Exp,
                    bias=biascol[:, kt:kt + 1]),
                    deps=[tS, pv_toks[-1] if pv_toks else None], tag=f"E{h}_{qc}_{kt}")
                last_tok = tE
                if K0 >= Q0:
                    last_tok = s.step("ve", lambda v, a=a: v.tensor_tensor(
                        out=pt_sb[:, a:a + 128], in0=pt_sb[:, a:a + 128], in1=dmask[:], op=ALU.mult),
                        deps=[tE], tag=f"M{h}_{qc}_{kt}")
                exp_toks.append(last_tok)
                first = (kt == 0)
                last_kt_bank = {0: Q0 // 128 + 3, 1: nkt - 1}
                def pv_emit(te, kt=kt, a=a, sub=sub, first=first, last_kt_bank=last_kt_bank):
                    insts = []
                    for off, n in _pieces(a, 1024):
                        stop_p = (kt == last_kt_bank[off // 512])
                        insts.append(te.matmul(psC[0:65, off:off + n],
                                               vt[:, kt * 130 + sub * 65: kt * 130 + sub * 65 + 65],
                                               pt_sb[:, off:off + n],
                                               start=first, stop=stop_p))
                    return insts
                extra_dep = []
                if first:
                    extra_dep = head_f4[-1:] if head_f4 else (prev_head_f4[-1:] if prev_head_f4 else [])
                    if h == 0 and qc == 0:
                        extra_dep = [qk_evac_all[-2], qk_evac_all[-1]]
                tPV = s.step("te", pv_emit, deps=[last_tok, v_done] + extra_dep,
                             tag=f"PV{h}_{qc}_{kt}")
                pv_toks.append(tPV)
            if stage == 'pv':
                continue
            f1 = s.step("ve", lambda v: v.reciprocal(rrow[:], psC[64:65, :]), deps=[pv_toks[-1]], tag=f"f1{h}_{qc}")
            f2 = s.step("te", lambda te: [
                te.matmul(psX[0:64, 0:512], ones_f[0:1, 0:64], rrow[0:1, 0:512], start=True, stop=True),
                te.matmul(psX[0:64, 512:1024], ones_f[0:1, 0:64], rrow[0:1, 512:1024], start=True, stop=True)],
                deps=[f1, s.tok("ve"), s.tok("sc")], tag=f"f2{h}_{qc}")
            f3 = s.step("ve", lambda v: v.tensor_copy(out=ctmp[:], in_=psC[0:64, :]), deps=[f1], tag=f"f3{h}_{qc}")
            f4 = s.step("ve", lambda v, qc=qc, sub=sub: v.tensor_tensor(
                out=ctx_sb[sub * 64:(sub + 1) * 64, qc * 1024:(qc + 1) * 1024],
                in0=ctmp[:], in1=psX[0:64, 0:1024], op=ALU.mult),
                deps=[f2, f3] + (ctx_store_toks[-1:] if (qc == 0 and sub == 0) else []),
                tag=f"f4{h}_{qc}")
            head_f4.append(f4)
            if stage == 'attn1':
                break
        if stage == 'attn1':
            tdg = s.step("ve", lambda v: v.tensor_copy(out=st_f32[:], in_=ctx_sb[:, 0:512]),
                         deps=[head_f4[-1]], tag="dbga")
            s.step("sy", lambda y: y.dma_start(out=ctx_d.ap()[0:128, 0:512], in_=st_f32[:]),
                   deps=[tdg], dma=True, tag="dbga_st")
            break
        prev_head_f4 = head_f4
        if stage == 'pv':
            break
        if sub == 1:
            dep0 = s.tok("ve")
            for pc in range(8):
                tcp = s.step("ve", lambda v, pc=pc: v.tensor_copy(
                    out=st_f32[:], in_=ctx_sb[:, pc * 512:(pc + 1) * 512]),
                    deps=[dep0, ctx_store_toks[-1] if ctx_store_toks else None], tag=f"cst{pr}_{pc}")
                tst = s.step("sy", lambda y, pr=pr, pc=pc: y.dma_start(
                    out=ctx_d.ap()[pr * 128:(pr + 1) * 128, pc * 512:(pc + 1) * 512], in_=st_f32[:]),
                    deps=[tcp], dma=True, tag=f"ctx_st{pr}_{pc}")
                ctx_store_toks.append(tst)

    if stage in ('qk_nosq', 'qk_non', 'qk', 'v', 'pv', 'qk_sqonly', 'qk_m12'):
        tdg = s.step("ve", lambda v: v.tensor_copy(out=st_f32[:], in_=qaug[:, 0:512]),
                     deps=[s.tok("te"), s.tok("sc")], tag="dbgz")
        s.step("sy", lambda y: y.dma_start(out=ctx_d.ap()[0:128, 0:512], in_=st_f32[:]),
               deps=[tdg], dma=True, tag="dbgz_st")
    return p.finish()



def build_proj_res():
    T = 2048
    p = Prog("proj_res")
    s = p.s
    x_d = p.din("x", [1024, T])
    ctx_d = p.din("ctx", [1024, T])
    w_d = p.din("w", [1024, 1024])
    b_d = p.din("biascol", [128, 8])
    xo_d = p.dout("xo", [1024, T])

    wsb = p.sb("wsb", [128, 8 * 1024], BF16)
    cs = p.sb("cs", [128, 8 * T], BF16)
    bcol = p.sb("bcol", [128, 8], F32)
    xbuf = [p.sb("xb0", [128, 512], F32), p.sb("xb1", [128, 512], F32)]
    obuf = [p.sb("ob0", [128, 512], F32), p.sb("ob1", [128, 512], F32)]
    psum = [p.ps("pA", [128, 512], F32), p.ps("pB", [128, 512], F32)]

    wt = []
    for c in range(8):
        wt.append(s.step("gp", lambda g, c=c: g.dma_start(out=wsb[:, c * 1024:(c + 1) * 1024],
                                                          in_=w_d.ap()[c * 128:(c + 1) * 128, :]),
                         dma=True, tag=f"w{c}"))
        wt.append(s.step("gp", lambda g, c=c: g.dma_start(out=cs[:, c * T:(c + 1) * T],
                                                          in_=ctx_d.ap()[c * 128:(c + 1) * 128, :]),
                         dma=True, tag=f"c{c}"))
    wt.append(s.step("sy", lambda y: y.dma_start(out=bcol[:], in_=b_d.ap()), dma=True, tag="b"))

    i = 0
    prev = {}
    for mc in range(8):
        for tc in range(T // 512):
            k = i % 2
            ps_, xb, ob = psum[k], xbuf[k], obuf[k]
            tlx = s.step("sy", lambda y, mc=mc, tc=tc, xb=xb: y.dma_start(
                out=xb[:], in_=x_d.ap()[mc * 128:(mc + 1) * 128, tc * 512:(tc + 1) * 512]),
                deps=[prev.get(("x", k))], dma=True, tag=f"lx{mc}_{tc}")
            tm = mm_acc(s, ps_[:],
                        [(wsb[:, kc * 1024 + mc * 128: kc * 1024 + (mc + 1) * 128],
                          cs[:, kc * T + tc * 512: kc * T + (tc + 1) * 512]) for kc in range(8)],
                        deps=(wt if i == 0 else []) + [prev.get(("p", k))], tag=f"mm{mc}_{tc}")
            tev = s.step("ve", lambda v, mc=mc, ps_=ps_, xb=xb, ob=ob: v.scalar_tensor_tensor(
                ob[:], ps_[:], bcol[:, mc:mc + 1], xb[:], ALU.add, ALU.add),
                deps=[tm, tlx, prev.get(("o", k))] + (wt if i == 0 else []), tag=f"ev{mc}_{tc}")
            tst = s.step("sy", lambda y, mc=mc, tc=tc, ob=ob: y.dma_start(
                out=xo_d.ap()[mc * 128:(mc + 1) * 128, tc * 512:(tc + 1) * 512], in_=ob[:]),
                deps=[tev], dma=True, tag=f"st{mc}_{tc}")
            prev[("x", k)] = tev
            prev[("p", k)] = tev
            prev[("o", k)] = tst
            i += 1
    return p.finish()



_PROGS = {}
_BUILDERS = {}


def register(name):
    def deco(f):
        _BUILDERS[name] = f
        return f
    return deco


_BUILDERS["attn_a"] = build_attn_a
_BUILDERS["proj_res"] = build_proj_res


def get_prog(name):
    if name not in _PROGS:
        _PROGS[name] = _BUILDERS[name]()
    return _PROGS[name]


def _attn_a_consts():
    T, H = 4096, 16
    slopes = 2.0 ** (-8.0 * np.arange(1, H + 1, dtype=np.float32) / H)
    posrow = np.arange(T, dtype=np.float32).reshape(1, T)
    poscol = (np.arange(32, dtype=np.float32)[None, :] * 128
              + np.arange(128, dtype=np.float32)[:, None]).astype(np.float32)
    dmask = (np.arange(128)[:, None] <= np.arange(128)[None, :]).astype(BF)
    return slopes, posrow, poscol, dmask


def run_attn_a(xT, inp):
    prog = get_prog("attn_a")
    slopes, posrow, poscol, dmask = _attn_a_consts()
    Wq, Wk, Wv = inp["L0_attn_a_W_query"], inp["L0_attn_a_W_key"], inp["L0_attn_a_W_value"]
    in_maps = []
    for core in range(NCORES):
        b, hg = core // 2, core % 2
        hsl = slice(hg * 512, (hg + 1) * 512)
        wqk = np.empty((1024, 1024), np.float32)
        for hh in range(8):
            habs = hg * 8 + hh
            wqk[:, hh * 128: hh * 128 + 64] = Wq[:, habs * 64:(habs + 1) * 64]
            wqk[:, hh * 128 + 64: (hh + 1) * 128] = Wk[:, habs * 64:(habs + 1) * 64]
        sl = slopes[hg * 8:(hg + 1) * 8]
        in_maps.append({
            "x": np.ascontiguousarray(xT[b]),
            "wqk": wqk,
            "wv": np.ascontiguousarray(Wv[:, hsl]),
            "slopes_bc": np.broadcast_to(sl / 8.0, (128, 8)).astype(np.float32).copy(),
            "nslopes": (-sl / 8.0).reshape(1, 8).astype(np.float32),
            "posrow": posrow, "poscol": poscol, "dmask": dmask,
        })
    res = prog.run(in_maps)
    return [np.concatenate([res[2 * b]["ctx"], res[2 * b + 1]["ctx"]], axis=0) for b in range(4)]


def run_proj_res(xT, ctxT, W, bvec):
    prog = get_prog("proj_res")
    bcol = np.ascontiguousarray(bvec.reshape(8, 128).T).astype(np.float32)
    in_maps = []
    for core in range(NCORES):
        b, half = core // 2, core % 2
        tsl = slice(half * 2048, (half + 1) * 2048)
        in_maps.append({
            "x": np.ascontiguousarray(xT[b][:, tsl]),
            "ctx": np.ascontiguousarray(ctxT[b][:, tsl]),
            "w": np.ascontiguousarray(W),
            "biascol": bcol,
        })
    res = prog.run(in_maps)
    return [np.concatenate([res[2 * b]["xo"], res[2 * b + 1]["xo"]], axis=1) for b in range(4)]





def build_moe():
    T = 2048
    NE = 17
    p = Prog("moe")
    s = p.s
    x_d = p.din("x", [1024, T])
    gwT_d = p.din("gwT", [1024, 16])
    gb_d = p.din("gb", [1, 16])
    w13t_d = p.din("w13t", [NE * 1024, 1024])
    w2t_d = p.din("w2t", [NE * 512, 1024])
    idf_d = p.din("ident_f", [128, 128])
    xo_d = p.dout("xo", [1024, T])

    ones_bf, ones_f, ctoks = emit_consts(p)
    P0 = p.ps("P0", [128, 512], F32)
    P1 = p.ps("P1", [128, 512], F32)
    P2 = p.ps("P2", [128, 1024], F32)
    P3 = p.ps("P3", [128, 1024], F32)
    P4 = p.ps("P4", [128, 1024], F32)

    xn = p.sb("xn", [128, 8 * T], BF16)
    ln_toks = emit_ln(p, x_d, T, xn, T, ones_bf, ones_f, ctoks,
                      P2[0:1, 0:512], P2[0:1, 512:1024], P3, pfx="ln")
    xn_done = ln_toks[-1]

    gwT = p.sb("gwT_sb", [128, 8 * 16], BF16)
    gbbc = p.sb("gb_bc", [128, 16], F32)
    idf = p.sb("idf_sb", [128, 128], F32)
    scores = p.sb("scores", [128, 16], F32)
    biased = p.sb("biased", [128, 16], F32)
    maskt = p.sb("maskt", [128, 16], F32)
    tmpt = p.sb("tmpt", [128, 16], F32)
    comb = p.sb("comb", [128, 16], F32)
    mcol = p.sb("mcol", [128, 2], F32)
    combT = p.sb("combT", [16, T], F32)
    combT_d = p.dtmp("combT_d", [16, T], F32)

    tw1 = s.step("gp", lambda g: g.dma_start(
        out=gwT[:], in_=AP(gwT_d, 0, [[16, 128], [128 * 16, 8], [1, 16]])), dma=True, tag="gw")
    tw2 = s.step("sy", lambda y: y.dma_start(out=gbbc[:], in_=AP(gb_d, 0, [[0, 128], [1, 16]])),
                 dma=True, tag="gb")
    tw3 = s.step("sy", lambda y: y.dma_start(out=idf[:], in_=idf_d.ap()), dma=True, tag="idf")

    route_toks = []
    prev_tr = None
    for tt in range(T // 128):
        tg = mm_acc(s, P4[:, 0:16],
                    [(xn[:, kc * T + tt * 128:(kc * T + (tt + 1) * 128)], gwT[:, kc * 16:(kc + 1) * 16])
                     for kc in range(8)],
                    deps=[xn_done, tw1, prev_tr], tag=f"rg{tt}")
        t0 = s.step("sc", lambda sc: sc.activation(out=scores[:], in_=P4[:, 0:16], func=AF.Sigmoid),
                    deps=[tg], tag=f"rs{tt}")
        t1 = s.step("ve", lambda v: v.tensor_add(biased[:], scores[:], gbbc[:]), deps=[t0, tw2], tag=f"rb{tt}")
        t2 = s.step("ve", lambda v: v.memset(comb[:], 0.0), deps=[], tag=f"rc{tt}")
        tprev = [t1, t2]
        for j in range(4):
            m1 = s.step("ve", lambda v: v.reduce_max(mcol[:, 0:1], biased[:], axis=AX.X),
                        deps=tprev, tag=f"rm{tt}_{j}")
            m2 = s.step("ve", lambda v: v.tensor_scalar(maskt[:], biased[:], mcol[:, 0:1], None, ALU.is_ge),
                        deps=[m1], tag=f"rx{tt}_{j}")
            m3 = s.step("ve", lambda v: v.tensor_mul(tmpt[:], maskt[:], scores[:]), deps=[m2], tag=f"rt{tt}_{j}")
            m4 = s.step("ve", lambda v: v.tensor_add(comb[:], comb[:], tmpt[:]), deps=[m3], tag=f"ra{tt}_{j}")
            m5 = s.step("ve", lambda v: v.scalar_tensor_tensor(biased[:], maskt[:], -1e30, biased[:],
                                                               ALU.mult, ALU.add), deps=[m3], tag=f"rz{tt}_{j}")
            tprev = [m4, m5]
        r1 = s.step("ve", lambda v: v.tensor_reduce(mcol[:, 1:2], comb[:], AX.X, ALU.add), deps=tprev, tag=f"rr{tt}")
        r2 = s.step("ve", lambda v: v.reciprocal(mcol[:, 1:2], mcol[:, 1:2]), deps=[r1], tag=f"ri{tt}")
        r3 = s.step("ve", lambda v: v.tensor_scalar_mul(mcol[:, 1:2], mcol[:, 1:2], 2.5), deps=[r2], tag=f"rj{tt}")
        r4 = s.step("ve", lambda v: v.tensor_scalar_mul(comb[:], comb[:], mcol[:, 1:2]), deps=[r3], tag=f"rk{tt}")
        tr = s.step("te", lambda te: te.transpose(P0[0:16, 0:128], comb[:], idf[:]),
                    deps=[r4, tw3, prev_tr], tag=f"rT{tt}")
        cp = s.step("ve", lambda v, tt=tt: v.tensor_copy(out=combT[:, tt * 128:(tt + 1) * 128],
                                                         in_=P0[0:16, 0:128]), deps=[tr], tag=f"rC{tt}")
        prev_tr = cp
        route_toks.append(cp)
    tstc = s.step("sy", lambda y: y.dma_start(out=combT_d.ap(), in_=combT[:]),
                  deps=[route_toks[-1]], dma=True, tag="combT_st")

    w13sb = [p.sb("w13A", [128, 8 * 1024], BF16), p.sb("w13B", [128, 8 * 1024], BF16)]
    w2sb = [p.sb("w2A", [128, 4 * 1024], BF16)]
    cbc = [p.sb("cbcA", [128, T], F32)]
    hbuf = [p.sb("hA", [128, 4 * 512], BF16), p.sb("hB", [128, 4 * 512], BF16)]
    h1t = p.sb("h1t", [128, 512], F32)
    Y = p.sb("Y", [128, 8 * T], F32)
    xr = [p.sb("xrA", [128, 512], F32), p.sb("xrB", [128, 512], F32)]
    orr = [p.sb("orA", [128, 512], F32), p.sb("orB", [128, 512], F32)]

    wfree = {0: None, 1: None}
    cfree = {0: None}
    hk = 0
    hfree = {0: None, 1: None}
    ylast = {}
    pslast = {0: None, 1: None}
    prev_a2 = None
    for e in range(NE):
        k = e % 2
        dep_w = [wfree[k]] if wfree[k] is not None else []
        l1 = s.step("gp", lambda g, e=e, k=k: g.dma_start(
            out=w13sb[k][:], in_=AP(w13t_d, e * 1024 * 1024, [[1024, 128], [128 * 1024, 8], [1, 1024]])),
            deps=dep_w, dma=True, tag=f"lw13_{e}")
        l2 = s.step("gp", lambda g, e=e: g.dma_start(
            out=w2sb[0][:], in_=AP(w2t_d, e * 512 * 1024, [[1024, 128], [128 * 1024, 4], [1, 1024]])),
            deps=[wfree[0], wfree[1]], dma=True, tag=f"lw2_{e}")
        if e < 16:
            l3 = s.step("sy", lambda y, e=e: y.dma_start(
                out=cbc[0][:], in_=AP(combT_d, e * T, [[0, 128], [1, T]])),
                deps=[tstc, cfree[0]], dma=True, tag=f"lcb_{e}")
        else:
            l3 = None
        for tc in range(T // 512):
            hb = hbuf[hk % 2]
            myh = []
            for gc in range(4):
                tg = mm_acc(s, P0[:, 0:512],
                            [(w13sb[k][:, kc * 1024 + gc * 128: kc * 1024 + (gc + 1) * 128],
                              xn[:, kc * T + tc * 512:(kc * T + (tc + 1) * 512)]) for kc in range(8)],
                            deps=[xn_done, l1, s.tok("sc"), s.tok("ve")], tag=f"hg{e}_{tc}_{gc}")
                tu = mm_acc(s, P1[:, 0:512],
                            [(w13sb[k][:, kc * 1024 + 512 + gc * 128: kc * 1024 + 512 + (gc + 1) * 128],
                              xn[:, kc * T + tc * 512:(kc * T + (tc + 1) * 512)]) for kc in range(8)],
                            deps=[xn_done, l1, s.tok("ve")], tag=f"hu{e}_{tc}_{gc}")
                a1 = s.step("sc", lambda sc: sc.activation(out=h1t[:], in_=P0[:, 0:512], func=AF.Silu),
                            deps=[tg, hfree[hk % 2], prev_a2], tag=f"sl{e}_{tc}_{gc}")
                a2 = s.step("ve", lambda v, hb=hb, gc=gc: v.scalar_tensor_tensor(
                    hb[:, gc * 512:(gc + 1) * 512], P1[:, 0:512], 1.0, h1t[:], ALU.mult, ALU.mult),
                    deps=[tu, a1], tag=f"hm{e}_{tc}_{gc}")
                prev_a2 = a2
                if e < 16:
                    a3 = s.step("ve", lambda v, hb=hb, gc=gc, k=k, tc=tc: v.tensor_tensor(
                        out=hb[:, gc * 512:(gc + 1) * 512], in0=hb[:, gc * 512:(gc + 1) * 512],
                        in1=cbc[0][:, tc * 512:(tc + 1) * 512], op=ALU.mult),
                        deps=[a2, l3], tag=f"hc{e}_{tc}_{gc}")
                else:
                    a3 = a2
                myh.append(a3)
            for mc in range(8):
                py = [P2, P3][mc % 2]
                tmy = mm_acc(s, py[:, 0:512],
                             [(w2sb[0][:, kc * 1024 + mc * 128: kc * 1024 + (mc + 1) * 128],
                               hb[:, kc * 512:(kc + 1) * 512]) for kc in range(4)],
                             deps=myh + [l2, pslast[mc % 2]], tag=f"ym{e}_{tc}_{mc}")
                if e == 0:
                    ta = s.step("ve", lambda v, py=py, mc=mc, tc=tc: v.tensor_copy(
                        out=Y[:, mc * T + tc * 512: mc * T + (tc + 1) * 512], in_=py[:, 0:512]),
                        deps=[tmy], tag=f"ya{e}_{tc}_{mc}")
                else:
                    ta = s.step("ve", lambda v, py=py, mc=mc, tc=tc: v.tensor_tensor(
                        out=Y[:, mc * T + tc * 512: mc * T + (tc + 1) * 512],
                        in0=Y[:, mc * T + tc * 512: mc * T + (tc + 1) * 512],
                        in1=py[:, 0:512], op=ALU.add),
                        deps=[tmy], tag=f"ya{e}_{tc}_{mc}")
                ylast[(tc, mc)] = ta
                pslast[mc % 2] = ta
            hfree[hk % 2] = ylast[(tc, 7)]
            hk += 1
        wfree[k] = s.tok("te")
        cfree[0] = s.tok("ve")

    prevx = {0: None, 1: None}
    prevo = {0: None, 1: None}
    i = 0
    for mc in range(8):
        for tc in range(T // 512):
            kk = i % 2
            tlx = s.step("sy", lambda y, mc=mc, tc=tc, kk=kk: y.dma_start(
                out=xr[kk][:], in_=x_d.ap()[mc * 128:(mc + 1) * 128, tc * 512:(tc + 1) * 512]),
                deps=[prevx[kk]], dma=True, tag=f"ox{mc}_{tc}")
            ta = s.step("ve", lambda v, mc=mc, tc=tc, kk=kk: v.tensor_tensor(
                out=orr[kk][:], in0=xr[kk][:], in1=Y[:, mc * T + tc * 512: mc * T + (tc + 1) * 512],
                op=ALU.add),
                deps=[tlx, ylast[(tc, mc)], prevo[kk]], tag=f"oa{mc}_{tc}")
            tst = s.step("sy", lambda y, mc=mc, tc=tc, kk=kk: y.dma_start(
                out=xo_d.ap()[mc * 128:(mc + 1) * 128, tc * 512:(tc + 1) * 512], in_=orr[kk][:]),
                deps=[ta], dma=True, tag=f"os{mc}_{tc}")
            prevx[kk] = ta
            prevo[kk] = tst
            i += 1
    return p.finish()


_BUILDERS["moe"] = build_moe


def run_moe(xT, inp):
    prog = get_prog("moe")
    w13 = inp["L0_ffn_w13"]
    w2 = inp["L0_ffn_w2"]
    w13t = np.ascontiguousarray(np.concatenate(
        [w13.transpose(0, 2, 1).reshape(16 * 1024, 1024), inp["L0_ffn_w13_s"].T], axis=0))
    w2t = np.ascontiguousarray(np.concatenate(
        [w2.transpose(0, 2, 1).reshape(16 * 512, 1024), inp["L0_ffn_w2_s"].T], axis=0))
    gwT = np.ascontiguousarray(inp["L0_ffn_gate_w"].T)
    gb = inp["L0_ffn_gate_b"].reshape(1, 16).astype(np.float32)
    idf = np.eye(128, dtype=np.float32)
    in_maps = []
    for core in range(NCORES):
        b, half = core // 2, core % 2
        tsl = slice(half * 2048, (half + 1) * 2048)
        in_maps.append({
            "x": np.ascontiguousarray(xT[b][:, tsl]),
            "gwT": gwT, "gb": gb, "w13t": w13t, "w2t": w2t, "ident_f": idf,
        })
    res = prog.run(in_maps)
    return [np.concatenate([res[2 * b]["xo"], res[2 * b + 1]["xo"]], axis=1) for b in range(4)]



def build_attn_b():
    T = 2048
    p = Prog("attn_b")
    s = p.s
    x_d = p.din("x", [1024, T])
    wq_d = p.din("wq", [1024, 1024])
    wk_d = p.din("wk", [1024, 1024])
    wv_d = p.din("wv", [1024, 1024])
    m0_d = p.din("m0", [64, 8 * 64])
    z0_d = p.din("z0c", [64, 8])
    betas_d = p.din("betas_c", [64, 8])
    dmask_d = p.din("dmask", [128, 128], BF16)
    idb_d = p.din("ident_b", [128, 128], BF16)
    idf_d = p.din("ident_f", [128, 128])
    ctx_d = p.dout("ctx", [T, 1024])

    ones_bf, ones_f, ctoks = emit_consts(p)
    A = p.ps("A", [128, 512], F32)
    B = p.ps("B", [128, 512], F32)
    C = p.ps("C", [128, 512], F32)
    D = p.ps("D", [128, 512], F32)
    E = p.ps("E", [128, 1024], F32)
    F_ = p.ps("F", [128, 512], F32)
    G = p.ps("G", [64, 512], BF16)

    xn = p.sb("xn", [128, 8 * T], BF16)
    ln_toks = emit_ln(p, x_d, T, xn, T, ones_bf, ones_f, ctoks,
                      A[0:1, 0:512], B[0:1, 0:512], E, pfx="ln")
    xn_done = ln_toks[-1]

    wq = p.sb("wq_sb", [128, 8 * 1024], BF16)
    wk = p.sb("wk_sb", [128, 8 * 1024], BF16)
    wv = p.sb("wv_sb", [128, 8 * 1024], BF16)
    wtok = []
    for nm, sb_, dd in (("q", wq, wq_d), ("k", wk, wk_d), ("v", wv, wv_d)):
        wtok.append(s.step("gp", lambda g, sb_=sb_, dd=dd: g.dma_start(
            out=sb_[:], in_=AP(dd, 0, [[1024, 128], [128 * 1024, 8], [1, 1024]])),
            dma=True, tag=f"w{nm}"))
    m0 = p.sb("m0_sb", [64, 8 * 64], F32)
    z0c = p.sb("z0_sb", [64, 8], F32)
    betas = p.sb("betas_sb", [64, 8], F32)
    gcol = p.sb("gcol", [64, 8], F32)
    omg = p.sb("omg", [64, 8], F32)
    dmask = p.sb("dmask_sb", [128, 128], BF16)
    idb = p.sb("idb_sb", [128, 128], BF16)
    idf = p.sb("idf_sb", [128, 128], F32)
    for nm, sb_, dd in (("m0", m0, m0_d), ("z0", z0c, z0_d), ("be", betas, betas_d), ("dm", dmask, dmask_d),
                        ("ib", idb, idb_d), ("if", idf, idf_d)):
        wtok.append(s.step("sy", lambda y, sb_=sb_, dd=dd: y.dma_start(out=sb_[:], in_=dd.ap()),
                           dma=True, tag=f"c{nm}"))
    g1 = s.step("sc", lambda sc: sc.activation(out=gcol[:], in_=betas[:], func=AF.Sigmoid),
                deps=wtok, tag="gsig")
    g2 = s.step("ve", lambda v: v.scalar_tensor_tensor(omg[:], gcol[:], -1.0, ones_f[0:64, 0:8],
                                                       ALU.mult, ALU.add), deps=[g1] + list(ctoks), tag="gomg")

    M_f = p.sb("M_f", [64, 8 * 65], F32)
    M_b = p.sb("M_b", [64, 8 * 65], BF16)
    minit = {}
    for hh in range(8):
        t1 = s.step("ve", lambda v, hh=hh: v.tensor_copy(out=M_f[:, hh * 65: hh * 65 + 64],
                                                         in_=m0[:, hh * 64:(hh + 1) * 64]),
                    deps=wtok, tag=f"mi{hh}")
        t2 = s.step("ve", lambda v, hh=hh: v.tensor_copy(out=M_f[:, hh * 65 + 64: hh * 65 + 65],
                                                         in_=z0c[:, hh:hh + 1]),
                    deps=wtok, tag=f"zi{hh}")
        t3 = s.step("ve", lambda v, hh=hh: v.tensor_copy(out=M_b[:, hh * 65:(hh + 1) * 65],
                                                         in_=M_f[:, hh * 65:(hh + 1) * 65]),
                    deps=[t1, t2], tag=f"mb{hh}")
        minit[hh] = t3

    q_sc = p.dtmp("q_sc", [T, 1024], BF16)
    k_sc = p.dtmp("k_sc", [T, 1024], BF16)
    v_sc = p.dtmp("v_sc", [T, 1024], BF16)
    ev_bf = p.sb("ev_bf", [128, 1024], BF16)
    prev_ev = None
    prev_st = None
    for tt in range(T // 128):
        for wsb, dst in ((wq, q_sc), (wk, k_sc), (wv, v_sc)):
            for oc in range(2):
                ps_ = [A, B][oc]
                tm = mm_acc(s, ps_[:, 0:512],
                            [(xn[:, kc * T + tt * 128:(kc * T + (tt + 1) * 128)],
                              wsb[:, kc * 1024 + oc * 512: kc * 1024 + (oc + 1) * 512]) for kc in range(8)],
                            deps=[xn_done, prev_ev] + (wtok if tt == 0 else []), tag=f"pj{tt}_{oc}")
                prev_ev = s.step("ve", lambda v, oc=oc, ps_=ps_: v.tensor_copy(
                    out=ev_bf[:, oc * 512:(oc + 1) * 512], in_=ps_[:, 0:512]),
                    deps=[tm, prev_st], tag=f"pe{tt}_{oc}")
            tst = s.step("sy", lambda y, tt=tt, dst=dst: y.dma_start(
                out=dst.ap()[tt * 128:(tt + 1) * 128, :], in_=ev_bf[:]),
                deps=[prev_ev], dma=True, tag=f"pst{tt}")
            prev_st = tst
    proj_done = prev_st

    qf = p.sb("qf", [64, 512], BF16)
    qt4 = p.sb("qt4", [128, 4 * 64], BF16)
    kf = p.sb("kf", [128, 512], BF16)
    qaug = p.sb("qaug2", [128, 512], BF16)
    vt4 = p.sb("vt4", [128, 4 * 65], BF16)
    kt4 = p.sb("kt4", [128, 4 * 64], BF16)
    sk4 = p.sb("sk4", [128, 4 * 64], BF16)
    sqf = p.sb("sqf", [64, 512], BF16)
    elu1 = p.sb("elu1", [128, 512], F32)
    elu2 = p.sb("elu2", [128, 512], F32)
    q2s = p.sb("q2s", [1, 512], F32)
    k2s = p.sb("k2s", [1, 512], F32)
    qsq2 = p.sb("qsq2", [64, 512], BF16)
    maxk2 = p.sb("maxk2", [1, 2], F32)
    pt2 = p.sb("pt2", [128, 4 * 512], BF16)
    dotn = p.sb("dotn", [64, 512], F32)
    memn = p.sb("memn", [64, 512], F32)
    att_f = p.sb("att_f", [64, 512], F32)
    rrow2 = p.sb("rrow2", [1, 1024], F32)
    stg = p.sb("stg", [128, 4 * 64], F32)

    vt_ones = s.step("ve", lambda v: [v.memset(vt4[:, j * 65 + 64: j * 65 + 65], 1.0) for j in range(4)],
                     tag="vt4_ones")
    kf_ones = s.step("ve", lambda v: v.memset(kf[64:65, :], 1.0), tag="kf_ones")

    last_store = None
    last_exp = None
    last_pv = None
    for hh in range(8):
        m_prev = minit[hh]
        for sg in range(8):
            base = (sg * 256 + hh * 32) * 1024
            vetok = s.tok("ve")
            sctok = s.tok("sc")
            tetok = s.tok("te")
            lq = s.step("sy", lambda y, base=base: y.dma_start(
                out=qt4[:].rearrange("p (j c) -> p j c", j=4),
                in_=AP(q_sc, base, [[64, 128], [128 * 64, 4], [1, 64]])),
                deps=[proj_done, tetok], dma=True, tag=f"lq{hh}_{sg}")
            lv = s.step("gp", lambda g, base=base: g.dma_start(
                out=vt4[:].rearrange("p (j c) -> p j c", j=4)[:, :, 0:64],
                in_=AP(v_sc, base, [[64, 128], [128 * 64, 4], [1, 64]])),
                deps=[proj_done, vetok, vt_ones, tetok], dma=True, tag=f"lv{hh}_{sg}")
            lkt = s.step("gp", lambda g, base=base: g.dma_start(
                out=kt4[:].rearrange("p (j c) -> p j c", j=4),
                in_=AP(k_sc, base, [[64, 128], [128 * 64, 4], [1, 64]])),
                deps=[proj_done, vetok, sctok, tetok], dma=True, tag=f"lkt{hh}_{sg}")
            tq = s.step("te", lambda te: [te.transpose(G[:, j * 128:(j + 1) * 128],
                                                       qt4[:, j * 64:(j + 1) * 64], idb[:])
                                          for j in range(4)],
                        deps=[lq, vetok] + wtok, tag=f"tq{hh}_{sg}")
            eq = s.step("ve", lambda v: v.tensor_copy(out=qf[:], in_=G[0:64, :]), deps=[tq], tag=f"eq{hh}_{sg}")
            tk = s.step("te", lambda te: [te.transpose(G[:, j * 128:(j + 1) * 128],
                                                       kt4[:, j * 64:(j + 1) * 64], idb[:])
                                          for j in range(4)],
                        deps=[lkt, eq], tag=f"tk{hh}_{sg}")
            ek = s.step("ve", lambda v: v.tensor_copy(out=kf[0:64, :], in_=G[0:64, :]), deps=[tk], tag=f"ek{hh}_{sg}")
            lq = eq
            lk = ek
            n1 = s.step("sc", lambda sc: sc.activation(out=qsq2[:], in_=qf[:], func=AF.Square, scale=0.125),
                        deps=[lq], tag=f"nq{hh}_{sg}")
            n2 = mm_acc(s, A[0:1, 0:512], [(ones_bf[0:64, 0:1], qsq2[:])],
                        deps=[n1, s.tok("ve"), s.tok("sc")], tag=f"nqm{hh}_{sg}")
            n3 = s.step("sc", lambda sc: sc.activation(out=qsq2[:], in_=kf[0:64, :], func=AF.Square),
                        deps=[lk, n2], tag=f"nk{hh}_{sg}")
            n4 = mm_acc(s, B[0:1, 0:512], [(ones_bf[0:64, 0:1], qsq2[:])],
                        deps=[n3], tag=f"nkm{hh}_{sg}")
            n5 = s.step("ve", lambda v: v.tensor_copy(out=q2s[:], in_=A[0:1, 0:512]), deps=[n2], tag=f"nqe{hh}_{sg}")
            n6 = s.step("ve", lambda v: v.tensor_copy(out=k2s[:], in_=B[0:1, 0:512]), deps=[n4], tag=f"nke{hh}_{sg}")
            n7 = s.step("sc", lambda sc: [sc.activation(out=q2s[:], in_=q2s[:], func=AF.Sqrt),
                                          sc.activation(out=k2s[:], in_=k2s[:], func=AF.Sqrt)],
                        deps=[n5, n6], tag=f"nsq{hh}_{sg}")
            n8 = s.step("ve", lambda v: v.reduce_max(maxk2[:, 0:1], k2s[:], axis=AX.X), deps=[n7], tag=f"nmx{hh}_{sg}")
            n9 = s.step("ve", lambda v: v.tensor_scalar_mul(maxk2[:, 1:2], maxk2[:, 0:1], -1.0), deps=[n8], tag=f"nng{hh}_{sg}")
            n10 = s.step("ve", lambda v: v.tensor_scalar_mul(qaug[64:65, :], q2s[:], maxk2[:, 1:2]),
                         deps=[n9], tag=f"nc{hh}_{sg}")
            n11 = s.step("ve", lambda v: v.tensor_scalar_mul(qaug[0:64, :], qf[:], 0.125),
                         deps=[lq], tag=f"nqa{hh}_{sg}")
            qk_ready = [n10, n11, lk, kf_ones]
            e1 = s.step("ve", lambda v: v.tensor_scalar_min(elu1[0:64, :], qf[:], 0.0), deps=[lq], tag=f"e1{hh}_{sg}")
            e2 = s.step("sc", lambda sc: sc.activation(out=elu1[0:64, :], in_=elu1[0:64, :], func=AF.Exp),
                        deps=[e1], tag=f"e2{hh}_{sg}")
            e3 = s.step("ve", lambda v: v.tensor_scalar_max(elu2[0:64, :], qf[:], 0.0), deps=[lq], tag=f"e3{hh}_{sg}")
            e4 = s.step("ve", lambda v: v.tensor_add(sqf[:], elu1[0:64, :], elu2[0:64, :]), deps=[e2, e3], tag=f"e4{hh}_{sg}")
            f1 = s.step("ve", lambda v: v.tensor_scalar_min(elu1[:, 0:256], kt4[:], 0.0), deps=[lkt], tag=f"f1{hh}_{sg}")
            f2 = s.step("sc", lambda sc: sc.activation(out=elu1[:, 0:256], in_=elu1[:, 0:256], func=AF.Exp),
                        deps=[f1], tag=f"f2{hh}_{sg}")
            f3 = s.step("ve", lambda v: v.tensor_scalar_max(elu2[:, 0:256], kt4[:], 0.0), deps=[lkt], tag=f"f3{hh}_{sg}")
            f4 = s.step("ve", lambda v: v.tensor_add(sk4[:], elu1[:, 0:256], elu2[:, 0:256]), deps=[f2, f3], tag=f"f4{hh}_{sg}")
            pv_toks = []
            for kt in range(4):
                buf = [A, B][kt % 2]
                a = kt * 128
                tS = s.step("te", lambda te, kt=kt, a=a, buf=buf: te.matmul(
                    buf[:, a:512], kf[0:65, kt * 128:(kt + 1) * 128], qaug[0:65, a:512],
                    start=True, stop=True),
                    deps=qk_ready + [last_exp if kt >= 2 else None, s.tok("ve") if kt < 2 else None],
                    tag=f"S2{hh}_{sg}_{kt}")
                tE = s.step("sc", lambda sc, kt=kt, a=a, buf=buf: sc.activation(
                    out=pt2[:, kt * 512 + a: (kt + 1) * 512], in_=buf[:, a:512], func=AF.Exp),
                    deps=[tS, pv_toks[-1] if pv_toks else None], tag=f"E2{hh}_{sg}_{kt}")
                last_exp = tE
                tM = s.step("ve", lambda v, kt=kt, a=a: v.tensor_tensor(
                    out=pt2[:, kt * 512 + a: kt * 512 + a + 128],
                    in0=pt2[:, kt * 512 + a: kt * 512 + a + 128], in1=dmask[:], op=ALU.mult),
                    deps=[tE], tag=f"M2{hh}_{sg}_{kt}")
                tPV = s.step("te", lambda te, kt=kt, a=a: te.matmul(
                    C[0:65, a:512], vt4[:, kt * 65:(kt + 1) * 65], pt2[:, kt * 512 + a:(kt + 1) * 512],
                    start=(kt == 0), stop=(kt == 3)),
                    deps=[tM, lv, last_pv], tag=f"PV2{hh}_{sg}_{kt}")
                last_pv = None
                pv_toks.append(tPV)
            tMM = s.step("te", lambda te, hh=hh: te.matmul(
                D[0:65, 0:512], M_b[0:64, hh * 65:(hh + 1) * 65], sqf[:], start=True, stop=True),
                deps=[e4, m_prev, s.tok("ve")], tag=f"MM{hh}_{sg}")
            d1 = s.step("ve", lambda v: v.reciprocal(rrow2[0:1, 0:512], C[64:65, 0:512]),
                        deps=[pv_toks[-1]], tag=f"d1{hh}_{sg}")
            d2 = s.step("ve", lambda v: v.reciprocal(rrow2[0:1, 512:1024], D[64:65, 0:512]),
                        deps=[tMM, d1], tag=f"d2{hh}_{sg}")
            d3 = s.step("te", lambda te: [
                te.matmul(E[0:64, 0:512], ones_f[0:1, 0:64], rrow2[0:1, 0:512], start=True, stop=True),
                te.matmul(E[0:64, 512:1024], ones_f[0:1, 0:64], rrow2[0:1, 512:1024], start=True, stop=True)],
                deps=[d2, s.tok("ve")], tag=f"d3{hh}_{sg}")
            d4 = s.step("ve", lambda v: v.tensor_copy(out=dotn[:], in_=C[0:64, 0:512]), deps=[d1], tag=f"d4{hh}_{sg}")
            d5 = s.step("ve", lambda v: v.tensor_copy(out=memn[:], in_=D[0:64, 0:512]), deps=[d2], tag=f"d5{hh}_{sg}")
            d6 = s.step("ve", lambda v: v.tensor_tensor(out=dotn[:], in0=dotn[:], in1=E[0:64, 0:512],
                                                        op=ALU.mult), deps=[d3, d4], tag=f"d6{hh}_{sg}")
            d7 = s.step("ve", lambda v: v.tensor_tensor(out=memn[:], in0=memn[:], in1=E[0:64, 512:1024],
                                                        op=ALU.mult), deps=[d3, d5], tag=f"d7{hh}_{sg}")
            d8 = s.step("ve", lambda v, hh=hh: v.tensor_scalar_mul(dotn[:], dotn[:], omg[:, hh:hh + 1]),
                        deps=[d6, g2], tag=f"d8{hh}_{sg}")
            d9 = s.step("ve", lambda v, hh=hh: v.scalar_tensor_tensor(att_f[:], memn[:], gcol[:, hh:hh + 1],
                                                                      dotn[:], ALU.mult, ALU.add),
                        deps=[d7, d8], tag=f"d9{hh}_{sg}")
            tMU = s.step("te", lambda te: [te.matmul(F_[0:64, 0:65], sk4[:, kt * 64:(kt + 1) * 64],
                                                     vt4[:, kt * 65:(kt + 1) * 65],
                                                     start=(kt == 0), stop=(kt == 3)) for kt in range(4)],
                         deps=[f4, lv, d9, last_store], tag=f"MU{hh}_{sg}")
            m1_ = s.step("ve", lambda v, hh=hh: v.tensor_tensor(
                out=M_f[:, hh * 65:(hh + 1) * 65], in0=M_f[:, hh * 65:(hh + 1) * 65],
                in1=F_[0:64, 0:65], op=ALU.add), deps=[tMU, tMM], tag=f"mu{hh}_{sg}")
            m2_ = s.step("ve", lambda v, hh=hh: v.tensor_copy(
                out=M_b[:, hh * 65:(hh + 1) * 65], in_=M_f[:, hh * 65:(hh + 1) * 65]),
                deps=[m1_], tag=f"mc{hh}_{sg}")
            m_prev = m2_
            trs = []
            for j in range(4):
                tt_ = s.step("te", lambda te, j=j: te.transpose(
                    F_[:, j * 64:(j + 1) * 64], att_f[:, j * 128:(j + 1) * 128], idf[0:64, 0:64]),
                    deps=[d9, tMU, m1_], tag=f"tr{hh}_{sg}_{j}")
                trs.append(tt_)
            ev = s.step("ve", lambda v: v.tensor_copy(out=stg[:], in_=F_[:, 0:256]),
                        deps=trs + [last_store], tag=f"oe{hh}_{sg}")
            tst = s.step("sy", lambda y, base=base: y.dma_start(
                out=AP(ctx_d, base, [[64, 128], [8192, 4], [1, 64]]),
                in_=stg[:].rearrange("p (j c) -> p j c", j=4)),
                deps=[ev], dma=True, tag=f"ost{hh}_{sg}")
            last_store = tst
    return p.finish()


_BUILDERS["attn_b"] = build_attn_b


def run_attn_b(xT, inp):
    prog = get_prog("attn_b")
    dmask = (np.arange(128)[:, None] <= np.arange(128)[None, :]).astype(BF)
    mem0 = inp["L0_attn_b_memory0"][0]
    z0 = inp["L0_attn_b_z0"][0, :, 0, :]
    betas = inp["L0_attn_b_betas"][0, :, 0, :]
    in_maps = []
    for core in range(NCORES):
        b, hg = core // 2, core % 2
        hs = slice(hg * 8, (hg + 1) * 8)
        rows = np.concatenate([np.arange(sg * 512 + hg * 256, sg * 512 + hg * 256 + 256)
                               for sg in range(8)])
        xslab = np.ascontiguousarray(inp_x_slab(xT[b], rows))
        m0c = np.ascontiguousarray(mem0[hs].transpose(1, 0, 2).reshape(64, 8 * 64))
        in_maps.append({
            "x": xslab,
            "wq": inp["L0_attn_b_Wq"], "wk": inp["L0_attn_b_Wk"], "wv": inp["L0_attn_b_Wv"],
            "m0": m0c,
            "z0c": np.ascontiguousarray(z0[hs].T),
            "betas_c": np.ascontiguousarray(betas[hs].T),
            "dmask": dmask,
            "ident_b": np.eye(128, dtype=np.float32).astype(BF),
            "ident_f": np.eye(128, dtype=np.float32),
        })
    res = prog.run(in_maps)
    ctxT = []
    for b in range(4):
        full = np.empty((4096, 1024), np.float32)
        for hg in range(2):
            rows = np.concatenate([np.arange(sg * 512 + hg * 256, sg * 512 + hg * 256 + 256)
                                   for sg in range(8)])
            full[rows] = res[2 * b + hg]["ctx"]
        ctxT.append(np.ascontiguousarray(full.T))
    return ctxT


def inp_x_slab(xTb, rows):
    return xTb[:, rows]




def mm_acc2(s, psum_ap, pairs, deps, tag, start, stop):
    def emit(te):
        insts = []
        n = len(pairs)
        for i, (l, r) in enumerate(pairs):
            insts.append(te.matmul(psum_ap, l, r, start=(start and i == 0), stop=(stop and i == n - 1)))
        return insts
    return s.step("te", emit, deps=deps, tag=tag)


def build_m1_token():
    TE_ = 2560
    T = 2048
    p = Prog("m1_token")
    s = p.s
    x_d = p.din("x", [1024, TE_])
    inwT_d = p.din("inwT", [1024, 2048])
    inb_d = p.din("inb", [128, 16])
    convw_d = p.din("convw", [128, 16 * 3])
    convb_d = p.din("convb", [128, 16])
    hv_d = p.din("hv", [128, 2])
    xpjT_d = p.din("xpjT", [2048, 80])
    xpjb_d = p.din("xpjb", [80, 1])
    dtwT_d = p.din("dtwT", [64, 2048])
    dtb_d = p.din("dtb", [128, 16])
    dpar_d = p.din("dpar", [128, 16])
    delta_d = p.dout("delta", [2048, T])
    du_d = p.dout("du", [2048, T])
    uD_d = p.dout("uD", [2048, T])
    bm_d = p.dout("bm", [16, T])

    ones_bf, ones_f, ctoks = emit_consts(p)
    A = p.ps("A", [128, 512], F32)
    B = p.ps("B", [128, 512], F32)
    E = p.ps("E", [128, 1024], F32)
    X4 = [p.ps(f"X{i}", [80, 512], F32) for i in range(4)]

    xn = p.sb("xn", [128, 8 * TE_], BF16)
    ln_toks = emit_ln(p, x_d, TE_, xn, TE_, ones_bf, ones_f, ctoks,
                      A[0:1, 0:512], B[0:1, 0:512], E, pfx="ln")
    xn_done = ln_toks[-1]

    inwT = p.sb("inwT_sb", [128, 8 * 2048], BF16)
    xpjT = p.sb("xpjT_sb", [128, 16 * 80], BF16)
    dtwT = p.sb("dtwT_sb", [64, 2048], BF16)
    smalls = {}
    wtok = []
    wtok.append(s.step("gp", lambda g: g.dma_start(
        out=inwT[:], in_=AP(inwT_d, 0, [[2048, 128], [128 * 2048, 8], [1, 2048]])), dma=True, tag="winw"))
    wtok.append(s.step("gp", lambda g: g.dma_start(
        out=xpjT[:], in_=AP(xpjT_d, 0, [[80, 128], [128 * 80, 16], [1, 80]])), dma=True, tag="wxpj"))
    wtok.append(s.step("gp", lambda g: g.dma_start(out=dtwT[:], in_=dtwT_d.ap()), dma=True, tag="wdtw"))
    for nm, dd, shp in (("inb", inb_d, [128, 16]), ("convw", convw_d, [128, 48]),
                        ("convb", convb_d, [128, 16]), ("xpjb", xpjb_d, [80, 1]),
                        ("dtb", dtb_d, [128, 16]), ("dpar", dpar_d, [128, 16]),
                        ("hv", hv_d, [128, 2])):
        sb_ = p.sb(nm + "_sb", shp, F32)
        smalls[nm] = sb_
        wtok.append(s.step("sy", lambda y, sb_=sb_, dd=dd: y.dma_start(out=sb_[:], in_=dd.ap()),
                           dma=True, tag=f"c{nm}"))

    u_sc = p.dtmp("u_sc", [2048, T], F32)
    dlr = p.sb("dlr", [64, T], BF16)
    h_sb = p.sb("h_sb", [128, 2052], F32)
    u_f = p.sb("u_f", [128, T], F32)
    u_b = p.sb("u_b", [128, T], BF16)
    uD_sb = p.sb("uD_sb", [128, T], F32)
    t1sb = p.sb("t1sb", [128, 512], F32)
    t2sb = p.sb("t2sb", [128, 512], F32)
    t3sb = p.sb("t3sb", [128, 512], F32)
    xdbl = p.sb("xdbl", [80, T], F32)
    ut = [p.sb("utA", [128, 512], F32), p.sb("utB", [128, 512], F32)]

    prev_ev = None
    prev_u = None
    xdbl_last = [None] * 4
    for mc in range(16):
        ev_toks = []
        for tc in range(5):
            c0 = tc * 512
            n = min(512, 2052 - c0)
            tm = mm_acc(s, A[0:128, 0:n],
                        [(inwT[:, kc * 2048 + mc * 128: kc * 2048 + (mc + 1) * 128],
                          xn[:, kc * TE_ + c0: kc * TE_ + c0 + n]) for kc in range(8)],
                        deps=[xn_done, prev_ev] + (wtok if mc == 0 and tc == 0 else []),
                        tag=f"h{mc}_{tc}")
            pe = s.step("ve", lambda v, c0=c0, n=n, mc=mc: v.tensor_scalar(
                h_sb[:, c0:c0 + n], A[0:128, 0:n], smalls["inb"][:, mc:mc + 1], None, ALU.add),
                deps=[tm, prev_u], tag=f"he{mc}_{tc}")
            prev_ev = pe
            ev_toks.append(pe)
        hv1 = s.step("ve", lambda v: v.tensor_scalar(h_sb[:, 0:1], h_sb[:, 0:1],
                                                     smalls["hv"][:, 0:1], None, ALU.mult),
                     deps=ev_toks, tag=f"hv1_{mc}")
        hv2 = s.step("ve", lambda v: v.tensor_scalar(h_sb[:, 2049:2050], h_sb[:, 2049:2050],
                                                     smalls["hv"][:, 1:2], None, ALU.mult),
                     deps=ev_toks, tag=f"hv2_{mc}")
        c1 = s.step("ve", lambda v, mc=mc: v.tensor_scalar(
            u_f[:], h_sb[:, 0:T], smalls["convw"][:, mc * 3: mc * 3 + 1], None, ALU.mult),
            deps=ev_toks + [hv1, hv2], tag=f"cv1_{mc}")
        c2 = s.step("ve", lambda v, mc=mc: v.scalar_tensor_tensor(
            u_f[:], h_sb[:, 1:T + 1], smalls["convw"][:, mc * 3 + 1: mc * 3 + 2], u_f[:],
            ALU.mult, ALU.add), deps=[c1], tag=f"cv2_{mc}")
        c3 = s.step("ve", lambda v, mc=mc: v.scalar_tensor_tensor(
            u_f[:], h_sb[:, 2:T + 2], smalls["convw"][:, mc * 3 + 2: mc * 3 + 3], u_f[:],
            ALU.mult, ALU.add), deps=[c2], tag=f"cv3_{mc}")
        c4 = s.step("ve", lambda v, mc=mc: v.tensor_scalar(
            u_f[:], u_f[:], smalls["convb"][:, mc: mc + 1], None, ALU.add), deps=[c3], tag=f"cv4_{mc}")
        c5 = s.step("sc", lambda sc: sc.activation(out=u_f[:], in_=u_f[:], func=AF.Silu),
                    deps=[c4], tag=f"cv5_{mc}")
        c6 = s.step("ve", lambda v: v.tensor_copy(out=u_b[:], in_=u_f[:]),
                    deps=[c5, s.tok("te")], tag=f"cv6_{mc}")
        c7 = s.step("ve", lambda v, mc=mc: v.tensor_scalar(
            uD_sb[:], u_f[:], smalls["dpar"][:, mc:mc + 1], None, ALU.mult), deps=[c5], tag=f"cv7_{mc}")
        st1 = s.step("sy", lambda y, mc=mc: y.dma_start(
            out=u_sc.ap()[mc * 128:(mc + 1) * 128, :], in_=u_f[:]), deps=[c5], dma=True, tag=f"su{mc}")
        st2 = s.step("sy", lambda y, mc=mc: y.dma_start(
            out=uD_d.ap()[mc * 128:(mc + 1) * 128, :], in_=uD_sb[:]), deps=[c7], dma=True, tag=f"suD{mc}")
        prev_u = st2
        for tc in range(4):
            xdbl_last[tc] = mm_acc2(s, X4[tc][:, 0:512],
                                    [(xpjT[:, mc * 80:(mc + 1) * 80],
                                      u_b[:, tc * 512:(tc + 1) * 512])],
                                    deps=[c6, xdbl_last[tc]],
                                    tag=f"xd{mc}_{tc}", start=(mc == 0), stop=(mc == 15))

    bm_toks = []
    dlr_toks = []
    for tc in range(4):
        xe = s.step("ve", lambda v, tc=tc: v.tensor_scalar(
            xdbl[:, tc * 512:(tc + 1) * 512], X4[tc][:, 0:512], smalls["xpjb"][:, 0:1], None, ALU.add),
            deps=[xdbl_last[tc]], tag=f"xe{tc}")
        dl = s.step("ve", lambda v, tc=tc: v.tensor_copy(out=dlr[:, tc * 512:(tc + 1) * 512],
                                                         in_=xdbl[0:64, tc * 512:(tc + 1) * 512]),
                    deps=[xe], tag=f"dl{tc}")
        dlr_toks.append(dl)
        bm_toks.append(xe)
    stbm = s.step("sy", lambda y: y.dma_start(out=bm_d.ap(), in_=xdbl[64:80, :]),
                  deps=bm_toks, dma=True, tag="stbm")

    prev = {}
    i = 0
    for mc in range(16):
        for tc in range(4):
            k = i % 2
            ps_ = [A, B][k]
            tm = mm_acc(s, ps_[:, 0:512],
                        [(dtwT[:, mc * 128:(mc + 1) * 128], dlr[:, tc * 512:(tc + 1) * 512])],
                        deps=[dlr_toks[tc], prev.get(("p", k))], tag=f"dm{mc}_{tc}")
            lu = s.step("sy", lambda y, mc=mc, tc=tc, k=k: y.dma_start(
                out=ut[k][:], in_=u_sc.ap()[mc * 128:(mc + 1) * 128, tc * 512:(tc + 1) * 512]),
                deps=[prev.get(("u", k)), s.tok("sy") if i == 0 else None], dma=True, tag=f"lu{mc}_{tc}")
            s1 = s.step("ve", lambda v, mc=mc, ps_=ps_: v.tensor_scalar(
                t1sb[:], ps_[:, 0:512], smalls["dtb"][:, mc:mc + 1], None, ALU.add),
                deps=[tm, prev.get("std"), prev.get("stu")], tag=f"s1{mc}_{tc}")
            s2 = s.step("ve", lambda v: v.tensor_scalar_max(t2sb[:], t1sb[:], 0.0), deps=[s1], tag=f"s2{mc}_{tc}")
            s3 = s.step("ve", lambda v: v.tensor_scalar_mul(t3sb[:], t1sb[:], -1.0),
                        deps=[s1, prev.get("s6")], tag=f"s3{mc}_{tc}")
            s4 = s.step("ve", lambda v: v.tensor_tensor(t3sb[:], t1sb[:], t3sb[:], ALU.min), deps=[s3], tag=f"s4{mc}_{tc}")
            s5 = s.step("sc", lambda sc: sc.activation(out=t3sb[:], in_=t3sb[:], func=AF.Exp), deps=[s4], tag=f"s5{mc}_{tc}")
            s6 = s.step("sc", lambda sc: sc.activation(out=t3sb[:], in_=t3sb[:], func=AF.Ln, bias=1.0), deps=[s5], tag=f"s6{mc}_{tc}")
            s7 = s.step("ve", lambda v: v.tensor_add(t1sb[:], t2sb[:], t3sb[:]), deps=[s6, s2], tag=f"s7{mc}_{tc}")
            std = s.step("sy", lambda y, mc=mc, tc=tc: y.dma_start(
                out=delta_d.ap()[mc * 128:(mc + 1) * 128, tc * 512:(tc + 1) * 512], in_=t1sb[:]),
                deps=[s7], dma=True, tag=f"std{mc}_{tc}")
            s8 = s.step("ve", lambda v, k=k: v.tensor_tensor(t2sb[:], t1sb[:], ut[k][:], ALU.mult),
                        deps=[s7, lu], tag=f"s8{mc}_{tc}")
            stu = s.step("sy", lambda y, mc=mc, tc=tc: y.dma_start(
                out=du_d.ap()[mc * 128:(mc + 1) * 128, tc * 512:(tc + 1) * 512], in_=t2sb[:]),
                deps=[s8], dma=True, tag=f"stu{mc}_{tc}")
            prev[("p", k)] = s1
            prev[("u", k)] = s8
            prev["std"] = std
            prev["stu"] = stu
            prev["s6"] = s6
            i += 1
    return p.finish()


def build_m1_scan():
    T = 4096
    p = Prog("m1_scan")
    s = p.s
    de_d = p.din("delta", [1024, T])
    du_d = p.din("du", [1024, T])
    b_d = p.din("bmat", [64, T])
    alog_d = p.din("alog", [128, 2 * 16])
    y_d = p.dout("y", [1024, T])

    Acol = p.sb("Acol", [128, 32], F32)
    dsb = [p.sb(f"dsb{i}", [128, T], F32) for i in range(2)]
    usb = [p.sb(f"usb{i}", [128, T], F32) for i in range(2)]
    ysb = [p.sb(f"ysb{i}", [128, T], F32) for i in range(2)]
    bbc = p.sb("bbc", [128, T], F32)
    dA = p.sb("dA", [128, T], F32)
    dBu = p.sb("dBu", [128, T], F32)
    sco = p.sb("sco", [128, T], F32)

    la = s.step("sy", lambda y: y.dma_start(out=Acol[:], in_=alog_d.ap()), dma=True, tag="la")
    ta = s.step("sc", lambda sc: sc.activation(out=Acol[:], in_=Acol[:], func=AF.Exp), deps=[la], tag="aexp")
    tn = s.step("ve", lambda v: v.tensor_scalar_mul(Acol[:], Acol[:], -1.0), deps=[ta], tag="aneg")

    prev_ld = {}
    prev_y = {}
    for b in range(4):
        lds = []
        for ct in range(2):
            l1 = s.step("sy", lambda y, b=b, ct=ct: y.dma_start(
                out=dsb[ct][:], in_=de_d.ap()[b * 256 + ct * 128: b * 256 + (ct + 1) * 128, :]),
                deps=[prev_ld.get(("d", ct))], dma=True, tag=f"ld{b}_{ct}")
            l2 = s.step("gp", lambda g, b=b, ct=ct: g.dma_start(
                out=usb[ct][:], in_=du_d.ap()[b * 256 + ct * 128: b * 256 + (ct + 1) * 128, :]),
                deps=[prev_ld.get(("u", ct))], dma=True, tag=f"lu{b}_{ct}")
            lds.append((l1, l2))
        ytoks = {0: [], 1: []}
        for n in range(16):
            lb = s.step("sy", lambda y, b=b, n=n: y.dma_start(
                out=bbc[:], in_=AP(b_d, (b * 16 + n) * T, [[0, 128], [1, T]])),
                deps=[s.tok("ve")], dma=True, tag=f"lb{b}_{n}")
            for ct in range(2):
                l1, l2 = lds[ct]
                d1 = s.step("sc", lambda sc, n=n, ct=ct: sc.activation(out=dA[:], in_=dsb[ct][:], func=AF.Exp,
                                                                scale=Acol[:, ct * 16 + n: ct * 16 + n + 1]),
                            deps=[l1, tn, s.tok("ve")], tag=f"da{b}_{n}_{ct}")
                d2 = s.step("ve", lambda v, ct=ct: v.tensor_mul(dBu[:], usb[ct][:], bbc[:]),
                            deps=[l2, lb], tag=f"db{b}_{n}_{ct}")
                d3 = s.step("ve", lambda v: v.tensor_tensor_scan(sco[:], dA[:], dBu[:], 0.0,
                                                                 ALU.mult, ALU.add),
                            deps=[d1, d2], tag=f"sc{b}_{n}_{ct}")
                if n == 0:
                    d4 = s.step("ve", lambda v, ct=ct: v.tensor_copy(out=ysb[ct][:], in_=sco[:]),
                                deps=[d3, prev_y.get(ct)], tag=f"ya{b}_{n}_{ct}")
                else:
                    d4 = s.step("ve", lambda v, ct=ct: v.tensor_add(ysb[ct][:], ysb[ct][:], sco[:]),
                                deps=[d3], tag=f"ya{b}_{n}_{ct}")
                ytoks[ct].append(d4)
        for ct in range(2):
            ty = s.step("gp", lambda g, b=b, ct=ct: g.dma_start(
                out=y_d.ap()[b * 256 + ct * 128: b * 256 + (ct + 1) * 128, :], in_=ysb[ct][:]),
                deps=[ytoks[ct][-1]], dma=True, tag=f"sy{b}_{ct}")
            prev_y[ct] = ty
            prev_ld[("d", ct)] = s.tok("sc")
            prev_ld[("u", ct)] = s.tok("ve")
    return p.finish()


def build_m1_out():
    T = 2048
    p = Prog("m1_out")
    s = p.s
    x_d = p.din("x", [1024, T])
    y_din = p.din("y", [2048, T])
    uD_d = p.din("uD", [2048, T])
    w_d = p.din("outwT", [2048, 1024])
    b_d = p.din("biascol", [128, 8])
    xo_d = p.dout("xo", [1024, T])

    wsb = p.sb("wsb", [128, 16 * 1024], BF16)
    yb = p.sb("yb", [128, 16 * T], BF16)
    bcol = p.sb("bcol", [128, 8], F32)
    ybuf = [p.sb("yb0", [128, T], F32), p.sb("yb1", [128, T], F32)]
    ubuf = [p.sb("ub0", [128, T], F32), p.sb("ub1", [128, T], F32)]
    xbuf = [p.sb("xb0", [128, 512], F32), p.sb("xb1", [128, 512], F32)]
    obuf = [p.sb("ob0", [128, 512], F32), p.sb("ob1", [128, 512], F32)]
    psum = [p.ps("pA", [128, 512], F32), p.ps("pB", [128, 512], F32)]

    wt = [s.step("gp", lambda g: g.dma_start(
        out=wsb[:], in_=AP(w_d, 0, [[1024, 128], [128 * 1024, 16], [1, 1024]])), dma=True, tag="w"),
        s.step("sy", lambda y: y.dma_start(out=bcol[:], in_=b_d.ap()), dma=True, tag="b")]
    add_toks = []
    prev = {}
    for kc in range(16):
        k = kc % 2
        l1 = s.step("sy", lambda y, kc=kc, k=k: y.dma_start(
            out=ybuf[k][:], in_=y_din.ap()[kc * 128:(kc + 1) * 128, :]),
            deps=[prev.get(("y", k))], dma=True, tag=f"ly{kc}")
        l2 = s.step("gp", lambda g, kc=kc, k=k: g.dma_start(
            out=ubuf[k][:], in_=uD_d.ap()[kc * 128:(kc + 1) * 128, :]),
            deps=[prev.get(("u", k))], dma=True, tag=f"lud{kc}")
        tadd = s.step("ve", lambda v, kc=kc, k=k: v.tensor_tensor(
            out=yb[:, kc * T:(kc + 1) * T], in0=ybuf[k][:], in1=ubuf[k][:], op=ALU.add),
            deps=[l1, l2], tag=f"ad{kc}")
        prev[("y", k)] = tadd
        prev[("u", k)] = tadd
        add_toks.append(tadd)
    yb_done = add_toks[-1]

    i = 0
    prev2 = {}
    for mc in range(8):
        for tc in range(T // 512):
            k = i % 2
            ps_, xb, ob = psum[k], xbuf[k], obuf[k]
            tlx = s.step("sy", lambda y, mc=mc, tc=tc, xb=xb: y.dma_start(
                out=xb[:], in_=x_d.ap()[mc * 128:(mc + 1) * 128, tc * 512:(tc + 1) * 512]),
                deps=[prev2.get(("x", k))], dma=True, tag=f"lx{mc}_{tc}")
            tm = mm_acc(s, ps_[:],
                        [(wsb[:, kc * 1024 + mc * 128: kc * 1024 + (mc + 1) * 128],
                          yb[:, kc * T + tc * 512: kc * T + (tc + 1) * 512]) for kc in range(16)],
                        deps=([yb_done] + wt if i == 0 else []) + [prev2.get(("p", k))], tag=f"mm{mc}_{tc}")
            tev = s.step("ve", lambda v, mc=mc, ps_=ps_, xb=xb, ob=ob: v.scalar_tensor_tensor(
                ob[:], ps_[:], bcol[:, mc:mc + 1], xb[:], ALU.add, ALU.add),
                deps=[tm, tlx, prev2.get(("o", k))] + (wt if i == 0 else []), tag=f"ev{mc}_{tc}")
            tst = s.step("sy", lambda y, mc=mc, tc=tc, ob=ob: y.dma_start(
                out=xo_d.ap()[mc * 128:(mc + 1) * 128, tc * 512:(tc + 1) * 512], in_=ob[:]),
                deps=[tev], dma=True, tag=f"st{mc}_{tc}")
            prev2[("x", k)] = tev
            prev2[("p", k)] = tev
            prev2[("o", k)] = tst
            i += 1
    return p.finish()


_BUILDERS["m1_token"] = build_m1_token
_BUILDERS["m1_scan"] = build_m1_scan
_BUILDERS["m1_out"] = build_m1_out


def run_mamba1(xT, inp):
    prog = get_prog("m1_token")
    inwT = np.ascontiguousarray(inp["L1_attn_a_in_w"].T)
    cw = inp["L1_attn_a_conv_w"][:, 0, :]
    convw = np.ascontiguousarray(cw.reshape(16, 128, 3).transpose(1, 0, 2).reshape(128, 48))
    col16 = lambda v: np.ascontiguousarray(v.reshape(16, 128).T)
    xpjT = np.ascontiguousarray(inp["L1_attn_a_x_proj_w"].T)
    dtwT = np.ascontiguousarray(inp["L1_attn_a_dt_w"].T)
    in_maps = []
    for core in range(NCORES):
        b, half = core // 2, core % 2
        t0 = half * 2048
        xe = np.zeros((1024, 2560), np.float32)
        xe[:, 1:2049] = xT[b][:, t0:t0 + 2048]
        hv = np.zeros((128, 2), np.float32)
        if half == 1:
            xe[:, 0] = xT[b][:, 2047]
            hv[:, 0] = 1.0
        if half == 0:
            xe[:, 2049] = xT[b][:, 2048]
            hv[:, 1] = 1.0
        in_maps.append({
            "x": xe, "inwT": inwT, "inb": col16(inp["L1_attn_a_in_b"]),
            "convw": convw, "convb": col16(inp["L1_attn_a_conv_b"]), "hv": hv,
            "xpjT": xpjT, "xpjb": inp["L1_attn_a_x_proj_b"].reshape(80, 1),
            "dtwT": dtwT, "dtb": col16(inp["L1_attn_a_dt_b"]),
            "dpar": col16(inp["L1_attn_a_D_param"]),
        })
    res = prog.run(in_maps)
    delta = [np.concatenate([res[2 * b]["delta"], res[2 * b + 1]["delta"]], axis=1) for b in range(4)]
    du = [np.concatenate([res[2 * b]["du"], res[2 * b + 1]["du"]], axis=1) for b in range(4)]
    uD = [np.concatenate([res[2 * b]["uD"], res[2 * b + 1]["uD"]], axis=1) for b in range(4)]
    bm = [np.concatenate([res[2 * b]["bm"], res[2 * b + 1]["bm"]], axis=1) for b in range(4)]

    prog2 = get_prog("m1_scan")
    bmat_all = np.concatenate(bm, axis=0)
    alog = inp["L1_attn_a_A_log"]
    in_maps2 = []
    for core in range(NCORES):
        ch0 = core * 256
        de_c = np.concatenate([delta[b][ch0:ch0 + 256] for b in range(4)], axis=0)
        du_c = np.concatenate([du[b][ch0:ch0 + 256] for b in range(4)], axis=0)
        al = alog[ch0:ch0 + 256].reshape(2, 128, 16).transpose(1, 0, 2).reshape(128, 32)
        in_maps2.append({
            "delta": np.ascontiguousarray(de_c), "du": np.ascontiguousarray(du_c),
            "bmat": np.ascontiguousarray(bmat_all),
            "alog": np.ascontiguousarray(al),
        })
    res2 = prog2.run(in_maps2)
    y_full = []
    for b in range(4):
        y_full.append(np.concatenate([res2[c]["y"][b * 256:(b + 1) * 256] for c in range(NCORES)], axis=0))

    prog3 = get_prog("m1_out")
    outwT = np.ascontiguousarray(inp["L1_attn_a_out_w"].T)
    bcol = np.ascontiguousarray(inp["L1_attn_a_out_b"].reshape(8, 128).T)
    in_maps3 = []
    for core in range(NCORES):
        b, half = core // 2, core % 2
        tsl = slice(half * 2048, (half + 1) * 2048)
        in_maps3.append({
            "x": np.ascontiguousarray(xT[b][:, tsl]),
            "y": np.ascontiguousarray(y_full[b][:, tsl]),
            "uD": np.ascontiguousarray(uD[b][:, tsl]),
            "outwT": outwT, "biascol": bcol,
        })
    res3 = prog3.run(in_maps3)
    return [np.concatenate([res3[2 * b]["xo"], res3[2 * b + 1]["xo"]], axis=1) for b in range(4)]





def build_mamba2():
    T = 4096
    NC_ = 32
    p = Prog("mamba2")
    s = p.s
    x_d = p.din("x", [1024, T])
    normw_d = p.din("normw", [128, 8])
    inwT_d = p.din("inwT", [512, 2320])
    convw_d = p.din("convw", [128, 40])
    convb_d = p.din("convb", [128, 10])
    dtb_d = p.din("dtbias", [16, 1])
    alog_d = p.din("alog", [16, 1])
    dbc_d = p.din("dbc", [64, 16])
    gnw_d = p.din("gnw", [128, 8])
    outpwT_d = p.din("outpwT", [1024, 512])
    maskf_d = p.din("maskf", [128, 128])
    idb_d = p.din("ident_b", [128, 128], BF16)
    idf_d = p.din("ident_f", [128, 128])
    out_d = p.dout("outh", [T, 512])

    z_sc = p.dtmp("z_sc", [1024, T])
    xf_sc = p.dtmp("xf_sc", [1024, T])
    xbf_sc = p.dtmp("xbf_sc", [1024, T], BF16)
    yg_sc = p.dtmp("yg_sc", [1024, T])
    yh_sc = p.dtmp("yh_sc", [1024, T], BF16)
    Y_sc = p.dtmp("Y_sc", [1024, T])
    sg_sc = p.dtmp("sg_sc", [16, T])
    daf_sc = p.dtmp("daf_sc", [1, 512])

    ones_bf, ones_f, ctoks = emit_consts(p)
    A = p.ps("A", [128, 512], F32)
    B = p.ps("B", [128, 512], F32)
    C2 = p.ps("C2", [128, 512], F32)
    D2 = p.ps("D2", [128, 512], F32)
    E2 = p.ps("E2", [128, 1024], F32)
    Gb = p.ps("Gb", [128, 256], BF16)

    outpwT = p.sb("outpwT_sb", [128, 8 * 512], BF16)
    smalls = {}
    wtok = [s.step("gp", lambda g: g.dma_start(
        out=outpwT[:], in_=AP(outpwT_d, 0, [[512, 128], [128 * 512, 8], [1, 512]])), dma=True, tag="wout")]
    for nm, dd, shp in (("normw", normw_d, [128, 8]), ("convw", convw_d, [128, 40]),
                        ("convb", convb_d, [128, 10]), ("dtbias", dtb_d, [16, 1]),
                        ("alog", alog_d, [16, 1]), ("dbc", dbc_d, [64, 16]),
                        ("gnw", gnw_d, [128, 8]), ("maskf", maskf_d, [128, 128])):
        sb_ = p.sb(nm + "_sb", shp, F32)
        smalls[nm] = sb_
        wtok.append(s.step("sy", lambda y, sb_=sb_, dd=dd: y.dma_start(out=sb_[:], in_=dd.ap()),
                           dma=True, tag=f"c{nm}"))
    idb = p.sb("idb_sb", [128, 128], BF16)
    idf = p.sb("idf_sb", [128, 128], F32)
    wtok.append(s.step("sy", lambda y: y.dma_start(out=idb[:], in_=idb_d.ap()), dma=True, tag="cidb"))
    wtok.append(s.step("sy", lambda y: y.dma_start(out=idf[:], in_=idf_d.ap()), dma=True, tag="cidf"))

    xnh = p.sb("xnh", [128, 4 * T], BF16)
    xt2 = [p.sb("m2xtA", [128, 512], F32), p.sb("m2xtB", [128, 512], F32)]
    sqb = p.sb("m2sq", [128, 512], BF16)
    rqr = p.sb("m2rq", [1, 512], F32)
    xd = x_d.ap()
    mm_last = None
    bc_free = None
    prev_x = {}
    rms_last = None
    for t in range(8):
        for c in range(8):
            k = c % 2
            ld = s.step("sy", lambda y, c=c, t=t, k=k: y.dma_start(
                out=xt2[k][:], in_=xd[c * 128:(c + 1) * 128, t * 512:(t + 1) * 512]),
                deps=[prev_x.get(k)], dma=True, tag=f"rm_ld{t}_{c}")
            tsq = s.step("sc", lambda sc, k=k: sc.activation(out=sqb[:], in_=xt2[k][:], func=AF.Square),
                         deps=[ld, mm_last], tag=f"rm_sq{t}_{c}")
            mm_last = s.step("te", lambda te, c=c: te.matmul(
                A[0:1, 0:512], ones_bf[:, 0:1], sqb[:], start=(c == 0), stop=(c == 7)),
                deps=[tsq] + (list(ctoks) if t == 0 and c == 0 else []), tag=f"rm_mm{t}_{c}")
            prev_x[k] = tsq
        t1 = s.step("ve", lambda v: v.tensor_scalar(rqr[:], A[0:1, 0:512], 1.0 / 1024.0, 1e-5,
                                                    ALU.mult, ALU.add), deps=[mm_last, bc_free], tag=f"rm_s1{t}")
        t2 = s.step("sc", lambda sc: sc.activation(out=rqr[:], in_=rqr[:], func=AF.Sqrt), deps=[t1], tag=f"rm_s2{t}")
        t3 = s.step("ve", lambda v: v.reciprocal(rqr[:], rqr[:]), deps=[t2], tag=f"rm_s3{t}")
        t4 = s.step("te", lambda te: te.matmul(E2[:, 0:512], ones_f[0:1, 0:128], rqr[:],
                                               start=True, stop=True), deps=[t3, bc_free], tag=f"rm_bc{t}")
        toks = []
        for c in range(4):
            k = c % 2
            ld = s.step("sy", lambda y, c=c, t=t, k=k: y.dma_start(
                out=xt2[k][:], in_=xd[c * 128:(c + 1) * 128, t * 512:(t + 1) * 512]),
                deps=[prev_x.get(k), mm_last], dma=True, tag=f"rm_l2{t}_{c}")
            tk = s.step("ve", lambda v, c=c, t=t, k=k: v.scalar_tensor_tensor(
                xnh[:, c * T + t * 512: c * T + (t + 1) * 512], xt2[k][:],
                smalls["normw"][:, c:c + 1], E2[:, 0:512], ALU.mult, ALU.mult),
                deps=[t4, ld] + wtok, tag=f"rm_x{t}_{c}")
            prev_x[k] = tk
            toks.append(tk)
        bc_free = toks[-1]
        rms_last = toks[-1]
    xn_done = rms_last

    inwt2 = [p.sb("m2iwA", [128, 4 * 128], BF16), p.sb("m2iwB", [128, 4 * 128], BF16)]
    Bf_bf = p.sb("m2B", [128, T], BF16)
    Cf_bf = p.sb("m2C", [128, T], BF16)
    pk1 = p.sb("m2pk1", [128, T], F32)
    pk2 = p.sb("m2pk2", [128, T], F32)
    hc = p.sb("m2hc", [128, T + 3], F32)
    cv = p.sb("m2cv", [128, T], F32)
    dtsb = pk1[0:16, :]
    dtA = pk2[0:16, :]
    Sg = hc[0:16, 0:T]
    nSg = pk1[32:48, :]
    t16 = cv[0:16, 0:T]
    ones16 = t16
    ev32 = p.sb("m2ev", [128, 512], F32)
    hz = s.step("ve", lambda v: v.memset(hc[:, 0:3], 0.0), tag="m2hz")
    prev_ev = None
    prev_st = None
    prev_w = {}
    dt_ready = None
    conv_toks = []
    for m in range(19):
        nrow = 128 if m < 18 else 16
        k = m % 2
        lw = s.step("gp", lambda g, m=m, k=k, nrow=nrow: g.dma_start(
            out=inwt2[k][:, 0:4 * nrow].rearrange("p (kc c) -> p kc c", kc=4),
            in_=AP(inwT_d, m * 128, [[2320, 128], [128 * 2320, 4], [1, nrow]])),
            deps=[prev_w.get(k)], dma=True, tag=f"lw{m}")
        ev_toks = []
        for tc in range(8):
            tm = mm_acc(s, A[0:nrow, 0:512],
                        [(inwt2[k][:, kc * nrow: (kc + 1) * nrow],
                          xnh[:, kc * T + tc * 512: kc * T + (tc + 1) * 512]) for kc in range(4)],
                        deps=[xn_done, prev_ev, lw] + (wtok if m == 0 and tc == 0 else []), tag=f"pj{m}_{tc}")
            if m < 8:
                pe = s.step("ve", lambda v, tc=tc: v.tensor_copy(out=ev32[:], in_=A[0:128, 0:512]),
                            deps=[tm, prev_st], tag=f"pz{m}_{tc}")
                st = s.step("sy", lambda y, m=m, tc=tc: y.dma_start(
                    out=z_sc.ap()[m * 128:(m + 1) * 128, tc * 512:(tc + 1) * 512], in_=ev32[:]),
                    deps=[pe], dma=True, tag=f"pzs{m}_{tc}")
                prev_st = st
                prev_ev = pe
            elif m < 18:
                pe = s.step("ve", lambda v, tc=tc: v.tensor_copy(out=hc[:, 3 + tc * 512: 3 + (tc + 1) * 512],
                                                                 in_=A[0:128, 0:512]),
                            deps=[tm], tag=f"px{m}_{tc}")
                prev_ev = pe
                ev_toks.append(pe)
            else:
                pe = s.step("ve", lambda v, tc=tc: v.tensor_copy(out=dtsb[:, tc * 512:(tc + 1) * 512],
                                                                 in_=A[0:16, 0:512]),
                            deps=[tm], tag=f"pd{m}_{tc}")
                prev_ev = pe
                ev_toks.append(pe)
        prev_w[k] = s.tok("te")
        if 8 <= m < 18:
            mc = m - 8
            c1 = s.step("ve", lambda v, mc=mc: v.tensor_scalar(
                cv[:], hc[:, 0:T], smalls["convw"][:, mc * 4: mc * 4 + 1], None, ALU.mult),
                deps=ev_toks + [hz, s.tok("gp"), s.tok("sy")], tag=f"cva{m}")
            c2 = s.step("ve", lambda v, mc=mc: v.scalar_tensor_tensor(
                cv[:], hc[:, 1:T + 1], smalls["convw"][:, mc * 4 + 1: mc * 4 + 2], cv[:],
                ALU.mult, ALU.add), deps=[c1], tag=f"cvb{m}")
            c3 = s.step("ve", lambda v, mc=mc: v.scalar_tensor_tensor(
                cv[:], hc[:, 2:T + 2], smalls["convw"][:, mc * 4 + 2: mc * 4 + 3], cv[:],
                ALU.mult, ALU.add), deps=[c2], tag=f"cvc{m}")
            c4 = s.step("ve", lambda v, mc=mc: v.scalar_tensor_tensor(
                cv[:], hc[:, 3:T + 3], smalls["convw"][:, mc * 4 + 3: mc * 4 + 4], cv[:],
                ALU.mult, ALU.add), deps=[c3], tag=f"cvd{m}")
            c5 = s.step("ve", lambda v, mc=mc: v.tensor_scalar(
                cv[:], cv[:], smalls["convb"][:, mc:mc + 1], None, ALU.add), deps=[c4], tag=f"cve{m}")
            c6 = s.step("sc", lambda sc: sc.activation(out=cv[:], in_=cv[:], func=AF.Silu),
                        deps=[c5], tag=f"cvf{m}")
            if mc < 8:
                c8 = s.step("gp", lambda g, mc=mc: g.dma_start(out=xf_sc.ap()[mc * 128:(mc + 1) * 128, :],
                                                              in_=cv[:]),
                            deps=[c6], dma=True, tag=f"cvh{m}")
                c9 = s.step("gp", lambda g, mc=mc: g.dma_start(out=xbf_sc.ap()[mc * 128:(mc + 1) * 128, :],
                                                               in_=cv[:]),
                            deps=[c6], dma=True, tag=f"cvi{m}")
                conv_toks += [c8, c9]
                prev_ev = s.tok("ve")
            elif mc == 8:
                c7 = s.step("ve", lambda v: v.tensor_copy(out=Bf_bf[:], in_=cv[:]), deps=[c6], tag=f"cvg{m}")
                conv_toks.append(c7)
                prev_ev = c7
            else:
                c7 = s.step("ve", lambda v: v.tensor_copy(out=Cf_bf[:], in_=cv[:]), deps=[c6], tag=f"cvg{m}")
                conv_toks.append(c7)
                prev_ev = c7
        if m == 18:
            dt_ready = ev_toks[-1]

    negSgT = p.sb("m2negSgT", [128, 32 * 16], F32)
    dtT = p.sb("m2dtT", [128, 32 * 16], F32)
    acol = p.sb("m2acol", [16, 2], F32)
    dafr = p.sb("m2dafr", [16, 32], F32)
    dafbc = p.sb("m2dafbc", [128, 512], F32)

    q1 = s.step("ve", lambda v: v.tensor_scalar(dtsb[:], dtsb[:], smalls["dtbias"][:, 0:1], None, ALU.add),
                deps=[dt_ready] + wtok, tag="m2q1")
    q2 = s.step("ve", lambda v: v.tensor_scalar_max(t16[:], dtsb[:], 0.0), deps=[q1], tag="m2q2")
    q3 = s.step("ve", lambda v: v.tensor_scalar_mul(dtA[:], dtsb[:], -1.0), deps=[q1], tag="m2q3")
    q4 = s.step("ve", lambda v: v.tensor_tensor(dtA[:], dtsb[:], dtA[:], ALU.min), deps=[q3], tag="m2q4")
    q5 = s.step("sc", lambda sc: sc.activation(out=dtA[:], in_=dtA[:], func=AF.Exp), deps=[q4], tag="m2q5")
    q6 = s.step("sc", lambda sc: sc.activation(out=dtA[:], in_=dtA[:], func=AF.Ln, bias=1.0), deps=[q5], tag="m2q6")
    q7 = s.step("ve", lambda v: v.tensor_add(dtsb[:], t16[:], dtA[:]), deps=[q6, q2], tag="m2q7")
    o1 = s.step("ve", lambda v: v.memset(ones16[:], 1.0), deps=[q7], tag="m2o16")
    a1 = s.step("sc", lambda sc: sc.activation(out=acol[:, 0:1], in_=smalls["alog"][:, 0:1], func=AF.Exp),
                deps=wtok + [q6], tag="m2a1")
    a2 = s.step("ve", lambda v: v.tensor_scalar_mul(acol[:, 0:1], acol[:, 0:1], -1.0), deps=[a1], tag="m2a2")
    q8 = s.step("ve", lambda v: v.tensor_scalar(dtA[:], dtsb[:], acol[:, 0:1], None, ALU.mult),
                deps=[q7, a2], tag="m2q8")
    q9 = s.step("ve", lambda v: v.tensor_tensor_scan(Sg[:], ones16[:], dtA[:], 0.0, ALU.mult, ALU.add),
                deps=[q8, o1], tag="m2q9")
    q10 = s.step("ve", lambda v: v.tensor_scalar_mul(nSg[:], Sg[:], -1.0), deps=[q9], tag="m2q10")
    q11 = s.step("sy", lambda y: y.dma_start(out=sg_sc.ap(), in_=Sg[:]), deps=[q9], dma=True, tag="m2q11")
    q12 = s.step("ve", lambda v: v.tensor_copy(out=dafr[:], in_=dtA[:, ::128]), deps=[q8], tag="m2q12")
    q13 = s.step("sc", lambda sc: sc.activation(out=dafr[:], in_=dafr[:], func=AF.Exp), deps=[q12], tag="m2q13")
    q14 = s.step("sy", lambda y: y.dma_start(out=AP(daf_sc, 0, [[32, 16], [1, 32]]), in_=dafr[:]),
                 deps=[q13], dma=True, tag="m2q14")
    q15 = s.step("sy", lambda y: y.dma_start(out=dafbc[:], in_=AP(daf_sc, 0, [[0, 128], [1, 512]])),
                 deps=[q14], dma=True, tag="m2q15")
    prev_tr = None
    for c in range(NC_):
        ta_ = s.step("te", lambda te, c=c: [
            te.transpose(E2[0:128, 0:16], nSg[:, c * 128:(c + 1) * 128], idf[32:48, 32:48]),
            te.transpose(E2[0:128, 16:32], dtsb[:, c * 128:(c + 1) * 128], idf[0:16, 0:16])],
            deps=[q10, q7, prev_tr] + wtok, tag=f"m2tr{c}")
        tb_ = s.step("ve", lambda v, c=c: [
            v.tensor_copy(out=negSgT[:, c * 16:(c + 1) * 16], in_=E2[0:128, 0:16]),
            v.tensor_copy(out=dtT[:, c * 16:(c + 1) * 16], in_=E2[0:128, 16:32])],
            deps=[ta_], tag=f"m2te{c}")
        prev_tr = tb_
    tr_done = prev_tr

    sgbc = p.sb("m2sgbc", [128, T], F32)
    xhf = p.sb("m2xhf", [64, T], F32)
    xhb = p.sb("m2xhb", [64, T], BF16)
    Yh = p.sb("m2Yh", [64, T], F32)
    Esb = p.sb("m2Esb", [128, 128], F32)
    Gt_f = p.sb("m2Gtf", [128, 128], F32)
    Gt_b = p.sb("m2Gtb", [128, 128], BF16)
    Xs_b = p.sb("m2Xs", [128, 64], BF16)
    bc_b = p.sb("m2bcb", [128, 128], BF16)
    Cs_b = p.sb("m2Cs", [128, 128], BF16)
    Bw_b = p.sb("m2Bw", [128, 128], BF16)
    Et = p.sb("m2Et", [128, 128], F32)
    dec_c = p.sb("m2dec", [128, 1], F32)
    hst_f = p.sb("m2hstf", [128, 64], F32)
    hst_b = p.sb("m2hstb", [128, 64], BF16)

    prev_head = {}
    for h in range(16):
        lsg = s.step("sy", lambda y, h=h: y.dma_start(
            out=sgbc[:], in_=AP(sg_sc, h * T, [[0, 128], [1, T]])),
            deps=[q11, prev_head.get("sg")], dma=True, tag=f"m2sg{h}")
        lxh = s.step("sy", lambda y, h=h: y.dma_start(
            out=xhf[:], in_=xf_sc.ap()[h * 64:(h + 1) * 64, :]),
            deps=conv_toks + [prev_head.get("xh")], dma=True, tag=f"m2xh{h}")
        lxb = s.step("gp", lambda g, h=h: g.dma_start(
            out=xhb[:], in_=xbf_sc.ap()[h * 64:(h + 1) * 64, :]),
            deps=conv_toks + [prev_head.get("xb")], dma=True, tag=f"m2xb{h}")
        hz1 = s.step("ve", lambda v: v.memset(hst_f[:], 0.0), deps=[prev_head.get("hst")], tag=f"m2hz{h}")
        hz2 = s.step("ve", lambda v: v.memset(hst_b[:], 0.0), deps=[prev_head.get("hstb")], tag=f"m2hzb{h}")
        prev_c = {}
        for c in range(NC_):
            e0 = s.step("ve", lambda v, c=c, h=h: v.tensor_scalar(
                Et[:], sgbc[:, c * 128:(c + 1) * 128],
                negSgT[:, c * 16 + h: c * 16 + h + 1], 0.0, ALU.add, ALU.min),
                deps=[lsg, tr_done, prev_c.get("E")], tag=f"mE0{h}_{c}")
            e1 = s.step("sc", lambda sc: sc.activation(out=Esb[:], in_=Et[:], func=AF.Exp),
                deps=[e0], tag=f"mE{h}_{c}")
            m1 = s.step("te", lambda te, c=c: te.matmul(
                A[0:128, 0:128], Bf_bf[:, c * 128:(c + 1) * 128], Cf_bf[:, c * 128:(c + 1) * 128],
                start=True, stop=True),
                deps=conv_toks + [prev_c.get("CB")], tag=f"mCB{h}_{c}")
            g1 = s.step("ve", lambda v: v.tensor_tensor(Gt_f[:], Esb[:], A[0:128, 0:128], ALU.mult),
                        deps=[e1, m1], tag=f"mG1{h}_{c}")
            g2 = s.step("ve", lambda v: v.tensor_tensor(Gt_b[:], Gt_f[:], smalls["maskf"][:], ALU.mult),
                        deps=[g1], tag=f"mG2{h}_{c}")
            m2 = s.step("te", lambda te, c=c: te.transpose(
                Gb[:, 0:64], xhb[0:64, c * 128:(c + 1) * 128], idb[0:64, 0:64]),
                deps=[lxb, prev_c.get("Xs")] + wtok, tag=f"mXt{h}_{c}")
            x1 = s.step("ve", lambda v, c=c, h=h: v.tensor_scalar(
                Xs_b[:], Gb[:, 0:64], dtT[:, c * 16 + h: c * 16 + h + 1], None, ALU.mult),
                deps=[m2, tr_done], tag=f"mXs{h}_{c}")
            m3 = s.step("te", lambda te: te.matmul(B[0:128, 0:128], ones_f[0:1, 0:128], Esb[0:1, 0:128],
                                                   start=True, stop=True),
                        deps=[e1, prev_c.get("bc")], tag=f"mbc{h}_{c}")
            d1 = s.step("ve", lambda v: v.tensor_copy(out=dec_c[:], in_=B[0:128, 127:128]), deps=[m3], tag=f"mdc{h}_{c}")
            cb1 = s.step("ve", lambda v: v.tensor_copy(out=bc_b[:], in_=B[0:128, 0:128]), deps=[m3], tag=f"mcb{h}_{c}")
            c1_ = s.step("ve", lambda v, c=c: v.tensor_tensor(Cs_b[:], Cf_bf[:, c * 128:(c + 1) * 128],
                                                              bc_b[:], ALU.mult),
                         deps=[cb1] + conv_toks, tag=f"mCs{h}_{c}")
            def ymm(te, c=c, first=(c == 0)):
                insts = [te.matmul(C2[0:64, 0:128], Xs_b[:], Gt_b[:], start=True, stop=first)]
                if not first:
                    insts.append(te.matmul(C2[0:64, 0:128], hst_b[:], Cs_b[:], start=False, stop=True))
                return insts
            m4 = s.step("te", ymm, deps=[g2, x1, c1_, prev_c.get("Y"), prev_c.get("hstb_w")], tag=f"mY{h}_{c}")
            y1 = s.step("ve", lambda v, c=c, h=h: v.scalar_tensor_tensor(
                Yh[:, c * 128:(c + 1) * 128], xhf[:, c * 128:(c + 1) * 128],
                smalls["dbc"][:, h:h + 1], C2[0:64, 0:128], ALU.mult, ALU.add),
                deps=[m4, lxh] + ([prev_head.get("Yst")] if c == 0 else []), tag=f"mYe{h}_{c}")
            m5 = s.step("te", lambda te, c=c: te.transpose(
                Gb[:, 64:192], Bf_bf[:, c * 128:(c + 1) * 128], idb[:]),
                deps=conv_toks + [prev_c.get("Bw")] + wtok, tag=f"mBt{h}_{c}")
            b1 = s.step("ve", lambda v: v.tensor_scalar(Bw_b[:], Gb[:, 64:192], Esb[:, 127:128], None, ALU.mult),
                        deps=[m5, e1], tag=f"mBw{h}_{c}")
            m6 = s.step("te", lambda te: te.matmul(D2[0:128, 0:64], Bw_b[:], Xs_b[:], start=True, stop=True),
                        deps=[b1, x1, prev_c.get("U")], tag=f"mU{h}_{c}")
            v1 = s.step("ve", lambda v: v.scalar_tensor_tensor(hst_f[:], hst_f[:], dec_c[:, 0:1],
                                                               D2[0:128, 0:64], ALU.mult, ALU.add),
                        deps=[m6, d1, hz1], tag=f"mh1{h}_{c}")
            if c < NC_ - 1:
                v2 = s.step("ve", lambda v, c=c, h=h: v.tensor_scalar(
                    hst_f[:], hst_f[:], dafbc[:, h * 32 + c + 1: h * 32 + c + 2], None, ALU.mult),
                    deps=[v1, q15], tag=f"mh2{h}_{c}")
            else:
                v2 = v1
            v3 = s.step("ve", lambda v: v.tensor_copy(out=hst_b[:], in_=hst_f[:]), deps=[v2, hz2], tag=f"mh3{h}_{c}")
            prev_c = {"E": e1, "CB": g1, "Xs": x1, "bc": c1_, "Y": y1,
                      "Bw": b1, "U": v1, "hstb_w": v3}
        sy1 = s.step("sy", lambda y, h=h: y.dma_start(out=Y_sc.ap()[h * 64:(h + 1) * 64, :], in_=Yh[:]),
                     deps=[prev_c["Y"], prev_head.get("Yst")], dma=True, tag=f"m2Yst{h}")
        prev_head = {"sg": s.tok("sc"), "xh": s.tok("ve"), "xb": s.tok("te"),
                     "hst": v2, "hstb": v3, "Yst": sy1}
    head_done = s.tok("sy")

    ygA = [p.sb("m2ygA", [128, 512], F32), p.sb("m2ygB", [128, 512], F32)]
    zA = [p.sb("m2zA", [128, 512], F32), p.sb("m2zB", [128, 512], F32)]
    rq2 = pk2[64:65, :]
    gst_prev = {}
    mmq = None
    for tc in range(8):
        for c in range(8):
            k = c % 2
            l1 = s.step("sy", lambda y, tc=tc, c=c, k=k: y.dma_start(
                out=ygA[k][:], in_=Y_sc.ap()[c * 128:(c + 1) * 128, tc * 512:(tc + 1) * 512]),
                deps=[head_done, gst_prev.get(("y", k))], dma=True, tag=f"gy{tc}_{c}")
            l2 = s.step("gp", lambda g, tc=tc, c=c, k=k: g.dma_start(
                out=zA[k][:], in_=z_sc.ap()[c * 128:(c + 1) * 128, tc * 512:(tc + 1) * 512]),
                deps=[gst_prev.get(("z", k))], dma=True, tag=f"gz{tc}_{c}")
            s1 = s.step("sc", lambda sc, k=k: sc.activation(out=zA[k][:], in_=zA[k][:], func=AF.Silu),
                        deps=[l2], tag=f"gs{tc}_{c}")
            s2 = s.step("ve", lambda v, k=k: v.tensor_tensor(ygA[k][:], ygA[k][:], zA[k][:], ALU.mult),
                        deps=[l1, s1], tag=f"gm{tc}_{c}")
            s3 = s.step("sc", lambda sc, k=k: sc.activation(out=sqb[:], in_=ygA[k][:], func=AF.Square),
                        deps=[s2, mmq], tag=f"gq{tc}_{c}")
            mmq = s.step("te", lambda te, c=c: te.matmul(A[0:1, 0:512], ones_bf[:, 0:1], sqb[:],
                                                         start=(c == 0), stop=(c == 7)),
                         deps=[s3], tag=f"gmm{tc}_{c}")
            s4 = s.step("sy", lambda y, tc=tc, c=c, k=k: y.dma_start(
                out=yg_sc.ap()[c * 128:(c + 1) * 128, tc * 512:(tc + 1) * 512], in_=ygA[k][:]),
                deps=[s2], dma=True, tag=f"gw{tc}_{c}")
            gst_prev[("y", k)] = s2
            gst_prev[("z", k)] = s2
            gst_prev[("st", k)] = s4
        r1 = s.step("ve", lambda v, tc=tc: v.tensor_scalar(rq2[0:1, tc * 512:(tc + 1) * 512],
                                                           A[0:1, 0:512], 1.0 / 1024.0, 1e-5,
                                                           ALU.mult, ALU.add), deps=[mmq], tag=f"gr1{tc}")
        r2 = s.step("sc", lambda sc, tc=tc: sc.activation(out=rq2[0:1, tc * 512:(tc + 1) * 512],
                                                          in_=rq2[0:1, tc * 512:(tc + 1) * 512], func=AF.Sqrt),
                    deps=[r1], tag=f"gr2{tc}")
        r3 = s.step("ve", lambda v, tc=tc: v.reciprocal(rq2[0:1, tc * 512:(tc + 1) * 512],
                                                        rq2[0:1, tc * 512:(tc + 1) * 512]), deps=[r2], tag=f"gr3{tc}")
        mmq = r3
    prevn = {}
    last_n = None
    sc_all = s.tok("sc")
    for tc in range(8):
        t4 = s.step("te", lambda te, tc=tc: te.matmul(E2[:, 0:512], ones_f[64:65, 0:128],
                                                      rq2[0:1, tc * 512:(tc + 1) * 512], start=True, stop=True),
                    deps=[s.tok("ve"), prevn.get("bc")], tag=f"nbc{tc}")
        for c in range(8):
            k = c % 2
            l1 = s.step("sy", lambda y, tc=tc, c=c, k=k: y.dma_start(
                out=ygA[k][:], in_=yg_sc.ap()[c * 128:(c + 1) * 128, tc * 512:(tc + 1) * 512]),
                deps=[gst_prev.get(("st", 0)), gst_prev.get(("st", 1)), prevn.get(("y", k)), sc_all],
                dma=True, tag=f"ny{tc}_{c}")
            n1 = s.step("ve", lambda v, c=c, k=k: v.scalar_tensor_tensor(
                zA[k][:], ygA[k][:], smalls["gnw"][:, c:c + 1], E2[:, 0:512], ALU.mult, ALU.mult),
                deps=[l1, t4, prevn.get(("z", k))], tag=f"nn{tc}_{c}")
            n2 = s.step("gp", lambda g, tc=tc, c=c, k=k: g.dma_start(
                out=yh_sc.ap()[c * 128:(c + 1) * 128, tc * 512:(tc + 1) * 512], in_=zA[k][:]),
                deps=[n1], dma=True, tag=f"ns{tc}_{c}")
            prevn[("y", k)] = n1
            prevn[("z", k)] = n2
            last_n = n2
        prevn["bc"] = last_n
    yhat_done = last_n

    ybk = hc[:, 0:2048].bitcast(BF16)
    ost = [ygA[0], ygA[1]]
    prevo = {}
    for tg in range(8):
        lb = s.step("sy", lambda y, tg=tg: y.dma_start(
            out=ybk[:].rearrange("p (kc n) -> p kc n", kc=8),
            in_=AP(yh_sc, tg * 512, [[T, 128], [128 * T, 8], [1, 512]])),
            deps=[yhat_done, prevo.get("ybk")], dma=True, tag=f"fl{tg}")
        for ti in range(4):
            tt = tg * 4 + ti
            k = tt % 2
            psf = [A, B][k]
            tm = mm_acc(s, psf[0:128, 0:512],
                        [(ybk[:, kc * 512 + ti * 128: kc * 512 + (ti + 1) * 128],
                          outpwT[:, kc * 512:(kc + 1) * 512]) for kc in range(8)],
                        deps=[lb, prevo.get(("p", k))] + (wtok if tt == 0 else []), tag=f"om{tt}")
            ev = s.step("ve", lambda v, k=k, psf=psf: v.tensor_copy(out=ost[k][:], in_=psf[0:128, 0:512]),
                        deps=[tm, prevo.get(("s", k))], tag=f"oe{tt}")
            st = s.step("sy", lambda y, tt=tt, k=k: y.dma_start(
                out=out_d.ap()[tt * 128:(tt + 1) * 128, :], in_=ost[k][:]),
                deps=[ev], dma=True, tag=f"os{tt}")
            prevo[("p", k)] = ev
            prevo[("s", k)] = st
        prevo["ybk"] = s.tok("te")
    return p.finish()


_BUILDERS["mamba2"] = build_mamba2


def run_mamba2(xT, inp):
    prog = get_prog("mamba2")
    maskf = (np.arange(128)[:, None] <= np.arange(128)[None, :]).astype(np.float32)
    idb = np.eye(128, dtype=np.float32).astype(BF)
    idf = np.eye(128, dtype=np.float32)
    nw = inp["L1_attn_b_norm_w"]
    in_maps = []
    for core in range(NCORES):
        b, dr = core // 2, core % 2
        if dr == 0:
            keys = {"in_w": "L1_attn_b_f_in_w", "conv_w": "L1_attn_b_f_conv_w",
                    "conv_b": "L1_attn_b_f_conv_b", "dt_bias": "L1_attn_b_f_dt_bias",
                    "A_log": "L1_attn_b_f_A_log", "D": "L1_attn_b_f_D",
                    "gnorm_w": "L1_attn_b_f_gnorm_w", "outp_w": "L1_attn_b_f_outp_w"}
        else:
            keys = {"in_w": "L1_attn_b_b_in_w", "conv_w": "L1_attn_b_b_conv_w",
                    "conv_b": "L1_attn_b_b_conv_b", "dt_bias": "L1_attn_b_b_dt_bias",
                    "A_log": "L1_attn_b_b_A_log", "D": "L1_attn_b_b_D",
                    "gnorm_w": "L1_attn_b_b_gnorm_w", "outp_w": "L1_attn_b_b_outp_w"}
        g = lambda k: inp[keys[k]]
        xb = xT[b]
        if dr == 0:
            xp = xb
            nwp = nw
        else:
            xp = np.concatenate([xb[512:], xb[:512]], axis=0)[:, ::-1]
            nwp = np.concatenate([nw[512:], nw[:512]])
        convw = np.ascontiguousarray(g("conv_w").reshape(10, 128, 4).transpose(1, 0, 2).reshape(128, 40))
        convb = np.ascontiguousarray(g("conv_b").reshape(10, 128).T)
        in_maps.append({
            "x": np.ascontiguousarray(xp),
            "normw": np.ascontiguousarray(nwp.reshape(8, 128).T),
            "inwT": np.ascontiguousarray(g("in_w").T),
            "convw": convw, "convb": convb,
            "dtbias": g("dt_bias").reshape(16, 1),
            "alog": g("A_log").reshape(16, 1),
            "dbc": np.ascontiguousarray(np.broadcast_to(g("D").reshape(1, 16), (64, 16))).copy(),
            "gnw": np.ascontiguousarray(g("gnorm_w").reshape(8, 128).T),
            "outpwT": np.ascontiguousarray(g("outp_w").T),
            "maskf": maskf, "ident_b": idb, "ident_f": idf,
        })
    res = prog.run(in_maps)
    ctxT = []
    for b in range(4):
        outf = res[2 * b]["outh"]
        outb = res[2 * b + 1]["outh"][::-1]
        full = np.concatenate([outf, outb], axis=1)
        ctxT.append(np.ascontiguousarray(full.T))
    return ctxT


def build_ln_out():
    T = 2048
    p = Prog("ln_out")
    s = p.s
    x_d = p.din("x", [1024, T])
    o_d = p.dout("xn", [1024, T])
    ones_bf, ones_f, ctoks = emit_consts(p)
    P2 = p.ps("P2", [128, 1024], F32)
    P3 = p.ps("P3", [128, 1024], F32)
    xn = p.sb("xn_sb", [128, 8 * T], F32)
    toks = emit_ln(p, x_d, T, xn, T, ones_bf, ones_f, ctoks,
                   P2[0:1, 0:512], P2[0:1, 512:1024], P3, pfx="ln")
    for c in range(8):
        s.step("sy", lambda y, c=c: y.dma_start(out=o_d.ap()[c * 128:(c + 1) * 128, :],
                                                in_=xn[:, c * T:(c + 1) * T]),
               deps=[toks[-1]], dma=True, tag=f"st{c}")
    return p.finish()


def build_proj_res2():
    T = 2048
    p = Prog("proj_res2")
    s = p.s
    x_d = p.din("x", [1024, T])
    x2_d = p.din("x2", [1024, T])
    ctx_d = p.din("ctx", [1024, T])
    w_d = p.din("w", [1024, 1024])
    b_d = p.din("biascol", [128, 8])
    xo_d = p.dout("xo", [1024, T])

    wsb = p.sb("wsb", [128, 8 * 1024], BF16)
    cs = p.sb("cs", [128, 8 * T], BF16)
    bcol = p.sb("bcol", [128, 8], F32)
    xbuf = [p.sb("xb0", [128, 512], F32), p.sb("xb1", [128, 512], F32)]
    x2buf = [p.sb("xc0", [128, 512], F32), p.sb("xc1", [128, 512], F32)]
    obuf = [p.sb("ob0", [128, 512], F32), p.sb("ob1", [128, 512], F32)]
    psum = [p.ps("pA", [128, 512], F32), p.ps("pB", [128, 512], F32)]

    wt = []
    for c in range(8):
        wt.append(s.step("gp", lambda g, c=c: g.dma_start(out=wsb[:, c * 1024:(c + 1) * 1024],
                                                          in_=w_d.ap()[c * 128:(c + 1) * 128, :]),
                         dma=True, tag=f"w{c}"))
        wt.append(s.step("gp", lambda g, c=c: g.dma_start(out=cs[:, c * T:(c + 1) * T],
                                                          in_=ctx_d.ap()[c * 128:(c + 1) * 128, :]),
                         dma=True, tag=f"c{c}"))
    wt.append(s.step("sy", lambda y: y.dma_start(out=bcol[:], in_=b_d.ap()), dma=True, tag="b"))

    i = 0
    prev = {}
    for mc in range(8):
        for tc in range(T // 512):
            k = i % 2
            ps_, xb, x2b, ob = psum[k], xbuf[k], x2buf[k], obuf[k]
            tlx = s.step("sy", lambda y, mc=mc, tc=tc, xb=xb: y.dma_start(
                out=xb[:], in_=x_d.ap()[mc * 128:(mc + 1) * 128, tc * 512:(tc + 1) * 512]),
                deps=[prev.get(("x", k))], dma=True, tag=f"lx{mc}_{tc}")
            tlx2 = s.step("gp", lambda g, mc=mc, tc=tc, x2b=x2b: g.dma_start(
                out=x2b[:], in_=x2_d.ap()[mc * 128:(mc + 1) * 128, tc * 512:(tc + 1) * 512]),
                deps=[prev.get(("x2", k))], dma=True, tag=f"l2{mc}_{tc}")
            tm = mm_acc(s, ps_[:],
                        [(wsb[:, kc * 1024 + mc * 128: kc * 1024 + (mc + 1) * 128],
                          cs[:, kc * T + tc * 512: kc * T + (tc + 1) * 512]) for kc in range(8)],
                        deps=(wt if i == 0 else []) + [prev.get(("p", k))], tag=f"mm{mc}_{tc}")
            ta = s.step("ve", lambda v, xb=xb, x2b=x2b, ob=ob: v.tensor_add(ob[:], xb[:], x2b[:]),
                        deps=[tlx, tlx2, prev.get(("o", k))], tag=f"ta{mc}_{tc}")
            tev = s.step("ve", lambda v, mc=mc, ps_=ps_, ob=ob: v.scalar_tensor_tensor(
                ob[:], ps_[:], bcol[:, mc:mc + 1], ob[:], ALU.add, ALU.add),
                deps=[tm, ta] + (wt if i == 0 else []), tag=f"ev{mc}_{tc}")
            tst = s.step("sy", lambda y, mc=mc, tc=tc, ob=ob: y.dma_start(
                out=xo_d.ap()[mc * 128:(mc + 1) * 128, tc * 512:(tc + 1) * 512], in_=ob[:]),
                deps=[tev], dma=True, tag=f"st{mc}_{tc}")
            prev[("x", k)] = ta
            prev[("x2", k)] = ta
            prev[("p", k)] = tev
            prev[("o", k)] = tst
            i += 1
    return p.finish()


_BUILDERS["ln_out"] = build_ln_out
_BUILDERS["proj_res2"] = build_proj_res2


def run_ln_out(xT):
    prog = get_prog("ln_out")
    in_maps = []
    for core in range(NCORES):
        b, half = core // 2, core % 2
        tsl = slice(half * 2048, (half + 1) * 2048)
        in_maps.append({"x": np.ascontiguousarray(xT[b][:, tsl])})
    res = prog.run(in_maps)
    return [np.concatenate([res[2 * b]["xn"], res[2 * b + 1]["xn"]], axis=1) for b in range(4)]


def run_proj_res2(xT, x2T, ctxT, W, bvec):
    prog = get_prog("proj_res2")
    bcol = np.ascontiguousarray(bvec.reshape(8, 128).T).astype(np.float32)
    in_maps = []
    for core in range(NCORES):
        b, half = core // 2, core % 2
        tsl = slice(half * 2048, (half + 1) * 2048)
        in_maps.append({
            "x": np.ascontiguousarray(xT[b][:, tsl]),
            "x2": np.ascontiguousarray(x2T[b][:, tsl]),
            "ctx": np.ascontiguousarray(ctxT[b][:, tsl]),
            "w": np.ascontiguousarray(W), "biascol": bcol,
        })
    res = prog.run(in_maps)
    return [np.concatenate([res[2 * b]["xo"], res[2 * b + 1]["xo"]], axis=1) for b in range(4)]



def build_l1f_attn():
    T = 4096
    p = Prog("l1f_attn")
    s = p.s
    x_d = p.din("x", [1024, T])
    wqk_d = p.din("wqk", [1024, 1024])
    wv_d = p.din("wv", [1024, 512])
    qs_d = p.din("qs", [128, 2])
    csq_d = p.din("csq", [128, T])
    ssq_d = p.din("ssq", [128, T])
    csk_d = p.din("csk", [128, T])
    ssk_d = p.din("ssk", [128, T])
    mks_d = p.din("mks", [128, 2304])
    idb_d = p.din("ident_b", [128, 128], BF16)
    e2c_d = p.din("e2col", [128, 2], BF16)
    on2_d = p.din("ones2", [2, 128])
    swp_d = p.din("swp", [128, 128], BF16)
    ctx_d = p.dout("ctx", [512, T], BF16)

    Q2 = p.ps("Q2", [128, 512], F32)
    K2 = p.ps("K2", [128, 512], F32)
    An = p.ps("An", [128, 512], F32)
    B2 = p.ps("B2", [128, 512], F32)
    S2 = p.ps("S2", [128, 1024], F32)
    Gb = p.ps("Gb", [128, 1024], BF16)
    C2 = p.ps("C2", [128, 512], F32)

    xsh = p.sb("xsh", [128, 8 * T], BF16)
    xt2 = [p.sb("xtA", [128, 512], F32), p.sb("xtB", [128, 512], F32)]
    prev_x = {}
    for c in range(8):
        for t in range(8):
            k = (c * 8 + t) % 2
            ld = s.step("sy", lambda y, c=c, t=t, k=k: y.dma_start(
                out=xt2[k][:], in_=x_d.ap()[c * 128:(c + 1) * 128, t * 512:(t + 1) * 512]),
                deps=[prev_x.get(k)], dma=True, tag=f"shl{c}_{t}")
            if c < 4:
                tk = s.step("ve", lambda v, c=c, t=t, k=k: v.tensor_copy(
                    out=xsh[:, c * T + t * 512: c * T + (t + 1) * 512], in_=xt2[k][:]),
                    deps=[ld], tag=f"shc{c}_{t}")
            elif t < 7:
                tk = s.step("ve", lambda v, c=c, t=t, k=k: v.tensor_copy(
                    out=xsh[:, c * T + t * 512 + 1: c * T + (t + 1) * 512 + 1], in_=xt2[k][:]),
                    deps=[ld], tag=f"shc{c}_{t}")
            else:
                tk = s.step("ve", lambda v, c=c, t=t, k=k: v.tensor_copy(
                    out=xsh[:, c * T + t * 512 + 1: (c + 1) * T], in_=xt2[k][:, 0:511]),
                    deps=[ld], tag=f"shc{c}_{t}")
            prev_x[k] = tk
    for c in range(4, 8):
        s.step("ve", lambda v, c=c: v.memset(xsh[:, c * T: c * T + 1], 0.0), tag=f"shz{c}")
    xsh_done = s.tok("ve")

    wqk = p.sb("wqk_sb", [128, 8 * 1024], BF16)
    wv = p.sb("wv_sb", [128, 8 * 512], BF16)
    wtok = [s.step("gp", lambda g: g.dma_start(
        out=wqk[:], in_=AP(wqk_d, 0, [[1024, 128], [128 * 1024, 8], [1, 1024]])), dma=True, tag="wqk"),
        s.step("gp", lambda g: g.dma_start(
            out=wv[:], in_=AP(wv_d, 0, [[512, 128], [128 * 512, 8], [1, 512]])), dma=True, tag="wv")]
    qs_sb = p.sb("qs_sb", [128, 2], F32)
    csq = p.sb("csq_sb", [128, T], BF16)
    ssq = p.sb("ssq_sb", [128, T], BF16)
    csk = p.sb("csk_sb", [128, T], BF16)
    ssk = p.sb("ssk_sb", [128, T], BF16)
    mks = p.sb("mks_sb", [128, 2304], F32)
    idb = p.sb("idb_l", [128, 128], BF16)
    e2c = p.sb("e2c_l", [128, 2], BF16)
    on2 = p.sb("on2_l", [2, 128], F32)
    swp = p.sb("swp_l", [128, 128], BF16)
    for nm, sb_, dd in (("cq", csq, csq_d), ("sq", ssq, ssq_d),
                        ("ck", csk, csk_d), ("sk", ssk, ssk_d)):
        wtok.append(s.step("gp", lambda g, sb_=sb_, dd=dd: g.dma_start(out=sb_[:], in_=dd.ap()),
                           dma=True, tag=f"c{nm}"))
    for nm, sb_, dd in (("qs", qs_sb, qs_d), ("mk", mks, mks_d), ("ib", idb, idb_d),
                        ("e2", e2c, e2c_d), ("o2", on2, on2_d), ("sw", swp, swp_d)):
        wtok.append(s.step("sy", lambda y, sb_=sb_, dd=dd: y.dma_start(out=sb_[:], in_=dd.ap()),
                           dma=True, tag=f"c{nm}"))

    qfb = [p.sb("qf0", [128, 512], F32), p.sb("qf1", [128, 512], F32)]
    kfb = [p.sb("kf0", [128, 512], F32), p.sb("kf1", [128, 512], F32)]
    sqtq = xt2[0].bitcast(BF16)[:, 0:512]
    sqtk = xt2[1].bitcast(BF16)[:, 0:512]
    nr = p.sb("nr_l", [2, 1024], F32)
    swbq = p.sb("swbq", [128, 512], BF16)
    swbk = p.sb("swbk", [128, 512], BF16)
    qnbq = p.sb("qnbq", [128, 512], BF16)
    qnbk = p.sb("qnbk", [128, 512], BF16)
    tmpq = p.sb("tmpq", [128, 512], BF16)
    tmpk = p.sb("tmpk", [128, 512], BF16)
    qp2 = p.sb("qp2_l", [128, T], BF16)
    kp2 = p.sb("kp2_l", [128, T], BF16)
    vt = p.sb("vt_l", [128, 32 * 130], BF16)
    sm2 = p.sb("sm2_l", [128, 1024], F32)
    pbf = p.sb("pbf_l", [128, 1024], BF16)
    ptb = p.sb("ptb_l", [128, 768], BF16)
    mcol = p.sb("mcol_l", [128, 4], F32)
    drow = nr
    ctmp2 = p.sb("ctmp2", [128, 512], F32)
    ctx_sb = p.sb("ctx_sb_l", [128, 512], BF16)
    qs8 = p.sb("qs8_l", [128, 1], F32)

    def memset_vt(v):
        i = None
        for j in range(32):
            i = v.memset(vt[:, j * 130 + 64: j * 130 + 65], 1.0)
            i = v.memset(vt[:, j * 130 + 129: j * 130 + 130], 1.0)
        return i
    t_vones = s.step("ve", memset_vt, tag="vt_ones")
    t_qs8 = s.step("ve", lambda v: v.tensor_scalar_mul(qs8[:], qs_sb[:, 0:1], 8.0),
                   deps=wtok, tag="qs8")

    pair_gate = ()
    pv = {}
    fin_dep = None
    t1_prev = None
    t4_prev = None
    tt_prev = None
    tev_prev = None
    f4_prev = None
    tst_prev = None
    for g in range(4):
        d_all = []
        for t in range(8):
            tb = t % 2
            base = ([xsh_done, t_qs8] + wtok) if (t == 0 and g == 0) else []
            gate = list(pair_gate) if t == 0 else []
            mmQ = mm_acc(s, Q2[0:128, 0:512],
                         [(wqk[:, kc * 1024 + g * 256: kc * 1024 + g * 256 + 128],
                           xsh[:, kc * T + t * 512: kc * T + (t + 1) * 512]) for kc in range(8)],
                         deps=base + gate + [pv.get("sqQ"), pv.get("evQ")], tag=f"pj{g}q{t}")
            mmK = mm_acc(s, K2[0:128, 0:512],
                         [(wqk[:, kc * 1024 + g * 256 + 128: kc * 1024 + g * 256 + 256],
                           xsh[:, kc * T + t * 512: kc * T + (t + 1) * 512]) for kc in range(8)],
                         deps=base + gate + [pv.get("sqK"), pv.get("evK")], tag=f"pj{g}k{t}")
            sqQ = s.step("sc", lambda sc: sc.activation(out=sqtq, in_=Q2[0:128, 0:512], func=AF.Square),
                         deps=[mmQ, pv.get("mmNq")], tag=f"sq{g}q{t}")
            evQ = s.step("sc", lambda sc, tb=tb: sc.activation(out=qfb[tb][:], in_=Q2[0:128, 0:512],
                                                               func=AF.Copy),
                         deps=[mmQ, pv.get(("sttQ", tb))], tag=f"ev{g}q{t}")
            sqK = s.step("sc", lambda sc: sc.activation(out=sqtk, in_=K2[0:128, 0:512], func=AF.Square),
                         deps=[mmK, pv.get("mmNk")], tag=f"sq{g}k{t}")
            evK = s.step("sc", lambda sc, tb=tb: sc.activation(out=kfb[tb][:], in_=K2[0:128, 0:512],
                                                               func=AF.Copy),
                         deps=[mmK, pv.get(("sttK", tb))], tag=f"ev{g}k{t}")
            mmNq = s.step("te", lambda te: te.matmul(An[0:2, 0:512], e2c[:, 0:2], sqtq,
                                                     start=True, stop=True),
                          deps=[sqQ, pv.get("sttK")], tag=f"nm{g}q{t}")
            mmNk = s.step("te", lambda te: te.matmul(An[64:66, 0:512], e2c[:, 0:2], sqtk,
                                                     start=True, stop=True),
                          deps=[sqK], tag=f"nm{g}k{t}")
            mxQ = s.step("ve", lambda v: v.tensor_scalar_max(nr[0:2, 0:512], An[0:2, 0:512], 1e-24),
                         deps=[mmNq, pv.get("bcQ")], tag=f"mx{g}q{t}")
            mxK = s.step("ve", lambda v: v.tensor_scalar_max(nr[0:2, 512:1024], An[64:66, 0:512], 1e-24),
                         deps=[mmNk, pv.get("bcK")], tag=f"mx{g}k{t}")
            rsQ = s.step("sc", lambda sc: sc.activation(out=nr[0:2, 0:512], in_=nr[0:2, 0:512],
                                                        func=AF.Sqrt), deps=[mxQ], tag=f"rs{g}q{t}")
            rsK = s.step("sc", lambda sc: sc.activation(out=nr[0:2, 512:1024], in_=nr[0:2, 512:1024],
                                                        func=AF.Sqrt), deps=[mxK], tag=f"rs{g}k{t}")
            rcQ = s.step("ve", lambda v: v.reciprocal(nr[0:2, 0:512], nr[0:2, 0:512]),
                         deps=[rsQ], tag=f"rc{g}q{t}")
            rcK = s.step("ve", lambda v: v.reciprocal(nr[0:2, 512:1024], nr[0:2, 512:1024]),
                         deps=[rsK], tag=f"rc{g}k{t}")
            bcQ = s.step("te", lambda te: te.matmul(B2[0:128, 0:512], on2[0:2, 0:128], nr[0:2, 0:512],
                                                    start=True, stop=True),
                         deps=[rcQ, pv.get("sttQ")], tag=f"bc{g}q{t}")
            bcK = s.step("te", lambda te: te.matmul(An[0:128, 0:512], on2[0:2, 0:128], nr[0:2, 512:1024],
                                                    start=True, stop=True),
                         deps=[rcK, mxQ, pv.get("sttK")], tag=f"bc{g}k{t}")
            sttQ = s.step("ve", lambda v, tb=tb: v.scalar_tensor_tensor(
                qnbq[:], qfb[tb][:], qs8[:, 0:1], B2[0:128, 0:512], ALU.mult, ALU.mult),
                deps=[bcQ, evQ, pv.get("d1q"), pv.get("swQ")], tag=f"st{g}q{t}")
            sttK = s.step("ve", lambda v, tb=tb: v.scalar_tensor_tensor(
                qnbk[:], kfb[tb][:], qs_sb[:, 1:2], An[0:128, 0:512], ALU.mult, ALU.mult),
                deps=[bcK, evK, pv.get("d1k"), pv.get("swK")], tag=f"st{g}k{t}")
            swQ = s.step("te", lambda te: te.matmul(S2[0:128, 0:512], swp[:], qnbq[:],
                                                    start=True, stop=True),
                         deps=[sttQ, pv.get("evSQ")], tag=f"sw{g}q{t}")
            swK = s.step("te", lambda te: te.matmul(S2[0:128, 512:1024], swp[:], qnbk[:],
                                                    start=True, stop=True),
                         deps=[sttK, pv.get("evSK")], tag=f"sw{g}k{t}")
            evSQ = s.step("sc", lambda sc: sc.activation(out=swbq[:], in_=S2[0:128, 0:512],
                                                         func=AF.Copy),
                          deps=[swQ, pv.get("d2q")], tag=f"es{g}q{t}")
            evSK = s.step("sc", lambda sc: sc.activation(out=swbk[:], in_=S2[0:128, 512:1024],
                                                         func=AF.Copy),
                          deps=[swK, pv.get("d2k")], tag=f"es{g}k{t}")
            d1q = s.step("ve", lambda v, t=t: v.tensor_tensor(
                qp2[:, t * 512:(t + 1) * 512], qnbq[:], csq[:, t * 512:(t + 1) * 512], ALU.mult),
                deps=[sttQ], tag=f"d1{g}q{t}")
            d2q = s.step("ve", lambda v, t=t: v.tensor_tensor(
                tmpq[:], swbq[:], ssq[:, t * 512:(t + 1) * 512], ALU.mult),
                deps=[evSQ], tag=f"d2{g}q{t}")
            d3q = s.step("ve", lambda v, t=t: v.tensor_tensor(
                qp2[:, t * 512:(t + 1) * 512], qp2[:, t * 512:(t + 1) * 512], tmpq[:], ALU.add),
                deps=[d1q, d2q], tag=f"d3{g}q{t}")
            d1k = s.step("ve", lambda v, t=t: v.tensor_tensor(
                kp2[:, t * 512:(t + 1) * 512], qnbk[:], csk[:, t * 512:(t + 1) * 512], ALU.mult),
                deps=[sttK], tag=f"d1{g}k{t}")
            d2k = s.step("ve", lambda v, t=t: v.tensor_tensor(
                tmpk[:], swbk[:], ssk[:, t * 512:(t + 1) * 512], ALU.mult),
                deps=[evSK], tag=f"d2{g}k{t}")
            d3k = s.step("ve", lambda v, t=t: v.tensor_tensor(
                kp2[:, t * 512:(t + 1) * 512], kp2[:, t * 512:(t + 1) * 512], tmpk[:], ALU.add),
                deps=[d1k, d2k], tag=f"d3{g}k{t}")
            d_all += [d3q, d3k]
            pv.update({"sqQ": sqQ, "evQ": evQ, "sqK": sqK, "evK": evK,
                       "mmNq": mmNq, "mmNk": mmNk, "bcQ": bcQ, "bcK": bcK,
                       "sttQ": sttQ, "sttK": sttK, ("sttQ", tb): sttQ, ("sttK", tb): sttK,
                       "swQ": swQ, "swK": swK, "evSQ": evSQ, "evSK": evSK,
                       "d1q": d1q, "d2q": d2q, "d1k": d1k, "d2k": d2k})
        prep_q2_free = [pv["sqQ"], pv["evQ"]]
        prep_k2_free = [pv["sqK"], pv["evK"]]

        ev_prev = None
        for tt in range(32):
            dep = prep_q2_free if tt == 0 else [ev_prev]
            mmV = mm_acc(s, Q2[0:128, 0:128],
                         [(xsh[:, kc * T + tt * 128: kc * T + (tt + 1) * 128],
                           wv[:, kc * 512 + g * 128: kc * 512 + (g + 1) * 128]) for kc in range(8)],
                         deps=dep + ([t_vones] + list(pair_gate) if tt == 0 else []), tag=f"lv{g}_{tt}")
            ev_prev = s.step("ve", lambda v, tt=tt: v.tensor_copy(
                out=vt[:, tt * 130: tt * 130 + 130].rearrange("p (b c) -> p b c", b=2)[:, :, 0:64],
                in_=Q2[0:128, 0:128].rearrange("p (b c) -> p b c", b=2)),
                deps=[mmV], tag=f"lve{g}_{tt}")
        v_done = ev_prev
        pv["sqQ"] = pv["evQ"] = v_done

        for qt in range(32):
            q0 = qt * 128
            kstart = max(q0 - 256, 0)
            kw = q0 + 128 - kstart
            nkt = kw // 128
            mi = 0 if qt >= 2 else (1 if qt == 0 else 2)
            moff = (0, 896, 1536)[mi]
            oc = (qt % 4) * 128
            tSq = s.step("te", lambda te, qt=qt, kstart=kstart, kw=kw: te.matmul(
                S2[0:128, 0:kw], qp2[0:64, qt * 128:(qt + 1) * 128], kp2[0:64, kstart:kstart + kw],
                start=True, stop=True),
                deps=(d_all if qt == 0 else []) + [t1_prev, pv.get("evSQ") if qt == 0 else None],
                tag=f"S{g}q{qt}")
            tSk = s.step("te", lambda te, qt=qt, kstart=kstart, kw=kw: te.matmul(
                S2[0:128, 512:512 + kw], qp2[64:128, qt * 128:(qt + 1) * 128],
                kp2[64:128, kstart:kstart + kw], start=True, stop=True),
                deps=[t1_prev, pv.get("evSK") if qt == 0 else None], tag=f"Sk{g}q{qt}")
            t1 = s.step("ve", lambda v, kw=kw, moff=moff: v.tensor_tensor(
                sm2[:, 0:512 + kw], S2[0:128, 0:512 + kw], mks[:, moff: moff + 512 + kw], ALU.add),
                deps=[tSq, tSk, t4_prev], tag=f"M{g}q{qt}")
            t1_prev = t1
            t2a = s.step("ve", lambda v, kw=kw: v.reduce_max(mcol[:, 0:1], sm2[:, 0:kw], axis=AX.X),
                         deps=[t1], tag=f"Ra{g}q{qt}")
            t2b = s.step("ve", lambda v, kw=kw: v.reduce_max(mcol[:, 1:2], sm2[:, 512:512 + kw],
                                                             axis=AX.X),
                         deps=[t1], tag=f"Rb{g}q{qt}")
            t3 = s.step("ve", lambda v: v.tensor_scalar_mul(mcol[:, 2:4], mcol[:, 0:2], -1.0),
                        deps=[t2a, t2b], tag=f"N{g}q{qt}")
            t4a = s.step("sc", lambda sc, kw=kw: sc.activation(out=pbf[:, 0:kw], in_=sm2[:, 0:kw],
                                                               func=AF.Exp, bias=mcol[:, 2:3]),
                         deps=[t3, tt_prev], tag=f"Ea{g}q{qt}")
            t4b = s.step("sc", lambda sc, kw=kw: sc.activation(out=pbf[:, 512:512 + kw],
                                                               in_=sm2[:, 512:512 + kw],
                                                               func=AF.Exp, bias=mcol[:, 3:4]),
                         deps=[t3, tt_prev], tag=f"Eb{g}q{qt}")
            t4_prev = t4b
            def tr_emit(te, nkt=nkt):
                insts = []
                for j in range(nkt):
                    insts.append(te.transpose(Gb[:, j * 128:(j + 1) * 128],
                                              pbf[:, j * 128:(j + 1) * 128], idb[:]))
                    insts.append(te.transpose(Gb[:, 512 + j * 128: 512 + (j + 1) * 128],
                                              pbf[:, 512 + j * 128: 512 + (j + 1) * 128], idb[:]))
                return insts
            tt_ = s.step("te", tr_emit, deps=[t4a, t4b, tev_prev], tag=f"T{g}q{qt}")
            tt_prev = tt_
            tev = s.step("ve", lambda v, nkt=nkt: v.tensor_copy(
                out=ptb[:].rearrange("p (b c) -> p b c", b=2)[:, :, 0:nkt * 128],
                in_=Gb[:].rearrange("p (b c) -> p b c", b=2)[:, :, 0:nkt * 128]),
                deps=[tt_], tag=f"TE{g}q{qt}")
            tev_prev = tev
            def pv_q(te, kstart=kstart, nkt=nkt, oc=oc):
                return [te.matmul(C2[0:65, oc:oc + 128],
                                  vt[:, (kstart // 128 + j) * 130: (kstart // 128 + j) * 130 + 65],
                                  ptb[:, j * 128:(j + 1) * 128],
                                  start=(j == 0), stop=(j == nkt - 1)) for j in range(nkt)]
            def pv_k(te, kstart=kstart, nkt=nkt, oc=oc):
                return [te.matmul(K2[0:65, oc:oc + 128],
                                  vt[:, (kstart // 128 + j) * 130 + 65: (kstart // 128 + j) * 130 + 130],
                                  ptb[:, 384 + j * 128: 384 + (j + 1) * 128],
                                  start=(j == 0), stop=(j == nkt - 1)) for j in range(nkt)]
            tPVq = s.step("te", pv_q, deps=[tev, v_done, fin_dep], tag=f"PVq{g}_{qt}")
            tPVk = s.step("te", pv_k,
                          deps=[tev, v_done, fin_dep] + (prep_k2_free if qt < 4 else []),
                          tag=f"PVk{g}_{qt}")
            pv[("pvq", qt % 4)] = tPVq
            pv[("pvk", qt % 4)] = tPVk
            if qt % 4 == 3:
                qb = qt - 3
                fd1 = s.step("ve", lambda v: v.tensor_copy(out=drow[0:1, 0:512],
                                                           in_=C2[64:65, 0:512]),
                             deps=[pv[("pvq", j)] for j in range(4)] + [pv.get("fb")],
                             tag=f"fd1{g}_{qt}")
                fd2 = s.step("ve", lambda v: v.tensor_copy(out=drow[0:1, 512:1024],
                                                           in_=K2[64:65, 0:512]),
                             deps=[pv[("pvk", j)] for j in range(4)], tag=f"fd2{g}_{qt}")
                fr = s.step("ve", lambda v: v.reciprocal(drow[0:1, 0:1024], drow[0:1, 0:1024]),
                            deps=[fd1, fd2], tag=f"fr{g}_{qt}")
                fb = s.step("te", lambda te: [
                    te.matmul(B2[0:64, 0:512], on2[0:1, 0:64], drow[0:1, 0:512],
                              start=True, stop=True),
                    te.matmul(B2[64:128, 0:512], on2[0:1, 0:64], drow[0:1, 512:1024],
                              start=True, stop=True)],
                    deps=[fr, f4_prev], tag=f"fb{g}_{qt}")
                pv["fb"] = fb
                fc1 = s.step("sc", lambda sc: sc.activation(out=ctmp2[0:64, 0:512],
                                                            in_=C2[0:64, 0:512], func=AF.Copy),
                             deps=[fd1, fd2], tag=f"fc1{g}_{qt}")
                fc2 = s.step("sc", lambda sc: sc.activation(out=ctmp2[64:128, 0:512],
                                                            in_=K2[0:64, 0:512], func=AF.Copy),
                             deps=[fd1, fd2], tag=f"fc2{g}_{qt}")
                f4 = s.step("ve", lambda v: v.tensor_tensor(
                    out=ctx_sb[:], in0=ctmp2[:], in1=B2[0:128, 0:512], op=ALU.mult),
                    deps=[fb, fc1, fc2, tst_prev], tag=f"f4{g}_{qt}")
                f4_prev = f4
                tst = s.step("sy", lambda y, g=g, qb=qb: y.dma_start(
                    out=ctx_d.ap()[g * 128:(g + 1) * 128, qb * 128:(qb + 4) * 128], in_=ctx_sb[:]),
                    deps=[f4], dma=True, tag=f"cx{g}_{qt}")
                tst_prev = tst
                fin_dep = f4
        pair_gate = (s.tok("ve"), s.tok("sc"))
    return p.finish()


def build_l1f_ffn():
    T = 2048
    TH = T + 1
    p = Prog("l1f_ffn")
    s = p.s
    x_d = p.din("x", [1024, TH])
    ctx_d = p.din("ctx", [1024, TH], BF16)
    wo_d = p.din("wo", [1024, 1024])
    lng_d = p.din("lng", [128, 8])
    lnb_d = p.din("lnb", [128, 8])
    w1_d = p.din("w1", [1024, 5632])
    b1a_d = p.din("b1a", [128, 22])
    b1g_d = p.din("b1g", [128, 22])
    w2_d = p.din("w2", [2816, 1024])
    b2_d = p.din("b2col", [128, 8])
    xo_d = p.dout("xo", [1024, T])
    x1_sc = p.dtmp("x1_sc", [1024, TH])

    ones_bf, ones_f, ctoks = emit_consts(p)
    A = p.ps("A", [128, 512], F32)
    B = p.ps("B", [128, 512], F32)
    A2 = p.ps("Ax", [128, 512], F32)
    B2 = p.ps("Bx", [128, 512], F32)
    An = [A, A2]
    Bn = [B, B2]
    E2 = p.ps("E2", [128, 1024], F32)
    P2 = p.ps("P2", [128, 1024], F32)

    wosb = p.sb("wosb", [128, 8 * 1024], BF16)
    ctxb = [p.sb("fcxA", [128, 8 * 544], BF16), p.sb("fcxB", [128, 8 * 544], BF16)]
    smalls = {}
    wtok = [s.step("gp", lambda g: g.dma_start(
        out=wosb[:], in_=AP(wo_d, 0, [[1024, 128], [128 * 1024, 8], [1, 1024]])), dma=True, tag="wo")]
    for nm, dd, shp in (("lng", lng_d, [128, 8]), ("lnb", lnb_d, [128, 8]),
                        ("b1a", b1a_d, [128, 22]), ("b1g", b1g_d, [128, 22]),
                        ("b2", b2_d, [128, 8])):
        sb_ = p.sb(nm + "_sb", shp, F32)
        smalls[nm] = sb_
        wtok.append(s.step("sy", lambda y, sb_=sb_, dd=dd: y.dma_start(out=sb_[:], in_=dd.ap()),
                           dma=True, tag=f"c{nm}"))

    xbuf = [p.sb("fxA", [128, 544], F32), p.sb("fxB", [128, 544], F32)]
    o1b = [p.sb("foA1", [128, 544], F32), p.sb("foB1", [128, 544], F32)]
    prev = {}
    tcs = [(0, 513), (513, 512), (1025, 512), (1537, 512)]
    x1_last = None
    for tci, (c0, n) in enumerate(tcs):
        k = tci % 2
        lc = s.step("gp", lambda g, c0=c0, n=n, k=k: g.dma_start(
            out=ctxb[k][:, 0:8 * n].rearrange("p (c m) -> p c m", c=8),
            in_=AP(ctx_d, c0, [[TH, 128], [128 * TH, 8], [1, n]])),
            deps=[prev.get(("c", k))], dma=True, tag=f"p1c{tci}")
        for mc in range(8):
            kk2 = (tci * 8 + mc) % 2
            tlx = s.step("sy", lambda y, mc=mc, c0=c0, n=n, k=k: y.dma_start(
                out=xbuf[k][:, 0:n], in_=x_d.ap()[mc * 128:(mc + 1) * 128, c0:c0 + n]),
                deps=[prev.get(("x", k))], dma=True, tag=f"p1x{tci}_{mc}")
            def mme(te, mc=mc, n=n, k=k, kk2=kk2):
                insts = []
                for off, nn in _pieces(0, n):
                    ps_ = An[kk2] if off == 0 else Bn[kk2]
                    for kc in range(8):
                        insts.append(te.matmul(ps_[0:128, 0:nn],
                                               wosb[:, kc * 1024 + mc * 128: kc * 1024 + (mc + 1) * 128],
                                               ctxb[k][:, kc * n + off: kc * n + off + nn],
                                               start=(kc == 0), stop=(kc == 7)))
                return insts
            tm = s.step("te", mme, deps=[lc, prev.get(("p", kk2))] + (wtok if tci == 0 and mc == 0 else []),
                        tag=f"p1m{tci}_{mc}")
            def eve(v, mc=mc, n=n, k=k, kk2=kk2):
                insts = [v.tensor_tensor(o1b[k][:, 0:min(n, 512)],
                                         xbuf[k][:, 0:min(n, 512)], An[kk2][0:128, 0:min(n, 512)], ALU.add)]
                if n == 513:
                    insts.append(v.tensor_tensor(o1b[k][:, 512:513],
                                                 xbuf[k][:, 512:513], Bn[kk2][0:128, 0:1], ALU.add))
                return insts
            tev = s.step("ve", eve, deps=[tm, tlx, prev.get(("e", k))], tag=f"p1e{tci}_{mc}")
            tst = s.step("sy", lambda y, mc=mc, c0=c0, n=n, k=k: y.dma_start(
                out=x1_sc.ap()[mc * 128:(mc + 1) * 128, c0:c0 + n], in_=o1b[k][:, 0:n]),
                deps=[tev], dma=True, tag=f"p1s{tci}_{mc}")
            prev[("x", k)] = tev
            prev[("p", kk2)] = tev
            prev[("e", k)] = tst
            x1_last = tst
        prev[("c", k)] = s.tok("te")
    x1_done = x1_last

    xn = p.sb("xnf", [128, 8 * T], BF16)
    xg = p.sb("fxg", [128, 8 * 512], F32)
    cast_bf = p.sb("fcast", [128, 512], BF16)
    sq_bf = p.sb("fsq", [128, 512], BF16)
    sm_tmp = p.sb("fsm", [128, 512], F32)
    row_v = p.sb("frv", [1, 512], F32)
    row_r = p.sb("frr", [1, 512], F32)
    row_m = p.sb("frm", [1, 512], F32)

    mm_last_m = mm_last_q = None
    row_free = None
    bc_free = None
    grp_free = None
    for t in range(4):
        ld = []
        for c in range(8):
            off = (1 + t * 512) if c < 4 else (t * 512)
            ld.append(s.step("sy", lambda y, c=c, off=off: y.dma_start(
                out=xg[:, c * 512:(c + 1) * 512], in_=x1_sc.ap()[c * 128:(c + 1) * 128, off:off + 512]),
                deps=[x1_done if t == 0 and c == 0 else None,
                      grp_free if c == 0 else None], dma=True, tag=f"l2d{t}_{c}"))
        for c in range(8):
            tcast = s.step("ve", lambda v, c=c: v.tensor_copy(out=cast_bf[:], in_=xg[:, c * 512:(c + 1) * 512]),
                           deps=[ld[c], mm_last_m] + list(ctoks), tag=f"lca{t}_{c}")
            tsq = s.step("sc", lambda sc, c=c: sc.activation(out=sq_bf[:], in_=xg[:, c * 512:(c + 1) * 512],
                                                             func=AF.Square),
                         deps=[ld[c], mm_last_q], tag=f"lsq{t}_{c}")
            mm_last_m = s.step("te", lambda te, c=c: te.matmul(
                A[0:1, 0:512], ones_bf[:, 0:1], cast_bf[:], start=(c == 0), stop=(c == 7)),
                deps=[tcast, row_free if c == 0 else None], tag=f"lnm{t}_{c}")
            mm_last_q = s.step("te", lambda te, c=c: te.matmul(
                B[0:1, 0:512], ones_bf[:, 0:1], sq_bf[:], start=(c == 0), stop=(c == 7)),
                deps=[tsq, row_free if c == 0 else None], tag=f"lnq{t}_{c}")
        t1 = s.step("ve", lambda v: v.tensor_scalar_mul(row_m[:], A[0:1, 0:512], 1.0 / 1024.0),
                    deps=[mm_last_m, row_free], tag=f"ls1{t}")
        t2 = s.step("ve", lambda v: v.tensor_mul(row_v[:], row_m[:], row_m[:]), deps=[t1], tag=f"ls2{t}")
        t3 = s.step("ve", lambda v: v.scalar_tensor_tensor(row_v[:], B[0:1, 0:512], 1.0 / 1024.0, row_v[:],
                                                           ALU.mult, ALU.subtract),
                    deps=[t2, mm_last_q], tag=f"ls3{t}")
        t4 = s.step("ve", lambda v: v.tensor_scalar_add(row_v[:], row_v[:], 1e-5), deps=[t3], tag=f"ls4{t}")
        t5 = s.step("sc", lambda sc: sc.activation(out=row_r[:], in_=row_v[:], func=AF.Sqrt),
                    deps=[t4, bc_free], tag=f"ls5{t}")
        t6 = s.step("ve", lambda v: v.reciprocal(row_r[:], row_r[:]), deps=[t5], tag=f"ls6{t}")
        t7 = s.step("ve", lambda v: v.tensor_mul(row_m[:], row_m[:], row_r[:]), deps=[t6], tag=f"ls7{t}")
        tb = s.step("te", lambda te: [
            te.matmul(E2[:, 0:512], ones_f[0:1, 0:128], row_r[:], start=True, stop=True),
            te.matmul(E2[:, 512:1024], ones_f[0:1, 0:128], row_m[:], start=True, stop=True)],
            deps=[t6, t7, bc_free], tag=f"lbc{t}")
        row_free = tb
        toks = []
        for c in range(8):
            ta = s.step("ve", lambda v, c=c: v.tensor_tensor(
                out=sm_tmp[:], in0=xg[:, c * 512:(c + 1) * 512], in1=E2[:, 512:1024], op=ALU.subtract),
                deps=[tb], tag=f"lxa{t}_{c}")
            tbb = s.step("ve", lambda v: v.tensor_tensor(
                out=sm_tmp[:], in0=sm_tmp[:], in1=E2[:, 0:512], op=ALU.mult),
                deps=[ta], tag=f"lxb{t}_{c}")
            tcc = s.step("ve", lambda v, c=c, t=t: v.tensor_scalar(
                xn[:, c * T + t * 512: c * T + (t + 1) * 512], sm_tmp[:],
                smalls["lng"][:, c:c + 1], smalls["lnb"][:, c:c + 1], ALU.mult, ALU.add),
                deps=[tbb], tag=f"lxc{t}_{c}")
            toks.append(tcc)
        bc_free = toks[-1]
        grp_free = toks[-1]
    xn_done = bc_free

    w1sb = [p.sb("w1A", [128, 8 * 128], BF16), p.sb("w1B", [128, 8 * 128], BF16)]
    w1gb = [p.sb("w1gA", [128, 8 * 128], BF16), p.sb("w1gB", [128, 8 * 128], BF16)]
    w2all = p.sb("w2all", [128, 22 * 1024], BF16)
    hall = p.sb("hall", [128, 22 * 1024], BF16)
    gltb = [p.sb("glt", [128, 512], BF16), p.sb("glt2", [128, 512], BF16)]
    obuf = [p.sb("foA", [128, 512], F32), p.sb("foB", [128, 512], F32)]
    x3b = [p.sb("fx3A", [128, 512], F32), p.sb("fx3B", [128, 512], F32)]
    w2toks = []
    for j in range(22):
        w2toks.append(s.step("gp", lambda g, j=j: g.dma_start(
            out=w2all[:, j * 1024:(j + 1) * 1024],
            in_=AP(w2_d, j * 128 * 1024, [[1024, 128], [1, 1024]])),
            deps=[], dma=True, tag=f"lw2_{j}"))
    prev_w = {}
    pslast = {}
    prevo = {}
    prevx3 = {}
    for tcg in range(2):
        h_toks = []
        for j in range(22):
            kw = j % 2
            lw1 = s.step("gp", lambda g, j=j, kw=kw: g.dma_start(
                out=w1sb[kw][:].rearrange("p (kc c) -> p kc c", kc=8),
                in_=AP(w1_d, j * 128, [[5632, 128], [128 * 5632, 8], [1, 128]])),
                deps=[prev_w.get(("a", kw))], dma=True, tag=f"lw1a{tcg}_{j}")
            lw2 = s.step("gp", lambda g, j=j, kw=kw: g.dma_start(
                out=w1gb[kw][:].rearrange("p (kc c) -> p kc c", kc=8),
                in_=AP(w1_d, 2816 + j * 128, [[5632, 128], [128 * 5632, 8], [1, 128]])),
                deps=[prev_w.get(("g", kw))], dma=True, tag=f"lw1g{tcg}_{j}")
            for u in range(2):
                tc = 2 * tcg + u
                kwg = u
                tma = mm_acc(s, A[0:128, 0:512],
                             [(w1sb[kw][:, kc * 128:(kc + 1) * 128],
                               xn[:, kc * T + tc * 512: kc * T + (tc + 1) * 512]) for kc in range(8)],
                             deps=[xn_done, lw1, s.tok("ve")], tag=f"f3a{tc}_{j}")
                tmg = mm_acc(s, B[0:128, 0:512],
                             [(w1gb[kw][:, kc * 128:(kc + 1) * 128],
                               xn[:, kc * T + tc * 512: kc * T + (tc + 1) * 512]) for kc in range(8)],
                             deps=[xn_done, lw2, s.tok("sc")], tag=f"f3g{tc}_{j}")
                prev_w[("a", kw)] = tma
                prev_w[("g", kw)] = tmg
                a1 = s.step("sc", lambda sc, j=j, kwg=kwg: sc.activation(
                    out=gltb[kwg][:], in_=B[0:128, 0:512],
                    func=AF.Gelu, bias=smalls["b1g"][:, j:j + 1]),
                    deps=[tmg, prev_w.get(("h", kwg))], tag=f"f3s{tc}_{j}")
                a2 = s.step("ve", lambda v, j=j, u=u, kwg=kwg: v.scalar_tensor_tensor(
                    hall[:, j * 1024 + u * 512: j * 1024 + (u + 1) * 512],
                    A[0:128, 0:512], smalls["b1a"][:, j:j + 1], gltb[kwg][:],
                    ALU.add, ALU.mult), deps=[tma, a1], tag=f"f3h{tc}_{j}")
                prev_w[("h", kwg)] = a2
                h_toks.append(a2)
        for u in range(2):
            tc = 2 * tcg + u
            for mc in range(8):
                kk = mc % 2
                py = [E2[:, 0:512], P2[:, 0:512]][kk]
                lx3 = s.step("sy", lambda y, mc=mc, tc=tc, kk=kk: y.dma_start(
                    out=x3b[kk][:], in_=x1_sc.ap()[mc * 128:(mc + 1) * 128, 1 + tc * 512: 1 + (tc + 1) * 512]),
                    deps=[x1_done, prevx3.get(kk)], dma=True, tag=f"f3x{tc}_{mc}")
                tmy = mm_acc(s, py,
                             [(w2all[:, j * 1024 + mc * 128: j * 1024 + (mc + 1) * 128],
                               hall[:, j * 1024 + u * 512: j * 1024 + (u + 1) * 512]) for j in range(22)],
                             deps=h_toks + w2toks + [pslast.get(kk)], tag=f"f3y{tc}_{mc}")
                tev = s.step("ve", lambda v, mc=mc, py=py, kk=kk: v.scalar_tensor_tensor(
                    obuf[kk][:], py, smalls["b2"][:, mc:mc + 1], x3b[kk][:], ALU.add, ALU.add),
                    deps=[tmy, lx3, prevo.get(kk)], tag=f"f3e{tc}_{mc}")
                tst = s.step("sy", lambda y, mc=mc, tc=tc, kk=kk: y.dma_start(
                    out=xo_d.ap()[mc * 128:(mc + 1) * 128, tc * 512:(tc + 1) * 512], in_=obuf[kk][:]),
                    deps=[tev], dma=True, tag=f"f3st{tc}_{mc}")
                pslast[kk] = tev
                prevo[kk] = tst
                prevx3[kk] = tev
    return p.finish()


def build_add2():
    T = 2048
    p = Prog("add2")
    s = p.s
    x_d = p.din("x", [1024, T])
    x2_d = p.din("x2", [1024, T])
    xo_d = p.dout("xo", [1024, T])
    b1 = [p.sb("aA", [128, 512], F32), p.sb("aB", [128, 512], F32)]
    b2 = [p.sb("bA", [128, 512], F32), p.sb("bB", [128, 512], F32)]
    b3 = [p.sb("cA", [128, 512], F32), p.sb("cB", [128, 512], F32)]
    prev = {}
    i = 0
    for mc in range(8):
        for tc in range(4):
            k = i % 2
            l1 = s.step("sy", lambda y, mc=mc, tc=tc, k=k: y.dma_start(
                out=b1[k][:], in_=x_d.ap()[mc * 128:(mc + 1) * 128, tc * 512:(tc + 1) * 512]),
                deps=[prev.get(("1", k))], dma=True, tag=f"l1{mc}_{tc}")
            l2 = s.step("gp", lambda g, mc=mc, tc=tc, k=k: g.dma_start(
                out=b2[k][:], in_=x2_d.ap()[mc * 128:(mc + 1) * 128, tc * 512:(tc + 1) * 512]),
                deps=[prev.get(("2", k))], dma=True, tag=f"l2{mc}_{tc}")
            ta = s.step("ve", lambda v, k=k: v.tensor_add(b3[k][:], b1[k][:], b2[k][:]),
                        deps=[l1, l2, prev.get(("3", k))], tag=f"a{mc}_{tc}")
            tst = s.step("sy", lambda y, mc=mc, tc=tc, k=k: y.dma_start(
                out=xo_d.ap()[mc * 128:(mc + 1) * 128, tc * 512:(tc + 1) * 512], in_=b3[k][:]),
                deps=[ta], dma=True, tag=f"s{mc}_{tc}")
            prev[("1", k)] = ta
            prev[("2", k)] = ta
            prev[("3", k)] = tst
            i += 1
    return p.finish()


_BUILDERS["l1f_attn"] = build_l1f_attn
_BUILDERS["l1f_ffn"] = build_l1f_ffn
_BUILDERS["add2"] = build_add2


def _l1f_tables():
    T, DH, W = 4096, 64, 256
    half = np.arange(0, DH, 2, dtype=np.float32)
    inv_freq = (1.0 / 10000.0 ** (half / DH)).astype(np.float32)
    t_ = np.arange(T, dtype=np.float32)
    fr = t_[:, None] * inv_freq[None, :]
    fr = np.concatenate([fr, fr], -1)
    base = (half + 0.4 * DH) / (1.4 * DH)
    power = (t_ - T // 2) / (W // 2)
    sc = (base[None, :] ** power[:, None]).astype(np.float32)
    sc = np.concatenate([sc, sc], -1)
    cos, sin = np.cos(fr), np.sin(fr)
    sign = np.concatenate([-np.ones(32), np.ones(32)]).astype(np.float32)
    dub = lambda a: np.ascontiguousarray(np.concatenate([a, a], axis=0))
    csq = dub((cos * sc).T)
    ssq = dub((sign[None, :] * sin * sc).T)
    csk = dub((cos / sc).T)
    ssk = dub((sign[None, :] * sin / sc).T)
    ql = np.arange(128)[:, None]
    kl = np.arange(384)[None, :]
    m_std = np.where((kl >= ql) & (kl <= ql + 256), 0.0, -1e30).astype(np.float32)
    m_q0 = np.where(kl[:, :128] <= ql, 0.0, -1e30).astype(np.float32)
    m_q1 = np.where(kl[:, :256] <= 128 + ql, 0.0, -1e30).astype(np.float32)
    mks = np.zeros((128, 2304), np.float32)
    mks[:, 0:384] = m_std
    mks[:, 512:896] = m_std
    mks[:, 896:1024] = m_q0
    mks[:, 1408:1536] = m_q0
    mks[:, 1536:1792] = m_q1
    mks[:, 2048:2304] = m_q1
    return csq, ssq, csk, ssk, mks


def run_l1f_attn(xT, inp, li):
    prog = get_prog("l1f_attn")
    csq, ssq, csk, ssk, mks = _l1f_tables()
    Wqkv = inp["L1_ffn_Wqkv"][li]
    qsc = inp["L1_ffn_q_scale"][li]
    ksc = inp["L1_ffn_k_scale"][li]
    idb = np.eye(128, dtype=np.float32).astype(BF)
    qs2 = np.stack([qsc, ksc], axis=1)
    qs2 = np.ascontiguousarray(np.concatenate([qs2, qs2], axis=0))
    e2c = np.zeros((128, 2), np.float32)
    e2c[0:64, 0] = 1.0
    e2c[64:128, 1] = 1.0
    on2 = np.zeros((2, 128), np.float32)
    on2[0, 0:64] = 1.0
    on2[1, 64:128] = 1.0
    swp = np.zeros((128, 128), np.float32)
    for m in range(128):
        k = (m // 64) * 64 + ((m % 64) + 32) % 64
        swp[k, m] = 1.0
    in_maps = []
    for core in range(NCORES):
        b, hg = core // 2, core % 2
        wqk = np.empty((1024, 1024), np.float32)
        for g in range(4):
            for j in range(2):
                habs = hg * 8 + 2 * g + j
                wqk[:, g * 256 + j * 64: g * 256 + (j + 1) * 64] = \
                    Wqkv[:, habs * 64:(habs + 1) * 64]
                wqk[:, g * 256 + 128 + j * 64: g * 256 + 128 + (j + 1) * 64] = \
                    Wqkv[:, 1024 + habs * 64: 1024 + (habs + 1) * 64]
        wv = np.ascontiguousarray(Wqkv[:, 2048 + hg * 512: 2048 + (hg + 1) * 512])
        in_maps.append({
            "x": np.ascontiguousarray(xT[b]),
            "wqk": wqk, "wv": wv, "qs": qs2,
            "csq": csq, "ssq": ssq, "csk": csk, "ssk": ssk, "mks": mks,
            "ident_b": idb, "e2col": e2c.astype(BF), "ones2": on2,
            "swp": swp.astype(BF),
        })
    res = prog.run(in_maps)
    return [np.concatenate([res[2 * b]["ctx"], res[2 * b + 1]["ctx"]], axis=0) for b in range(4)]


def run_l1f_ffn(xT, ctxT, inp, li):
    prog = get_prog("l1f_ffn")
    Wo = inp["L1_ffn_Wo"][li]
    W1 = inp["L1_ffn_W1"][li]
    b1 = inp["L1_ffn_b1"][li]
    W2 = inp["L1_ffn_W2"][li]
    b2 = inp["L1_ffn_b2"][li]
    lng = inp["L1_ffn_ln_g"][li]
    lnb = inp["L1_ffn_ln_b"][li]
    w1p = np.zeros((1024, 5632), np.float32)
    w1p[:, 0:2730] = W1[:, 0:2730]
    w1p[:, 2816:2816 + 2730] = W1[:, 2730:5460]
    b1a = np.zeros((2816,), np.float32)
    b1a[0:2730] = b1[0:2730]
    b1g = np.zeros((2816,), np.float32)
    b1g[0:2730] = b1[2730:5460]
    w2p = np.zeros((2816, 1024), np.float32)
    w2p[0:2730] = W2
    col = lambda v, n: np.ascontiguousarray(v.reshape(n, 128).T)
    in_maps = []
    for core in range(NCORES):
        b, half = core // 2, core % 2
        t0 = half * 2048
        xh = np.zeros((1024, 2049), np.float32)
        ch = np.zeros((1024, 2049), BF)
        xh[:, 1:] = xT[b][:, t0:t0 + 2048]
        ch[:, 1:] = ctxT[b][:, t0:t0 + 2048]
        if half == 1:
            xh[:, 0] = xT[b][:, 2047]
            ch[:, 0] = ctxT[b][:, 2047]
        in_maps.append({
            "x": xh, "ctx": ch, "wo": np.ascontiguousarray(Wo),
            "lng": col(lng, 8), "lnb": col(lnb, 8),
            "w1": w1p, "b1a": col(b1a, 22), "b1g": col(b1g, 22),
            "w2": w2p, "b2col": col(b2, 8),
        })
    res = prog.run(in_maps)
    return [np.concatenate([res[2 * b]["xo"], res[2 * b + 1]["xo"]], axis=1) for b in range(4)]


def run_add2(xT, x2T):
    prog = get_prog("add2")
    in_maps = []
    for core in range(NCORES):
        b, half = core // 2, core % 2
        tsl = slice(half * 2048, (half + 1) * 2048)
        in_maps.append({"x": np.ascontiguousarray(xT[b][:, tsl]),
                        "x2": np.ascontiguousarray(x2T[b][:, tsl])})
    res = prog.run(in_maps)
    return [np.concatenate([res[2 * b]["xo"], res[2 * b + 1]["xo"]], axis=1) for b in range(4)]


def run_l1ffn_block(xT, inp):
    inner = run_ln_out(xT)
    for li in range(6):
        ctxT = run_l1f_attn(inner, inp, li)
        inner = run_l1f_ffn(inner, ctxT, inp, li)
    return run_add2(xT, inner)


def kernel(**inputs):
    _ = inputs.get("L0_ffn_input_ids")
    xT = [np.ascontiguousarray(inputs["L0_attn_a_x"][b].T) for b in range(4)]
    ctxT = run_attn_a(xT, inputs)
    xT = run_proj_res(xT, ctxT, inputs["L0_attn_a_W_proj"], inputs["L0_attn_a_b_proj"])
    xT = run_moe(xT, inputs)
    ctxT = run_attn_b(xT, inputs)
    xT = run_proj_res(xT, ctxT, inputs["L0_attn_b_Wo"], np.zeros(1024, np.float32))
    xT = run_moe(xT, inputs)
    xT = run_mamba1(xT, inputs)
    xT = run_l1ffn_block(xT, inputs)
    xnT = run_ln_out(xT)
    ctxT = run_mamba2(xnT, inputs)
    xT = run_proj_res2(xT, xnT, ctxT,
                       np.ascontiguousarray(inputs["L1_attn_b_op_w"].T), inputs["L1_attn_b_op_b"])
    xT = run_l1ffn_block(xT, inputs)
    out = np.stack([xT[b].T for b in range(4)], axis=0).astype(np.float32)
    return out
